# version 10
# baseline (speedup 1.0000x reference)
import math
import numpy as np
import ml_dtypes
import concourse.bass as bass
import concourse.mybir as mybir
from concourse.bass_utils import run_bass_kernel_spmd

F32 = mybir.dt.float32
BF16 = mybir.dt.bfloat16
I8 = mybir.dt.int8
U8 = mybir.dt.uint8
AF = mybir.ActivationFunctionType
ALU = mybir.AluOpType

D, KD, VD, H, HK, HV = 2048, 1024, 2048, 8, 128, 256
LR, FH, QH, CH = 16, 5504, 8192, 64
GN = 16.0
QSCALE_LOG = -0.5 * math.log(128.0)
T_TOTAL, TS = 4096, 1024
NS, NCH, NPAIR = T_TOTAL // TS, TS // CH, TS // CH // 2
DC, KC, VC, FC, QC = D // 128, KD // 128, VD // 128, FH // 128, QH // 128
SLAB = 2048
NRING = 2
EPS_RMS, EPS_LN = 1e-6, 1e-5
KB = 1024

SEMS = ("WL0", "WL1", "XC", "XL0", "XL1", "XL2", "XL3", "XW0", "XW1", "PW0", "PW1",
        "PE", "DV", "AC", "PL", "ST0", "ST1", "WC0", "WC1")
NOW = (None, 0)


def cdiv(a, b):
    return (a + b - 1) // b


def _wc_sections():
    raw = [("wv", 2 * DC * 1024), ("wg", 2 * DC * 1024),
           ("wq", KC * DC * 128), ("wk", KC * DC * 128),
           ("gk1", DC * LR), ("wo", DC * VC * 128),
           ("fgu", FC * DC * 256), ("fd", DC * FC * 128),
           ("qgu", QC * DC * 256), ("qd", DC * QC * 128)]
    out, col = {}, 0
    for name, cols in raw:
        out[name] = (col, cols)
        col += cdiv(cols, SLAB) * SLAB
    return out, col


WC_SECS, WC_COLS = _wc_sections()


def lay_lhsT(W, kc, oc):
    return np.ascontiguousarray(
        W.reshape(kc, 128, oc, 128).transpose(1, 2, 0, 3).reshape(128, oc * kc * 128))


def lay_gu(Wg, Wu, kc, oc):
    g = Wg.reshape(kc, 128, oc, 128).transpose(1, 2, 0, 3)
    u = Wu.reshape(kc, 128, oc, 128).transpose(1, 2, 0, 3)
    return np.ascontiguousarray(np.stack([g, u], axis=3).reshape(128, oc * kc * 256))


def lay_rhs_nh(W, kc):
    w = W.reshape(kc, 128, 2, 1024)
    return np.ascontiguousarray(w.transpose(1, 2, 0, 3).reshape(128, 2 * kc * 1024))


def lay_vec(v):
    n = v.shape[0] // 128
    return np.ascontiguousarray(np.asarray(v, np.float32).reshape(n, 128).T)


class Ctx:
    def __init__(self, engname, eng, semh):
        self.engname, self.eng, self.semh = engname, eng, semh
        self.s = {k: 0 for k in SEMS}

    def n(self, sem):
        return self.s[sem]

    def mark(self, sem):
        return (sem, self.s[sem])

    def op(self, engname, fn, inc=(), wait=()):
        if self.engname == engname:
            for s, v in wait:
                if s is not None and v > 0:
                    self.eng.wait_ge(self.semh[s], v)
            bi = fn(self.eng)
            for s, k in inc:
                bi.then_inc(self.semh[s], k)
        for s, k in inc:
            self.s[s] += k

    def wait(self, engname, *sv):
        if self.engname == engname:
            for s, v in sv:
                if s is not None and v > 0:
                    self.eng.wait_ge(self.semh[s], v)


class WStream:
    def __init__(self, wc_ap, ring_ap):
        self.wc, self.ring, self.i = wc_ap, ring_ap, 0
        self.slot_free = [(None, 0)] * NRING
        self.gate = []

    def load(self, ctx, col, ncols=SLAB, extra_wait=()):
        slot = self.i % NRING
        self.i += 1
        dst = self.ring[:, slot * SLAB:slot * SLAB + ncols]
        src_ = self.wc[:, col:col + ncols]
        sem = "WL%d" % slot
        w = list(extra_wait) + list(self.gate) + [self.slot_free[slot],
                                                  (sem, ctx.n(sem))]
        ctx.op("sync", lambda e: e.dma_start(out=dst, in_=src_), inc=[(sem, 16)], wait=w)
        return slot, ctx.mark(sem)

    def buf(self, slot, off=0, ncols=128):
        return self.ring[:, slot * SLAB + off:slot * SLAB + off + ncols]


def build_nc(debug=False):
    nc = bass.Bass("TRN2", target_bir_lowering=False, debug=False)

    x0_d = nc.dram_tensor("x0", [D, T_TOTAL], F32, kind="ExternalInput").ap()
    y_d = nc.dram_tensor("y", [D, T_TOTAL], F32, kind="ExternalOutput").ap()
    src = {}
    for name, shp, dt in [
        ("wv_s", [128, 2 * DC * 1024], F32), ("wg_s", [128, 2 * DC * 1024], F32),
        ("wq_s", [128, KC * DC * 128], F32), ("wk_s", [128, KC * DC * 128], F32),
        ("gk1_s", [128, DC * LR], F32), ("wo_s", [128, DC * VC * 128], F32),
        ("fgu_s", [128, FC * DC * 256], F32), ("fd_s", [128, DC * FC * 128], F32),
        ("qgu_s", [128, QC * DC * 256], I8), ("qd_s", [128, DC * QC * 128], I8),
        ("gk2_s", [16, KD], F32), ("nw_s", [128, 16], F32), ("fw_s", [128, 16], F32),
        ("gb_s", [128, 8], F32), ("sg_s", [128, 64], F32), ("su_s", [128, 64], F32),
        ("sd_s", [128, 16], F32), ("gnw_s", [128, 2], F32), ("cst_s", [128, 192], BF16),
    ]:
        src[name] = nc.dram_tensor(name, shp, dt, kind="ExternalInput").ap()
    wc = nc.dram_tensor("wc", [128, WC_COLS], BF16, kind="Internal").ap()

    dbg = {}
    if debug:
        for name, shp in [
            ("d_h", [128, DC * TS]), ("d_q", [128, KC * TS]), ("d_k", [128, KC * TS]),
            ("d_g", [128, KC * TS]), ("d_v", [128, NPAIR * VD]), ("d_sg", [128, NPAIR * VD]),
            ("d_o", [128, NPAIR * VD]), ("d_u", [128, NPAIR * VD]), ("d_x1", [128, DC * TS]),
            ("d_h2", [128, DC * TS]), ("d_x3", [128, DC * TS]),
        ]:
            dbg[name] = nc.dram_tensor(name, shp, BF16, kind="ExternalOutput").ap()
        dbg["d_S"] = nc.dram_tensor("d_S", [128, H * HV], F32, kind="ExternalOutput").ap()

    ARENA = 212800
    arena_cm = nc.sbuf_tensor("arena", [128, ARENA], U8)
    A = arena_cm.__enter__()

    def carve(off, p, cols, dt):
        bsz = {F32: 4, BF16: 2, I8: 1, U8: 1}[dt]
        assert off + cols * bsz <= ARENA, (off, cols, dt)
        return A[0:p, off:off + cols * bsz].bitcast(dt)

    V = {}
    V["ident"] = carve(0, 128, 128, BF16)
    V["umask"] = carve(256, 128, 64, BF16)
    V["ones_b"] = carve(384, 128, 64, BF16)
    V["ones_f"] = carve(512, 128, 128, F32)
    V["nw"] = carve(1024, 128, 16, F32)
    V["fw"] = carve(1088, 128, 16, F32)
    V["gb"] = carve(1152, 128, 8, F32)
    V["sg"] = carve(1184, 128, 64, F32)
    V["su"] = carve(1440, 128, 64, F32)
    V["sd"] = carve(1696, 128, 16, F32)
    V["gnw"] = carve(1760, 128, 2, F32)
    V["cb"] = carve(1768, 128, 4, F32)
    V["gk2"] = carve(1792, 16, KD, BF16)
    V["S32"] = carve(3840, 128, H * HV, F32)
    V["S16"] = carve(12032, 128, H * HV, BF16)
    V["b0"] = carve(16128, 128, 512, F32)
    V["b1"] = carve(18176, 128, 512, F32)
    V["d2"] = carve(20224, 128, 512, F32)
    V["ebq"] = carve(22272, 128, 512, BF16)
    V["emq"] = carve(23296, 128, 512, BF16)
    V["klq"] = carve(24320, 128, 512, BF16)
    V["qe"] = carve(25344, 128, 512, BF16)
    V["kd"] = carve(26368, 128, 512, BF16)
    V["kl"] = carve(27392, 128, 512, BF16)
    V["at"] = carve(28416, 128, 512, BF16)
    V["klt"] = carve(29440, 128, 1024, BF16)
    V["blv"] = carve(31488, 128, 8, F32)
    V["eblv"] = carve(31520, 128, 8, F32)
    V["ssq"] = carve(31552, 128, 64, F32)
    V["r_o"] = carve(31808, 128, 64, BF16)
    V["rfs"] = carve(31936, 1, 512, F32)
    V["sqscr"] = None
    V["sq2_0"] = carve(16128, 128, 1024, BF16)
    V["sq2_1"] = carve(18176, 128, 1024, BF16)
    V["mrow"] = carve(20224, 1, 1024, F32)
    V["vrow"] = carve(24320, 1, 1024, F32)
    V["ga1"] = carve(16128, 128, 1024, F32)
    V["ga2"] = carve(20224, 128, 1024, F32)
    V["ge1"] = carve(24320, 128, 1024, BF16)
    V["gl1"] = carve(26368, 128, 1024, BF16)
    V["t1sb"] = carve(33984, 16, 1024, BF16)
    V["trow"] = carve(33984, 1, 1024, F32)
    V["sqx0"] = carve(29440, 128, 512, BF16)
    V["sqx1"] = carve(30464, 128, 512, BF16)
    XS = 38080
    V["xr"] = [carve(XS + i * 2048, 128, 512, F32) for i in range(4)]
    V["sqscr"] = carve(XS, 128, VD, BF16)
    V["x0w0"] = carve(XS, 128, 1024, F32)
    V["x0w1"] = carve(XS + 4096, 128, 1024, F32)
    V["outb0"] = carve(XS, 128, 512, F32)
    V["outb1"] = carve(XS + 2048, 128, 512, F32)
    V["rsb2"] = carve(XS + 8192, 128, 512, BF16)
    V["nm2"] = carve(XS + 9216, 128, 512, BF16)
    RING_OFF = XS + 10240
    V["ring"] = carve(RING_OFF, 128, NRING * SLAB, BF16)
    M = RING_OFF + NRING * SLAB * 2
    V["h"] = carve(M, 128, DC * TS, BF16)
    V["v"] = carve(M + 32 * KB, 128, NPAIR * VD, BF16)
    V["sgt"] = carve(M + 64 * KB, 128, NPAIR * VD, BF16)
    V["q"] = carve(M + 96 * KB, 128, KC * TS, BF16)
    V["k"] = carve(M + 112 * KB, 128, KC * TS, BF16)
    V["g"] = carve(M + 128 * KB, 128, KC * TS, BF16)
    assert M + 144 * KB <= ARENA, M
    V["o"] = carve(M, 128, NPAIR * VD, BF16)
    V["u"] = carve(M + 32 * KB, 128, NPAIR * VD, BF16)
    V["ufm"] = carve(M, 128, VC * TS, BF16)
    V["x1"] = carve(M + 96 * KB, 128, DC * TS, BF16)
    V["h2"] = carve(M + 64 * KB, 128, DC * TS, BF16)
    V["mA"] = carve(M, 128, 32 * TS, BF16)
    V["mB"] = carve(M + 128 * KB, 128, 11 * TS, BF16)
    V["x3"] = carve(M + 64 * KB, 128, DC * TS, BF16)
    V["f"] = carve(M, 128, QC * 512, BF16)
    V["s2"] = carve(M + 96 * KB, 128, DC * TS, BF16)
    V["pstg"] = [carve(M + 128 * KB, 128, 8 * KB, U8), carve(M + 136 * KB, 128, 8 * KB, U8)]
    V["pout"] = [carve(M + 144 * KB, 128, SLAB, BF16), carve(M + 148 * KB, 128, SLAB, BF16)]

    def m_sl(j, lo, ncols):
        if j < 32:
            return V["mA"][:, j * TS + lo:j * TS + lo + ncols]
        return V["mB"][:, (j - 32) * TS + lo:(j - 32) * TS + lo + ncols]

    PA = nc.alloc_psum_tensor("PA", [128, 1024], F32)
    PC = nc.alloc_psum_tensor("PC", [128, 1024], F32)
    PB = nc.alloc_psum_tensor("PB", [128, 2048], F32)
    PBb = PB[:, 0:512].bitcast(BF16)
    PBb2 = PB[:, 512:1024].bitcast(BF16)
    PCb = PC[:, 512:1024].bitcast(BF16)

    nc.const_aps.register(F32, QSCALE_LOG, V["cb"][:, 3:4])
    nc.const_aps.register(F32, EPS_RMS, V["cb"][:, 0:1])
    nc.const_aps.register(F32, EPS_LN, V["cb"][:, 1:2])
    nc.const_aps.register(F32, 1.0, V["cb"][:, 2:3])
    sem_cms = [nc.semaphore(s) for s in SEMS]
    semh = {s: cm.__enter__() for s, cm in zip(SEMS, sem_cms)}

    def build_program(ctx: Ctx):
        op, wt, n, mark = ctx.op, ctx.wait, ctx.n, ctx.mark
        ws = WStream(wc, V["ring"])
        E = {}

        for dst, sname in [(V["nw"], "nw_s"), (V["fw"], "fw_s"), (V["gb"], "gb_s"),
                           (V["sg"], "sg_s"), (V["su"], "su_s"), (V["sd"], "sd_s"),
                           (V["gnw"], "gnw_s")]:
            op("scalar", lambda e, d=dst, s=sname: e.dma_start(out=d, in_=src[s]),
               inc=[("XC", 16)])
        op("scalar", lambda e: e.dma_start(out=A[0:128, 0:384].bitcast(BF16), in_=src["cst_s"]),
           inc=[("XC", 16)])
        op("scalar", lambda e: e.dma_start(out=V["ga1"][0:16, 0:1024], in_=src["gk2_s"]),
           inc=[("XC", 16)])
        nXC0 = n("XC")
        op("vector", lambda e: e.tensor_copy(out=V["gk2"], in_=V["ga1"][0:16, 0:1024]),
           wait=[("XC", nXC0)])
        op("vector", lambda e: e.memset(V["ones_b"], 1.0))
        op("vector", lambda e: e.memset(V["ones_f"], 1.0))
        op("vector", lambda e: e.memset(V["cb"][:, 0:1], EPS_RMS))
        op("vector", lambda e: e.memset(V["cb"][:, 1:2], EPS_LN))
        op("vector", lambda e: e.memset(V["cb"][:, 2:3], 1.0))
        op("vector", lambda e: e.memset(V["cb"][:, 3:4], QSCALE_LOG))
        op("vector", lambda e: e.memset(V["S32"], 0.0))
        op("vector", lambda e: e.memset(V["S16"], 0.0), inc=[("DV", 1)])
        E["s16_prev"] = mark("DV")
        E["s16_ac"] = mark("DV")
        E["init_dv"] = mark("DV")

        prep_marks = {}
        ps_ = {"si": 0, "cast": {}, "store": {}}

        def emit_prep_slab(sname, wname, dt, j, extra=(), q="sync", lsem="WL"):
            col0, cols = WC_SECS[wname]
            ncols = min(SLAB, cols - j * SLAB)
            si = ps_["si"]
            st = si % 2
            bsz = 4 if dt == F32 else 1
            stage = V["pstg"][st][:, 0:ncols * bsz].bitcast(dt)
            fold = (wname == "wo")
            sem = "%s%d" % (lsem, st)
            op(q, lambda e, sv=stage, sp=src[sname], c0=j * SLAB, nco=ncols:
               e.dma_start(out=sv, in_=sp[:, c0:c0 + nco]),
               inc=[(sem, 16)],
               wait=[ps_["cast"].get(si - 2, NOW), (sem, n(sem))] + list(extra))
            cwait = [mark(sem), ps_["store"].get(si - 2, NOW)]
            if not fold:
                op("vector", lambda e, o=V["pout"][st][:, 0:ncols], s_=stage:
                   e.tensor_copy(out=o, in_=s_), inc=[("DV", 1)], wait=cwait)
            else:
                ntile = ncols // 128
                for tti in range(ntile):
                    gt = (j * SLAB) // 128 + tti
                    kk2 = (gt % VC) % 2
                    op("vector", lambda e, o=V["pout"][st][:, tti * 128:(tti + 1) * 128],
                       s_=stage.bitcast(F32)[:, tti * 128:(tti + 1) * 128], k2=kk2:
                       e.tensor_scalar(out=o, in0=s_, scalar1=V["gnw"][:, k2:k2 + 1],
                                       scalar2=None, op0=ALU.mult),
                       inc=[("DV", 1)] if tti == ntile - 1 else [],
                       wait=cwait if tti == 0 else [])
            ps_["cast"][si] = mark("DV")
            wcsem = "WC%d" % st
            op("gpsimd", lambda e, o=V["pout"][st][:, 0:ncols], c0=col0 + j * SLAB, nco=ncols:
               e.dma_start(out=wc[:, c0:c0 + nco], in_=o),
               inc=[(wcsem, 16)], wait=[ps_["cast"][si]])
            ps_["store"][si] = mark(wcsem)
            ps_["si"] = si + 1

        for sname, wname, dt in [("wv_s", "wv", F32), ("wg_s", "wg", F32),
                                 ("wq_s", "wq", F32), ("wk_s", "wk", F32),
                                 ("gk1_s", "gk1", F32), ("wo_s", "wo", F32)]:
            _, cols = WC_SECS[wname]
            for j in range(cdiv(cols, SLAB)):
                emit_prep_slab(sname, wname, dt, j)
            prep_marks[wname] = [mark("WC0"), mark("WC1")]
        prep_wc0, prep_wc1 = mark("WC0"), mark("WC1")

        p2items = []
        for sname, wname, dt in [("fgu_s", "fgu", F32), ("fd_s", "fd", F32),
                                 ("qgu_s", "qgu", I8), ("qd_s", "qd", I8)]:
            _, cols = WC_SECS[wname]
            nsl = cdiv(cols, SLAB)
            for j in range(nsl):
                p2items.append((sname, wname, dt, j, j == nsl - 1))
        p2i = [0]

        def drain2(k, extra=()):
            while k > 0 and p2i[0] < len(p2items):
                sname, wname, dt, j, last = p2items[p2i[0]]
                emit_prep_slab(sname, wname, dt, j,
                               extra=extra if p2i[0] < 2 else (),
                               q="gpsimd", lsem="PW")
                if last:
                    prep_marks[wname] = [mark("WC0"), mark("WC1")]
                p2i[0] += 1
                k -= 1

        dump_marks = {}

        def dump(name, view, s, *waits):
            if not debug or s != 0:
                return NOW
            op("gpsimd", lambda e: e.dma_start(out=dbg[name], in_=view),
               inc=[("ST0", 16)],
               wait=[("ST0", n("ST0"))] + list(waits))
            return mark("ST0")

        xri = [0]

        def xload(c, t_lo, war):
            slot = xri[0] % 4
            xri[0] += 1
            buf = V["xr"][slot]
            sem = "XL%d" % slot
            op("scalar", lambda e: e.dma_start(
                out=buf, in_=x0_d[c * 128:c * 128 + 128, t_lo:t_lo + 512]),
                inc=[(sem, 16)], wait=[war])
            return buf, mark(sem), slot

        E["h_region"] = NOW
        E["v_region"] = prep_wc0
        E["v_region2"] = prep_wc1
        E["sg_region"] = NOW
        E["qk_region"] = NOW
        E["g_region"] = NOW
        E["pa_free"] = NOW
        E["pc_free"] = NOW
        E["pb_free"] = NOW
        E["xr_war"] = [NOW] * 4
        E["sqx_war"] = {0: NOW, 1: NOW}
        E["sq2_war"] = {0: NOW, 1: NOW}
        def sec_gate(name, s):
            ws.gate = prep_marks[name] if s == 0 else []

        for s in range(NS):
            t0 = s * TS

            sumA = {}
            for grp in range(2):
                tok0 = t0 + grp * 512
                for c in range(DC):
                    buf, xl, slot = xload(c, tok0, E["xr_war"][xri[0] % 4])
                    sqx = V["sqx0"] if c % 2 == 0 else V["sqx1"]
                    w_ = [xl, E["sqx_war"][c % 2]]
                    if s == 0 and grp == 0 and c == 0:
                        w_.append(E["init_dv"])
                    op("scalar", lambda e, o=sqx, i=buf: e.activation(out=o, in_=i, func=AF.Square),
                       inc=[("AC", 1)], wait=w_)
                    sq_ac = mark("AC")
                    E["xr_war"][slot] = sq_ac
                    w_ = [sq_ac]
                    if c == 0:
                        w_.append(E["pc_free"])
                        if s == 0 and grp == 0:
                            w_.append(E["init_dv"])
                    op("tensor", lambda e, r=sqx, cc=c, gg=grp:
                       e.matmul(out=PC[0:1, gg * 512:gg * 512 + 512], lhsT=V["ones_b"][:, 0:1],
                                rhs=r, start=(cc == 0), stop=(cc == DC - 1)),
                       inc=[("PE", 1)], wait=w_)
                    E["sqx_war"][c % 2] = mark("PE")
                sumA[grp] = mark("PE")
            rb_m = {}
            for grp in range(2):
                op("scalar", lambda e, gg=grp:
                   e.activation(out=V["mrow"][0:1, gg * 512:gg * 512 + 512],
                                in_=PC[0:1, gg * 512:gg * 512 + 512],
                                func=AF.Ln, bias=EPS_RMS, scale=1.0 / D),
                   inc=[("AC", 1)], wait=[sumA[grp], E.get("mrow_war", NOW)])
                op("scalar", lambda e, gg=grp:
                   e.activation(out=V["mrow"][0:1, gg * 512:gg * 512 + 512],
                                in_=V["mrow"][0:1, gg * 512:gg * 512 + 512],
                                func=AF.Exp, scale=-0.5),
                   inc=[("AC", 1)], wait=[mark("AC")])
                rfs_m = mark("AC")
                op("tensor", lambda e, gg=grp:
                   e.matmul(out=PA[:, gg * 512:gg * 512 + 512], lhsT=V["ones_f"][0:1, :],
                            rhs=V["mrow"][0:1, gg * 512:gg * 512 + 512],
                            start=True, stop=True),
                   inc=[("PE", 1)], wait=[rfs_m] + ([E["pa_free"]] if grp == 0 else []))
                rb_m[grp] = mark("PE")
                op("scalar", lambda e, gg=grp, d=(V["rsb2"] if grp == 0 else V["nm2"]):
                   e.activation(out=d, in_=PA[:, gg * 512:gg * 512 + 512], func=AF.Copy),
                   inc=[("AC", 1)], wait=[rb_m[grp], E.get("rnb_war", NOW)])
                rb_m[grp] = mark("AC")
            E["pc_free"] = mark("AC")
            E["pa_free"] = mark("AC")
            E["mrow_war"] = rb_m[1]
            h_ch = {}
            tmp_war = {0: E.get("b0_war", NOW), 1: E.get("b0_war", NOW)}
            for c in range(DC):
                for grp in range(2):
                    rbuf = V["rsb2"] if grp == 0 else V["nm2"]
                    buf, xl, slot = xload(c, t0 + grp * 512, E["xr_war"][xri[0] % 4])
                    tmp = V["b0"] if grp == 0 else V["b1"]
                    w_ = [xl, tmp_war[grp]]
                    if c == 0 and grp == 0:
                        w_.append(E["h_region"])
                    op("gpsimd", lambda e, i=buf, cc=c, t_=tmp:
                       e.tensor_tensor(out=t_, in0=i,
                                       in1=V["nw"][:, cc:cc + 1].broadcast_to((128, 512)),
                                       op=ALU.mult),
                       inc=[("PL", 1)], wait=w_)
                    E["xr_war"][slot] = mark("PL")
                    t_pl = mark("PL")
                    op("gpsimd", lambda e, cc=c, gg=grp, rb=rbuf, t_=tmp:
                       e.tensor_tensor(
                           out=V["h"][:, cc * TS + gg * 512:cc * TS + gg * 512 + 512],
                           in0=t_, in1=rb, op=ALU.mult),
                       inc=[("PL", 1)], wait=[t_pl, rb_m[grp]])
                    tmp_war[grp] = mark("PL")
                h_ch[c] = mark("PL")
            hdone_grp = [mark("PL"), mark("PL")]
            E["rnb_war"] = mark("PL")
            E["b0_war"] = mark("PL")
            h_done = mark("PL")

            v_done = {}
            prev_evac = {"wv": E["v_region"], "wg": E["sg_region"]}
            vg_psum_war = NOW
            for mat, outv in (("wv", "v"), ("wg", "sgt")):
                col0, _ = WC_SECS[mat]
                eng, sem = ("scalar", "AC")
                pe_war = vg_psum_war
                sec_gate(mat, s)
                for nh in range(2):
                    for tcg in range(2):
                        for kk in range(DC):
                            if kk % 2 == 0:
                                cur_v = ws.load(ctx, col0 + (nh * DC + kk) * 1024)
                            slot, wl_t = cur_v
                            ro = (kk % 2) * 1024
                            for j in range(4):
                                tc = tcg * 4 + j
                                psl = [PA[:], PC[:], PB[:, 0:1024], PB[:, 1024:2048]][j]
                                lhs = V["h"][:, kk * TS + tc * 128:kk * TS + (tc + 1) * 128]
                                for nn2 in range(2):
                                    w_ = []
                                    if kk == 0 and nn2 == 0 and j == 0:
                                        w_ = [wl_t, pe_war, h_ch[kk + 1]]
                                        if mat == "wv" and nh == 0 and tcg == 0:
                                            w_ += [E["pa_free"], E["pc_free"], E["pb_free"]]
                                        if nh == 0 and tcg == 0:
                                            w_.append(prev_evac[mat])
                                    elif kk % 2 == 0 and nn2 == 0 and j == 0:
                                        w_ = [wl_t, h_ch[min(kk + 1, DC - 1)]]
                                    inc_ = []
                                    if kk % 2 == 1 and nn2 == 1 and j == 3:
                                        inc_.append(("PE", 1))
                                    op("tensor", lambda e, p=psl, l=lhs, sl=slot, r2=ro, n2=nn2, k2=kk:
                                       e.matmul(out=p[:, n2 * 512:(n2 + 1) * 512], lhsT=l,
                                                rhs=ws.buf(sl, r2 + n2 * 512, 512),
                                                start=(k2 == 0), stop=(k2 == DC - 1)),
                                       inc=inc_, wait=w_)
                                    if kk % 2 == 1 and nn2 == 1 and j == 3:
                                        ws.slot_free[slot] = mark("PE")
                        grp_pe = mark("PE")
                        for j in range(4):
                            tc = tcg * 4 + j
                            psl = [PA[:], PC[:], PB[:, 0:1024], PB[:, 1024:2048]][j]
                            dst = V[outv][:, tc * VD + nh * 1024:tc * VD + nh * 1024 + 1024]
                            fn_ = AF.Copy if mat == "wv" else AF.Silu
                            op(eng, lambda e, d=dst, p=psl, f_=fn_:
                               e.activation(out=d, in_=p, func=f_),
                               inc=[(sem, 1)], wait=[grp_pe] + ([prev_evac[mat]] if (nh == 0 and tcg == 0 and j == 0) else []))
                        pe_war = mark(sem)
                vg_psum_war = pe_war
                v_done[outv] = mark(sem)
            E["pa_free"] = v_done["sgt"]
            E["pc_free"] = v_done["sgt"]
            E["pb_free"] = v_done["sgt"]

            qk_ps_free = {0: E["pa_free"], 1: E["pc_free"]}
            for mat, outv in (("wq", "q"), ("wk", "k")):
                col0, _ = WC_SECS[mat]
                sec_gate(mat, s)
                region_free = E["qk_region"] if mat == "wq" else NOW
                for o in range(KC):
                    slot, wl_t = ws.load(ctx, col0 + o * DC * 128)
                    ps = PA[:] if o % 2 == 0 else PC[:]
                    for kk in range(DC):
                        for nn2 in range(2):
                            w_ = []
                            if kk == 0 and nn2 == 0:
                                w_ = [wl_t, qk_ps_free[o % 2], h_done]
                            op("tensor", lambda e, p=ps, sl=slot, k2=kk, n2=nn2:
                               e.matmul(out=p[:, n2 * 512:(n2 + 1) * 512],
                                        lhsT=ws.buf(sl, k2 * 128, 128),
                                        rhs=V["h"][:, k2 * TS + n2 * 512:k2 * TS + n2 * 512 + 512],
                                        start=(k2 == 0), stop=(k2 == DC - 1)),
                               inc=([("PE", 1)] if (kk == DC - 1 and nn2 == 1) else []),
                               wait=w_)
                    ws.slot_free[slot] = mark("PE")
                    pe_m = mark("PE")
                    op("scalar", lambda e, oo=o, p=ps:
                       e.activation(out=V[outv][:, oo * TS:(oo + 1) * TS], in_=p, func=AF.Copy),
                       inc=[("AC", 1)], wait=[pe_m, region_free])
                    region_free = NOW
                    qk_ps_free[o % 2] = mark("AC")
            qk_done = mark("AC")

            col0, _ = WC_SECS["gk1"]
            sec_gate("gk1", s)
            slot, wl_t = ws.load(ctx, col0, DC * LR)
            for kk in range(DC):
                for nn2 in range(2):
                    w_ = [wl_t, E["pb_free"], h_done] if (kk == 0 and nn2 == 0) else []
                    op("tensor", lambda e, sl=slot, k2=kk, n2=nn2:
                       e.matmul(out=PB[0:16, n2 * 512:(n2 + 1) * 512],
                                lhsT=ws.buf(sl, k2 * LR, LR),
                                rhs=V["h"][:, k2 * TS + n2 * 512:k2 * TS + n2 * 512 + 512],
                                start=(k2 == 0), stop=(k2 == DC - 1)),
                       inc=([("PE", 1)] if (kk == DC - 1 and nn2 == 1) else []),
                       wait=w_)
            ws.slot_free[slot] = mark("PE")
            t1_pe = mark("PE")
            op("vector", lambda e: e.tensor_copy(out=V["t1sb"], in_=PB[0:16, 0:1024]),
               inc=[("DV", 1)], wait=[t1_pe])
            t1_dv = mark("DV")
            E["pb_free"] = t1_dv
            g_ps_free = dict(qk_ps_free)
            prev_e1, prev_l1, prev_gs = NOW, NOW, NOW
            for o in range(KC):
                ps = PA[:] if o % 2 == 0 else PC[:]
                for nn2 in range(2):
                    w_ = [t1_dv, g_ps_free[o % 2]] if nn2 == 0 else []
                    op("tensor", lambda e, oo=o, p=ps, n2=nn2:
                       e.matmul(out=p[:, n2 * 512:(n2 + 1) * 512],
                                lhsT=V["gk2"][:, oo * 128:(oo + 1) * 128],
                                rhs=V["t1sb"][:, n2 * 512:n2 * 512 + 512],
                                start=True, stop=True),
                       inc=[("PE", 1)] if nn2 == 1 else [], wait=w_)
                t2_pe = mark("PE")
                op("vector", lambda e, oo=o, p=ps:
                   e.tensor_scalar_add(out=V["ga2"], in0=p, scalar1=V["gb"][:, oo:oo + 1]),
                   inc=[("DV", 1)], wait=[t2_pe, prev_gs, prev_e1])
                g_ps_free[o % 2] = mark("DV")
                t3_dv = mark("DV")
                op("scalar", lambda e: e.activation(out=V["ga1"], in_=V["ga2"], func=AF.Abs),
                   inc=[("AC", 1)], wait=[t3_dv, prev_e1])
                a1_ac = mark("AC")
                op("scalar", lambda e: e.activation(out=V["ge1"], in_=V["ga1"],
                                                    func=AF.Exp, scale=-1.0),
                   inc=[("AC", 1)], wait=[a1_ac, prev_l1])
                e1_ac = mark("AC")
                prev_e1 = e1_ac
                op("scalar", lambda e: e.activation(out=V["gl1"], in_=V["ge1"],
                                                    func=AF.Ln, bias=1.0),
                   inc=[("AC", 1)], wait=[e1_ac, prev_gs])
                l1_ac = mark("AC")
                prev_l1 = l1_ac
                w_ = [l1_ac]
                if o == 0:
                    w_.append(E["g_region"])
                    if s == 0:
                        w_ += [prep_wc0, prep_wc1]
                op("vector", lambda e, oo=o:
                   e.scalar_tensor_tensor(out=V["g"][:, oo * TS:(oo + 1) * TS],
                                          in0=V["ga2"], scalar=0.0, in1=V["gl1"],
                                          op0=ALU.min, op1=ALU.subtract),
                   inc=[("DV", 1)], wait=w_)
                prev_gs = mark("DV")
            g_done = mark("DV")
            E["pa_free"] = g_ps_free[0]
            E["pc_free"] = g_ps_free[1]
            dump_marks["h"] = dump("d_h", V["h"], s, h_done)
            dump_marks["q"] = dump("d_q", V["q"], s, qk_done)
            dump_marks["k"] = dump("d_k", V["k"], s, qk_done)
            dump_marks["g"] = dump("d_g", V["g"], s, g_done)
            dump_marks["v"] = dump("d_v", V["v"], s, v_done["v"])
            dump_marks["sg"] = dump("d_sg", V["sgt"], s, v_done["sgt"])

            ch = {"b0_free": NOW, "b1_free": NOW, "d2_free": NOW, "ebq_free": NOW,
                  "emq_free": NOW, "klq_free": NOW, "eblv_free": NOW,
                  "qe_free": NOW, "kd_free": NOW, "kl_free": NOW,
                  "at_ps": E["pc_free"], "at_free": NOW, "o_ps": E["pb_free"],
                  "klt_ps": E["pc_free"], "klt_free": NOW,
                  "su_ps": [E["pa_free"], E["pa_free"]], "s16": E["s16_ac"],
                  "sq_free": NOW}
            o_evacs = []
            g_read_pl = NOW
            for c in range(NCH):
                pair, base = c // 2, 64 * (c % 2)
                co = c * CH
                g3 = V["g"].rearrange("p (hh t) -> p hh t", hh=KC)[:, :, co:co + CH]
                q3 = V["q"].rearrange("p (hh t) -> p hh t", hh=KC)[:, :, co:co + CH]
                k3 = V["k"].rearrange("p (hh t) -> p hh t", hh=KC)[:, :, co:co + CH]
                op("gpsimd", lambda e, g_=g3: e.tensor_copy(out=V["b0"], in_=g_),
                   inc=[("PL", 1)], wait=[g_done if c == 0 else NOW, ch["b0_free"]])
                b0_pl = mark("PL")
                g_read_pl = b0_pl
                for hh in range(H):
                    op("vector", lambda e, h2=hh:
                       e.tensor_tensor_scan(out=V["b1"][:, h2 * CH:(h2 + 1) * CH],
                                            data0=V["b0"][:, h2 * CH:(h2 + 1) * CH],
                                            data1=V["ones_f"][:, 0:CH], initial=0.0,
                                            op0=ALU.add, op1=ALU.mult),
                       inc=[("DV", 1)] if hh == H - 1 else [],
                       wait=[b0_pl, ch["b1_free"]] if hh == 0 else [])
                ch["b0_free"] = mark("DV")
                scan_dv = mark("DV")
                op("vector", lambda e:
                   e.tensor_copy(out=V["blv"],
                                 in_=V["b1"].rearrange("p (hh t) -> p hh t", hh=H)[:, :, CH - 1:CH].squeeze(2)),
                   inc=[("DV", 1)], wait=[scan_dv])
                blv_dv = mark("DV")
                op("scalar", lambda e: e.activation(out=V["ebq"], in_=V["b1"], func=AF.Exp,
                                                    scale=1.0 / GN, bias=QSCALE_LOG),
                   inc=[("AC", 1)], wait=[scan_dv, ch["ebq_free"]])
                op("scalar", lambda e: e.activation(out=V["emq"], in_=V["b1"], func=AF.Exp,
                                                    scale=-1.0 / GN),
                   inc=[("AC", 1)], wait=[ch["emq_free"]])
                em_ac = mark("AC")
                op("vector", lambda e:
                   e.tensor_tensor(out=V["d2"], in0=V["b1"],
                                   in1=V["blv"].unsqueeze(2).broadcast_to((128, H, CH)),
                                   op=ALU.subtract),
                   inc=[("DV", 1)], wait=[blv_dv, ch["d2_free"]])
                ch["b1_free"] = mark("DV")
                d2_dv = mark("DV")
                op("scalar", lambda e: e.activation(out=V["klq"], in_=V["d2"], func=AF.Exp,
                                                    scale=-1.0 / GN),
                   inc=[("AC", 1)], wait=[d2_dv, ch["klq_free"]])
                ch["d2_free"] = mark("AC")
                op("scalar", lambda e: e.activation(out=V["eblv"], in_=V["blv"], func=AF.Exp,
                                                    scale=1.0 / GN),
                   inc=[("AC", 1)], wait=[ch["eblv_free"]])
                exps_ac = mark("AC")
                op("gpsimd", lambda e, q_=q3:
                   e.tensor_tensor(out=V["qe"], in0=q_,
                                   in1=V["ebq"].rearrange("p (hh t) -> p hh t", hh=H), op=ALU.mult),
                   inc=[("PL", 1)], wait=[em_ac, ch["qe_free"], qk_done if c == 0 else NOW])
                op("gpsimd", lambda e, k_=k3:
                   e.tensor_tensor(out=V["kd"], in0=k_,
                                   in1=V["emq"].rearrange("p (hh t) -> p hh t", hh=H), op=ALU.mult),
                   inc=[("PL", 1)], wait=[ch["kd_free"]])
                ch["ebq_free"] = mark("PL")
                ch["emq_free"] = mark("PL")
                kd_pl = mark("PL")
                op("gpsimd", lambda e, k_=k3:
                   e.tensor_tensor(out=V["kl"], in0=k_,
                                   in1=V["klq"].rearrange("p (hh t) -> p hh t", hh=H), op=ALU.mult),
                   inc=[("PL", 1)], wait=[exps_ac, ch["kl_free"]])
                ch["klq_free"] = mark("PL")
                kl_pl = mark("PL")
                for hh in range(H):
                    w_ = [kd_pl, ch["at_ps"]] if hh == 0 else []
                    op("tensor", lambda e, h2=hh:
                       e.matmul(out=PC[base:base + 64, h2 * 64:(h2 + 1) * 64],
                                lhsT=V["kd"][:, h2 * 64:(h2 + 1) * 64],
                                rhs=V["qe"][:, h2 * 64:(h2 + 1) * 64],
                                start=True, stop=True),
                       inc=[("PE", 1)] if hh == H - 1 else [], wait=w_)
                ch["kd_free"] = mark("PE")
                at_pe = mark("PE")
                op("vector", lambda e:
                   e.tensor_tensor(out=V["at"][base:base + 64, :],
                                   in0=PC[base:base + 64, 0:512],
                                   in1=V["umask"][base:base + 64, :]
                                   .unsqueeze(1).broadcast_to((64, H, CH)),
                                   op=ALU.mult),
                   inc=[("DV", 1)], wait=[at_pe, ch["at_free"]])
                ch["at_ps"] = mark("DV")
                at_dv = mark("DV")
                for hh in range(H):
                    w_ = [ch["s16"], ch["o_ps"]] if hh == 0 else []
                    op("tensor", lambda e, h2=hh:
                       e.matmul(out=PB[base:base + 64, h2 * 256:(h2 + 1) * 256],
                                lhsT=V["qe"][:, h2 * 64:(h2 + 1) * 64],
                                rhs=V["S16"][:, h2 * 256:(h2 + 1) * 256],
                                start=(h2 % 2 == 0), stop=False), wait=w_)
                for hh in range(H):
                    w_ = [kl_pl, ch["klt_ps"]] if hh == 0 else []
                    op("tensor", lambda e, h2=hh:
                       e.transpose(out=PCb[base:base + 64, h2 * 128:(h2 + 1) * 128],
                                   in_=V["kl"][:, h2 * 64:(h2 + 1) * 64],
                                   identity=V["ident"]),
                       inc=[("PE", 1)] if hh == H - 1 else [], wait=w_)
                ch["kl_free"] = mark("PE")
                kt_pe = mark("PE")
                op("vector", lambda e:
                   e.tensor_copy(out=V["klt"][base:base + 64, :], in_=PCb[base:base + 64, :]),
                   inc=[("DV", 1)], wait=[kt_pe, ch["klt_free"]])
                ch["klt_ps"] = mark("DV")
                kt_dv = mark("DV")
                for hh in range(H):
                    w_ = [at_dv, v_done["v"] if c == 0 else NOW] if hh == 0 else []
                    op("tensor", lambda e, h2=hh, p=pair:
                       e.matmul(out=PB[base:base + 64, h2 * 256:(h2 + 1) * 256],
                                lhsT=V["at"][base:base + 64, h2 * 64:(h2 + 1) * 64],
                                rhs=V["v"][base:base + 64, p * VD + h2 * 256:p * VD + (h2 + 1) * 256],
                                start=False, stop=(h2 % 2 == 1)),
                       inc=[("PE", 1)] if hh == H - 1 else [], wait=w_)
                ch["at_free"] = mark("PE")
                o_pe = mark("PE")
                op("scalar", lambda e, p=pair:
                   e.activation(out=V["o"][base:base + 64, p * VD:(p + 1) * VD],
                                in_=PB[base:base + 64, 0:2048], func=AF.Copy),
                   inc=[("AC", 1)],
                   wait=[o_pe] + ([dump_marks.get("h", NOW)] if c == 0 else []))
                ch["o_ps"] = mark("AC")
                o_evacs.append(mark("AC"))
                su_dv = [NOW, NOW]
                for hh in range(H):
                    w_ = [kt_dv] if hh == 0 else []
                    w_.append(ch["su_ps"][hh % 2] if hh < 2 else su_dv[hh % 2])
                    op("tensor", lambda e, h2=hh, p=pair:
                       e.matmul(out=PA[:, (h2 % 2) * 512:(h2 % 2) * 512 + 256],
                                lhsT=V["klt"][base:base + 64, h2 * 128:(h2 + 1) * 128],
                                rhs=V["v"][base:base + 64, p * VD + h2 * 256:p * VD + (h2 + 1) * 256],
                                start=True, stop=True),
                       inc=[("PE", 1)], wait=w_)
                    su_pe = mark("PE")
                    op("vector", lambda e, h2=hh:
                       e.scalar_tensor_tensor(out=V["S32"][:, h2 * 256:(h2 + 1) * 256],
                                              in0=V["S32"][:, h2 * 256:(h2 + 1) * 256],
                                              scalar=V["eblv"][:, h2:h2 + 1],
                                              in1=PA[:, (h2 % 2) * 512:(h2 % 2) * 512 + 256],
                                              op0=ALU.mult, op1=ALU.add),
                       inc=[("DV", 1)],
                       wait=[su_pe, exps_ac] + ([E["s16_prev"], E.pop("sdump", NOW)] if hh == 0 else []))
                    su_dv[hh % 2] = mark("DV")
                ch["klt_free"] = mark("PE")
                ch["su_ps"] = [su_dv[0], su_dv[1]]
                ch["eblv_free"] = mark("DV")
                su_all = mark("DV")
                op("scalar", lambda e: e.activation(out=V["S16"], in_=V["S32"], func=AF.Copy),
                   inc=[("AC", 1)], wait=[su_all])
                ch["s16"] = mark("AC")
                E["s16_prev"] = ch["s16"]
                ch["qe_free"] = o_pe
                if c % 2 == 1:
                    p = pair
                    su_pe_pair = mark("PE")
                    op("scalar", lambda e, pp=p:
                       e.activation(out=V["sqscr"], in_=V["o"][:, pp * VD:(pp + 1) * VD],
                                    func=AF.Square),
                       inc=[("AC", 1)],
                       wait=[o_evacs[-1], ch["sq_free"]] +
                            (list(E["xr_war"]) if p == 0 else []))
                    sq_ac = mark("AC")
                    op("vector", lambda e, pp=p:
                       e.tensor_reduce(out=V["ssq"][:, pp * 8:(pp + 1) * 8],
                                       in_=V["sqscr"].rearrange("pt (hh vv) -> pt hh vv", hh=H),
                                       axis=mybir.AxisListType.X, op=ALU.add),
                       inc=[("DV", 1)], wait=[sq_ac])
                    red_dv = mark("DV")
                    op("scalar", lambda e, pp=p:
                       e.activation(out=V["ssq"][:, pp * 8:(pp + 1) * 8],
                                    in_=V["ssq"][:, pp * 8:(pp + 1) * 8],
                                    func=AF.Ln, bias=EPS_RMS, scale=1.0 / HV),
                       inc=[("AC", 1)], wait=[red_dv])
                    op("scalar", lambda e, pp=p:
                       e.activation(out=V["r_o"][:, pp * 8:(pp + 1) * 8],
                                    in_=V["ssq"][:, pp * 8:(pp + 1) * 8],
                                    func=AF.Exp, scale=-0.5),
                       inc=[("AC", 1)], wait=[mark("AC")])
                    r_ac = mark("AC")
                    op("vector", lambda e, pp=p:
                       e.tensor_tensor(out=V["sqscr"],
                                       in0=V["o"][:, pp * VD:(pp + 1) * VD]
                                       .rearrange("pt (hh vv) -> pt hh vv", hh=H),
                                       in1=V["r_o"][:, pp * 8:(pp + 1) * 8]
                                       .unsqueeze(2).broadcast_to((128, H, HV)),
                                       op=ALU.mult),
                       inc=[("DV", 1)], wait=[r_ac])
                    t2_dv = mark("DV")
                    w_ = [t2_dv, su_pe_pair]
                    if p == 0:
                        w_ += [v_done["sgt"], dump_marks.get("v", NOW)]
                    op("gpsimd", lambda e, pp=p:
                       e.tensor_tensor(out=V["u"][:, pp * VD:(pp + 1) * VD], in0=V["sqscr"],
                                       in1=V["sgt"][:, pp * VD:(pp + 1) * VD], op=ALU.mult),
                       inc=[("PL", 1)], wait=w_)
                    ch["sq_free"] = mark("PL")
            u_done = mark("PL")
            gla_pe = mark("PE")
            dump_marks["o"] = dump("d_o", V["o"], s, ("AC", n("AC")))
            dump_marks["u"] = dump("d_u", V["u"], s, u_done)
            E["s16_ac"] = ch["s16"]
            E["pa_free"] = ch["su_ps"][1]
            E["pc_free"] = ch["klt_ps"]
            E["pb_free"] = ch["o_ps"]

            tp_ps_free = {0: E["pb_free"], 1: E["pb_free"]}
            for kk in range(VC):
                pbb = PBb if kk % 2 == 0 else PBb2
                for p in range(NPAIR):
                    w_ = []
                    if p == 0:
                        w_ = [tp_ps_free[kk % 2]]
                        if kk == 0:
                            w_.append(u_done)
                    op("tensor", lambda e, pp=p, k2=kk, pb=pbb:
                       e.transpose(out=pb[:, pp * 128:(pp + 1) * 128],
                                   in_=V["u"][:, pp * VD + k2 * 128:pp * VD + (k2 + 1) * 128],
                                   identity=V["ident"]),
                       inc=[("PE", 1)] if p == NPAIR - 1 else [], wait=w_)
                tp_pe = mark("PE")
                w_ = [tp_pe]
                if kk == 0:
                    w_.append(o_evacs[-1])
                    w_.append(u_done)
                    w_.append(dump_marks.get("o", NOW))
                op("vector", lambda e, k2=kk, pb=pbb:
                   e.tensor_copy(out=V["ufm"][:, k2 * TS:(k2 + 1) * TS], in_=pb),
                   inc=[("DV", 1)], wait=w_)
                tp_ps_free[kk % 2] = mark("DV")
            ufm_done = mark("DV")
            E["pb_free"] = ufm_done
            col0, _ = WC_SECS["wo"]
            sec_gate("wo", s)
            wo_ps_free = {0: E["pa_free"], 1: E["pc_free"]}
            x0w_free = {0: u_done, 1: u_done}
            x1_marks = {}
            for o in range(DC):
                slot, wl_t = ws.load(ctx, col0 + o * VC * 128)
                ps = PA[:] if o % 2 == 0 else PC[:]
                xw = V["x0w0"] if o % 2 == 0 else V["x0w1"]
                op("scalar", lambda e, d=xw, oo=o:
                   e.dma_start(out=d, in_=x0_d[oo * 128:(oo + 1) * 128, t0:t0 + TS]),
                   inc=[("XW%d" % (o % 2), 16)], wait=[x0w_free[o % 2]])
                xw_xl = mark("XW%d" % (o % 2))
                for kk in range(VC):
                    for nn2 in range(2):
                        w_ = []
                        if kk == 0 and nn2 == 0:
                            w_ = [wl_t, ufm_done, wo_ps_free[o % 2]]
                        op("tensor", lambda e, p=ps, sl=slot, k2=kk, n2=nn2:
                           e.matmul(out=p[:, n2 * 512:(n2 + 1) * 512],
                                    lhsT=ws.buf(sl, k2 * 128, 128),
                                    rhs=V["ufm"][:, k2 * TS + n2 * 512:k2 * TS + n2 * 512 + 512],
                                    start=(k2 == 0), stop=(k2 == VC - 1)),
                           inc=([("PE", 1)] if (kk == VC - 1 and nn2 == 1) else []),
                           wait=w_)
                ws.slot_free[slot] = mark("PE")
                wo_pe = mark("PE")
                w_ = [wo_pe, xw_xl]
                if o == 0:
                    w_ += [u_done, dump_marks.get("q", NOW), dump_marks.get("k", NOW)]
                op("vector", lambda e, oo=o, p=ps, x=xw:
                   e.scalar_tensor_tensor(out=V["x1"][:, oo * TS:(oo + 1) * TS],
                                          in0=p, scalar=1.0, in1=x,
                                          op0=ALU.mult, op1=ALU.add),
                   inc=[("DV", 1)], wait=w_)
                wo_ps_free[o % 2] = mark("DV")
                x0w_free[o % 2] = mark("DV")
                x1_marks[o] = mark("DV")
                if s == 0:
                    drain2(10, extra=[g_read_pl])
            E["pa_free"] = wo_ps_free[0]
            E["pc_free"] = wo_ps_free[1]
            dump_marks["x1"] = dump("d_x1", V["x1"], s, ("DV", n("DV")))

            rb2 = {}
            sq2war = dict(E["sq2_war"])
            for c in range(DC):
                sq2 = V["sq2_0"] if c % 2 == 0 else V["sq2_1"]
                op("scalar", lambda e, o_=sq2, cc=c:
                   e.activation(out=o_, in_=V["x1"][:, cc * TS:(cc + 1) * TS], func=AF.Square),
                   inc=[("AC", 1)], wait=[x1_marks[c], sq2war[c % 2]])
                sq_ac = mark("AC")
                for gg in range(2):
                    w_ = [sq_ac] if gg == 0 else []
                    if c == 0 and gg == 0:
                        w_.append(E["pb_free"])
                    op("tensor", lambda e, r=sq2, cc=c, g2=gg:
                       e.matmul(out=PB[0:1, g2 * 512:g2 * 512 + 512], lhsT=V["ones_b"][:, 0:1],
                                rhs=r[:, g2 * 512:g2 * 512 + 512],
                                start=(cc == 0), stop=(cc == DC - 1)),
                       inc=[("PE", 1)], wait=w_)
                sq2war[c % 2] = mark("PE")
                if s == 0:
                    drain2(6, extra=[g_read_pl])
            E["sq2_war"] = sq2war
            sum_pe = mark("PE")
            for grp in range(2):
                op("scalar", lambda e, gg=grp:
                   e.activation(out=V["rfs"], in_=PB[0:1, gg * 512:gg * 512 + 512],
                                func=AF.Ln, bias=EPS_RMS, scale=1.0 / D),
                   inc=[("AC", 1)], wait=[sum_pe] + ([rb2[0]] if grp == 1 else []))
                op("scalar", lambda e: e.activation(out=V["rfs"], in_=V["rfs"],
                                                    func=AF.Exp, scale=-0.5),
                   inc=[("AC", 1)], wait=[mark("AC")])
                r2_ac = mark("AC")
                op("tensor", lambda e, gg=grp:
                   e.matmul(out=PB[:, 1024 + gg * 512:1024 + gg * 512 + 512],
                            lhsT=V["ones_f"][0:1, :], rhs=V["rfs"], start=True, stop=True),
                   inc=[("PE", 1)], wait=[r2_ac])
                rb2[grp] = mark("PE")
            for grp in range(2):
                for c in range(DC):
                    w_ = [rb2[grp]]
                    if c == 0 and grp == 0:
                        w_ += [v_done["sgt"], u_done, dump_marks.get("sg", NOW)]
                    op("vector", lambda e, cc=c, gg=grp:
                       e.scalar_tensor_tensor(
                           out=V["h2"][:, cc * TS + gg * 512:cc * TS + gg * 512 + 512],
                           in0=V["x1"][:, cc * TS + gg * 512:cc * TS + gg * 512 + 512],
                           scalar=V["fw"][:, cc:cc + 1],
                           in1=PB[:, 1024 + gg * 512:1024 + gg * 512 + 512],
                           op0=ALU.mult, op1=ALU.mult),
                       inc=[("DV", 1)], wait=w_)
            h2_done = mark("DV")
            E["pb_free"] = h2_done
            dump_marks["h2"] = dump("d_h2", V["h2"], s, h2_done)

            col0, _ = WC_SECS["fgu"]
            sec_gate("fgu", s)
            ffn_ps_free = {0: [E["pa_free"], E["pc_free"]], 1: [E["pb_free"]]}
            fsc_free = NOW
            cur = None
            for m in range(FC):
                for kk in range(DC):
                    cl = m * DC * 256 + kk * 256
                    if cl % SLAB == 0:
                        cur = ws.load(ctx, col0 + cl)
                    sl, wlt = cur
                    off = cl % SLAB
                    if m % 2 == 0:
                        psG, psU = PA[:], PC[:]
                    else:
                        psG, psU = PB[:, 0:1024], PB[:, 1024:2048]
                    for gu in range(2):
                        pp = psG if gu == 0 else psU
                        for nn2 in range(2):
                            w_ = []
                            if kk == 0 and gu == 0 and nn2 == 0:
                                w_ = [wlt, h2_done] + ffn_ps_free[m % 2]
                            elif off == 0 and gu == 0 and nn2 == 0:
                                w_ = [wlt]
                            retire = (((cl + 256) % SLAB == 0 or (m == FC - 1 and kk == DC - 1))
                                      and gu == 1 and nn2 == 1)
                            inc_ = [("PE", 1)] if (retire or (kk == DC - 1 and gu == 1 and nn2 == 1)) else []
                            op("tensor", lambda e, p=pp, s_=sl, of=off, g2=gu, k2=kk, n2=nn2:
                               e.matmul(out=p[:, n2 * 512:(n2 + 1) * 512],
                                        lhsT=ws.buf(s_, of + g2 * 128, 128),
                                        rhs=V["h2"][:, k2 * TS + n2 * 512:k2 * TS + n2 * 512 + 512],
                                        start=(k2 == 0), stop=(k2 == DC - 1)),
                               inc=inc_, wait=w_)
                            if retire:
                                ws.slot_free[sl] = mark("PE")
                mm_pe = mark("PE")
                if m % 2 == 0:
                    psG, psU = PA[:], PC[:]
                else:
                    psG, psU = PB[:, 0:1024], PB[:, 1024:2048]
                op("scalar", lambda e, p=psG: e.activation(out=V["klt"], in_=p, func=AF.Silu),
                   inc=[("AC", 1)], wait=[mm_pe, fsc_free])
                sil_ac = mark("AC")
                if s == 0:
                    if m == 32:
                        drain2(len(p2items), extra=[g_read_pl])
                        E["p2_done"] = [mark("WC0"), mark("WC1")]
                    else:
                        drain2(4, extra=[g_read_pl])
                w_ = [sil_ac]
                if m == 0:
                    w_ += [u_done, ufm_done, x1_marks[DC - 1],
                           dump_marks.get("u", NOW)]
                if m == 32:
                    w_ += [g_read_pl, dump_marks.get("g", NOW)]
                    if s == 0:
                        w_ += E["p2_done"]
                op("vector", lambda e, mm=m, p=psU:
                   e.tensor_tensor(out=m_sl(mm, 0, TS), in0=V["klt"], in1=p, op=ALU.mult),
                   inc=[("DV", 1)], wait=w_)
                fsc_free = mark("DV")
                ffn_ps_free[m % 2] = [mark("DV")]
            m_done = mark("DV")
            E["pa_free"] = m_done
            E["pc_free"] = m_done
            E["pb_free"] = m_done

            col0, _ = WC_SECS["fd"]
            sec_gate("fd", s)
            fd_ps_free = {0: E["pa_free"], 1: E["pc_free"]}
            fsc2_free = fsc_free
            s_marks = {}
            cur = None
            for o in range(DC):
                ps = PA[:] if o % 2 == 0 else PC[:]
                xw = V["x0w0"] if o % 2 == 0 else V["x0w1"]
                op("scalar", lambda e, d=xw, oo=o:
                   e.dma_start(out=d, in_=x0_d[oo * 128:(oo + 1) * 128, t0:t0 + TS]),
                   inc=[("XW%d" % (o % 2), 16)], wait=[x0w_free[o % 2]])
                xw_xl = mark("XW%d" % (o % 2))
                for kk in range(FC):
                    cl = (o * FC + kk) * 128
                    if cl % SLAB == 0:
                        cur = ws.load(ctx, col0 + cl)
                    sl, wlt = cur
                    off = cl % SLAB
                    for nn2 in range(2):
                        w_ = []
                        if off == 0 and nn2 == 0:
                            w_.append(wlt)
                        if kk == 0 and nn2 == 0:
                            w_ += [m_done, fd_ps_free[o % 2]]
                        retire = (((cl + 128) % SLAB == 0 or (o == DC - 1 and kk == FC - 1))
                                  and nn2 == 1)
                        inc_ = [("PE", 1)] if (retire or (kk == FC - 1 and nn2 == 1)) else []
                        op("tensor", lambda e, p=ps, s_=sl, of=off, n2=nn2, k2=kk:
                           e.matmul(out=p[:, n2 * 512:(n2 + 1) * 512],
                                    lhsT=ws.buf(s_, of, 128),
                                    rhs=m_sl(k2, n2 * 512, 512),
                                    start=(k2 == 0), stop=(k2 == FC - 1)),
                           inc=inc_, wait=w_)
                        if retire:
                            ws.slot_free[s_ if False else sl] = mark("PE")
                fd_pe = mark("PE")
                op("vector", lambda e, oo=o, p=ps:
                   e.scalar_tensor_tensor(out=V["klt"], in0=p, scalar=1.0,
                                          in1=V["x1"][:, oo * TS:(oo + 1) * TS],
                                          op0=ALU.mult, op1=ALU.add),
                   inc=[("DV", 1)], wait=[fd_pe, fsc2_free])
                fd_ps_free[o % 2] = mark("DV")
                t_dv = mark("DV")
                op("vector", lambda e, oo=o, x=xw:
                   e.tensor_tensor(out=V["x1"][:, oo * TS:(oo + 1) * TS], in0=V["klt"],
                                   in1=x, op=ALU.add),
                   inc=[("DV", 1)],
                   wait=[xw_xl, t_dv] + ([dump_marks.get("x1", NOW)] if o == 0 else []))
                fsc2_free = mark("DV")
                x0w_free[o % 2] = mark("DV")
                s_dv = mark("DV")
                s_marks[o] = s_dv
                sq2 = V["sq2_0"] if o % 2 == 0 else V["sq2_1"]
                op("scalar", lambda e, oo=o, q_=sq2:
                   e.activation(out=q_, in_=V["x1"][:, oo * TS:(oo + 1) * TS], func=AF.Square),
                   inc=[("AC", 1)], wait=[s_dv, E["sq2_war"][o % 2]])
                sq_ac = mark("AC")
                for nn2 in range(2):
                    w_ = [s_dv] if nn2 == 0 else []
                    if o == 0 and nn2 == 0:
                        w_.append(E["pb_free"])
                    op("tensor", lambda e, oo=o, n2=nn2:
                       e.matmul(out=PB[0:1, n2 * 512:n2 * 512 + 512], lhsT=V["ones_b"][:, 0:1],
                                rhs=V["x1"][:, oo * TS + n2 * 512:oo * TS + n2 * 512 + 512],
                                start=(oo == 0), stop=(oo == DC - 1)), wait=w_)
                    op("tensor", lambda e, q_=sq2, n2=nn2, oo=o:
                       e.matmul(out=PB[0:1, 1024 + n2 * 512:1024 + n2 * 512 + 512],
                                lhsT=V["ones_b"][:, 0:1], rhs=q_[:, n2 * 512:n2 * 512 + 512],
                                start=(oo == 0), stop=(oo == DC - 1)),
                       inc=[("PE", 1)],
                       wait=[sq_ac] if nn2 == 0 else [])
                    E["sq2_war"][o % 2] = mark("PE")
            s_done = mark("DV")
            fd_all_pe = mark("PE")

            def emit_ln(sum_pe_mark, s_view, ph, out_view, fsc_war, region_war,
                        store_slices=None, store_war=NOW, row_war=NOW):
                tokens = 512
                pso = ph * 512
                op("scalar", lambda e: e.activation(out=V["mrow"][0:1, 0:tokens],
                                                    in_=PB[0:1, pso:pso + tokens],
                                                    func=AF.Copy, scale=1.0 / D),
                   inc=[("AC", 1)], wait=[sum_pe_mark, row_war])
                op("scalar", lambda e: e.activation(out=V["vrow"][0:1, 0:tokens],
                                                    in_=PB[0:1, 1024 + pso:1024 + pso + tokens],
                                                    func=AF.Copy, scale=1.0 / D),
                   inc=[("AC", 1)])
                mv_ac = mark("AC")
                op("vector", lambda e: e.tensor_tensor(out=V["trow"][0:1, 0:tokens],
                                                       in0=V["mrow"][0:1, 0:tokens],
                                                       in1=V["mrow"][0:1, 0:tokens], op=ALU.mult),
                   inc=[("DV", 1)], wait=[mv_ac])
                op("vector", lambda e: e.tensor_tensor(out=V["vrow"][0:1, 0:tokens],
                                                       in0=V["vrow"][0:1, 0:tokens],
                                                       in1=V["trow"][0:1, 0:tokens],
                                                       op=ALU.subtract),
                   inc=[("DV", 1)], wait=[mark("DV")])
                var_dv = mark("DV")
                op("scalar", lambda e: e.activation(out=V["vrow"][0:1, 0:tokens],
                                                    in_=V["vrow"][0:1, 0:tokens],
                                                    func=AF.Ln, bias=EPS_LN),
                   inc=[("AC", 1)], wait=[var_dv])
                op("scalar", lambda e: e.activation(out=V["vrow"][0:1, 0:tokens],
                                                    in_=V["vrow"][0:1, 0:tokens],
                                                    func=AF.Exp, scale=-0.5),
                   inc=[("AC", 1)], wait=[mark("AC")])
                r_ac = mark("AC")
                op("vector", lambda e:
                   e.scalar_tensor_tensor(out=V["mrow"][0:1, 0:tokens],
                                          in0=V["mrow"][0:1, 0:tokens], scalar=-1.0,
                                          in1=V["vrow"][0:1, 0:tokens],
                                          op0=ALU.mult, op1=ALU.mult),
                   inc=[("DV", 1)], wait=[r_ac])
                nm_dv = mark("DV")
                op("tensor", lambda e:
                   e.matmul(out=PB[:, 1024 + pso:1024 + pso + 512],
                            lhsT=V["ones_f"][0:1, :],
                            rhs=V["vrow"][0:1, 0:512],
                            start=True, stop=True), wait=[r_ac])
                op("tensor", lambda e:
                   e.matmul(out=PB[:, pso:pso + 512],
                            lhsT=V["ones_f"][0:1, :],
                            rhs=V["mrow"][0:1, 0:512],
                            start=True, stop=True),
                   inc=[("PE", 1)], wait=[nm_dv])
                bc_pe = mark("PE")
                emit_ln.last_bc = bc_pe
                op("vector", lambda e: e.tensor_copy(out=V["rsb2"][:, 0:tokens],
                                                     in_=PB[:, 1024 + pso:1024 + pso + tokens]),
                   inc=[("DV", 1)], wait=[bc_pe])
                op("vector", lambda e: e.tensor_copy(out=V["nm2"][:, 0:tokens],
                                                     in_=PB[:, pso:pso + tokens]),
                   inc=[("DV", 1)])
                bc_dv = mark("DV")
                outs, st_marks = {}, {}
                fw_ = fsc_war
                ob_war = dict(store_war) if isinstance(store_war, dict) else {0: store_war, 1: store_war}
                for c in range(DC):
                    w_ = [bc_dv, fw_]
                    if c == 0:
                        w_ += list(region_war) if isinstance(region_war, list) else [region_war]
                    op("gpsimd", lambda e, cc=c:
                       e.tensor_tensor(out=V["klt"][:, 0:tokens], in0=s_view(cc),
                                       in1=V["rsb2"][:, 0:tokens], op=ALU.mult),
                       inc=[("PL", 1)], wait=w_)
                    t_pl = mark("PL")
                    w_ = [t_pl]
                    if store_slices is not None:
                        w_.append(ob_war[c % 2])
                    op("vector", lambda e, cc=c:
                       e.tensor_tensor(out=out_view(cc), in0=V["klt"][:, 0:tokens],
                                       in1=V["nm2"][:, 0:tokens], op=ALU.add),
                       inc=[("DV", 1)], wait=w_)
                    fw_ = mark("DV")
                    outs[c] = mark("DV")
                    if store_slices is not None:
                        stsem = "ST%d" % (c % 2)
                        op("gpsimd", lambda e, cc=c:
                           e.dma_start(out=store_slices(cc), in_=out_view(cc)),
                           inc=[(stsem, 16)], wait=[outs[c], ob_war[c % 2]])
                        ob_war[c % 2] = mark(stsem)
                        st_marks[c] = mark(stsem)
                return outs, mark("DV"), st_marks, fw_

            fsc_after = fsc2_free
            row_war = E.get("ln_row_war", NOW)
            x3h = {}
            for ph in range(2):
                _, x3_done, _, fsc_after = emit_ln(
                    fd_all_pe,
                    lambda c, p_=ph: V["x1"][:, c * TS + p_ * 512:c * TS + p_ * 512 + 512], ph,
                    lambda c, p_=ph: V["x3"][:, c * TS + p_ * 512:c * TS + p_ * 512 + 512],
                    fsc_after, [m_done, dump_marks.get("h2", NOW)] if ph == 0 else NOW,
                    row_war=row_war)
                row_war = emit_ln.last_bc
                x3h[ph] = x3_done
            E["pb_free"] = x3_done
            dump_marks["x3"] = dump("d_x3", V["x3"], s, x3_done)

            qf_ps_free = {0: E["pa_free"], 1: E["pc_free"]}
            fsc_war = fsc_after
            f_region_war = fd_all_pe
            st_war = E.get("st_war", {0: NOW, 1: NOW})
            s2_read_done = NOW
            for half in range(2):
                th0 = half * 512
                col0, _ = WC_SECS["qgu"]
                sec_gate("qgu", s)
                cur = None
                for m in range(QC):
                    for kk in range(DC):
                        cl = m * DC * 256 + kk * 256
                        if cl % SLAB == 0:
                            cur = ws.load(ctx, col0 + cl)
                        sl, wlt = cur
                        off = cl % SLAB
                        psG = PA[:, 0:512] if m % 2 == 0 else PC[:, 0:512]
                        psU = PA[:, 512:1024] if m % 2 == 0 else PC[:, 512:1024]
                        for gu in range(2):
                            pp = psG if gu == 0 else psU
                            w_ = []
                            if kk == 0 and gu == 0:
                                w_ = [x3h[half], qf_ps_free[m % 2]]
                                if off == 0:
                                    w_.append(wlt)
                            elif off == 0 and gu == 0 and kk % (SLAB // 256) == 0:
                                w_ = [wlt]
                            retire = (((cl + 256) % SLAB == 0 or (m == QC - 1 and kk == DC - 1))
                                      and gu == 1)
                            inc_ = [("PE", 1)] if (retire or (kk == DC - 1 and gu == 1)) else []
                            op("tensor", lambda e, p=pp, s_=sl, of=off, g2=gu, k2=kk:
                               e.matmul(out=p,
                                        lhsT=ws.buf(s_, of + g2 * 128, 128),
                                        rhs=V["x3"][:, k2 * TS + th0:k2 * TS + th0 + 512],
                                        start=(k2 == 0), stop=(k2 == DC - 1)),
                               inc=inc_, wait=w_)
                            if retire:
                                ws.slot_free[sl] = mark("PE")
                    gu_pe = mark("PE")
                    psG = PA[:, 0:512] if m % 2 == 0 else PC[:, 0:512]
                    psU = PA[:, 512:1024] if m % 2 == 0 else PC[:, 512:1024]
                    op("scalar", lambda e, p=psG, mm=m:
                       e.activation(out=V["klt"][:, 0:512], in_=p, func=AF.Silu,
                                    scale=V["sg"][:, mm:mm + 1]),
                       inc=[("AC", 1)], wait=[gu_pe, fsc_war])
                    op("scalar", lambda e, p=psU, mm=m:
                       e.activation(out=V["klt"][:, 512:1024], in_=p, func=AF.Copy,
                                    scale=V["su"][:, mm:mm + 1]),
                       inc=[("AC", 1)])
                    sil_ac = mark("AC")
                    w_ = [sil_ac]
                    if m == 0:
                        w_.append(f_region_war)
                    op("vector", lambda e, mm=m:
                       e.tensor_tensor(out=V["f"][:, mm * 512:(mm + 1) * 512],
                                       in0=V["klt"][:, 0:512], in1=V["klt"][:, 512:1024],
                                       op=ALU.mult),
                       inc=[("DV", 1)], wait=w_)
                    fsc_war = mark("DV")
                    qf_ps_free[m % 2] = mark("DV")
                f_done = mark("DV")
                col0d, _ = WC_SECS["qd"]
                sec_gate("qd", s)
                cur = None
                for o in range(DC):
                    ps = PA[:, 0:512] if o % 2 == 0 else PC[:, 0:512]
                    for kk in range(QC):
                        cl = (o * QC + kk) * 128
                        if cl % SLAB == 0:
                            cur = ws.load(ctx, col0d + cl)
                        sl, wlt = cur
                        off = cl % SLAB
                        w_ = []
                        if off == 0:
                            w_.append(wlt)
                        if kk == 0:
                            w_ += [f_done, qf_ps_free[o % 2]]
                        retire = ((cl + 128) % SLAB == 0 or (o == DC - 1 and kk == QC - 1))
                        inc_ = [("PE", 1)] if (retire or kk == QC - 1) else []
                        op("tensor", lambda e, p=ps, s_=sl, of=off, k2=kk:
                           e.matmul(out=p, lhsT=ws.buf(s_, of, 128),
                                    rhs=V["f"][:, k2 * 512:(k2 + 1) * 512],
                                    start=(k2 == 0), stop=(k2 == QC - 1)),
                           inc=inc_, wait=w_)
                        if retire:
                            ws.slot_free[sl] = mark("PE")
                    qd_pe = mark("PE")
                    w_ = [qd_pe]
                    if o == 0 and half == 0:
                        w_.append(x3_done)
                    op("vector", lambda e, oo=o, p=ps:
                       e.scalar_tensor_tensor(
                           out=V["s2"][:, oo * TS + th0:oo * TS + th0 + 512],
                           in0=p, scalar=V["sd"][:, oo:oo + 1],
                           in1=V["x3"][:, oo * TS + th0:oo * TS + th0 + 512],
                           op0=ALU.mult, op1=ALU.add),
                       inc=[("DV", 1)], wait=w_)
                    qf_ps_free[o % 2] = mark("DV")
                    s2_dv = mark("DV")
                    sq2 = V["sq2_0"] if o % 2 == 0 else V["sq2_1"]
                    op("scalar", lambda e, oo=o, q_=sq2:
                       e.activation(out=q_[:, 0:512],
                                    in_=V["s2"][:, oo * TS + th0:oo * TS + th0 + 512],
                                    func=AF.Square),
                       inc=[("AC", 1)], wait=[s2_dv, E["sq2_war"][o % 2]])
                    sq_ac = mark("AC")
                    w0 = [s2_dv] + ([E["pb_free"]] if o == 0 else [])
                    op("tensor", lambda e, oo=o:
                       e.matmul(out=PB[0:1, 0:512], lhsT=V["ones_b"][:, 0:1],
                                rhs=V["s2"][:, oo * TS + th0:oo * TS + th0 + 512],
                                start=(oo == 0), stop=(oo == DC - 1)), wait=w0)
                    op("tensor", lambda e, q_=sq2, oo=o:
                       e.matmul(out=PB[0:1, 1024:1536], lhsT=V["ones_b"][:, 0:1],
                                rhs=q_[:, 0:512], start=(oo == 0), stop=(oo == DC - 1)),
                       inc=[("PE", 1)], wait=[sq_ac])
                    E["sq2_war"][o % 2] = mark("PE")
                f_region_war = mark("PE")
                qsum_pe = mark("PE")
                outs, fin, st_marks, fsc_after2 = emit_ln(
                    qsum_pe,
                    lambda c: V["s2"][:, c * TS + th0:c * TS + th0 + 512], 0,
                    lambda c: (V["outb0"] if c % 2 == 0 else V["outb1"]),
                    fsc_war,
                    x0w_free[0] if half == 0 else NOW,
                    store_slices=lambda c: y_d[c * 128:(c + 1) * 128, t0 + th0:t0 + th0 + 512],
                    store_war=st_war, row_war=row_war)
                row_war = emit_ln.last_bc
                E["ln_row_war"] = row_war
                fsc_war = fsc_after2
                st_war = {0: mark("ST0"), 1: mark("ST1")}
                E["st_war"] = st_war
                last_st0, last_st1 = mark("ST0"), mark("ST1")
                E["pb_free"] = fin
                s2_read_done = fin
            E["h_region"] = f_region_war
            E["v_region"] = f_region_war
            E["sg_region"] = s2_read_done
            E["qk_region"] = s2_read_done
            E["g_region"] = fd_all_pe
            E["pa_free"] = qf_ps_free[0]
            E["pc_free"] = qf_ps_free[1]
            E["xr_war"] = [last_st0, last_st1, x0w_free[1], x0w_free[1]]
            E["sqx_war"] = {0: s2_read_done, 1: s2_read_done}

            if debug and s == 0:
                op("gpsimd", lambda e: e.dma_start(out=dbg["d_S"], in_=V["S32"]),
                   inc=[("ST0", 16)],
                   wait=[("ST0", n("ST0")), ("DV", n("DV"))])
                E["sdump"] = mark("ST0")

        wt("gpsimd", ("ST0", n("ST0")), ("ST1", n("ST1")),
           ("WC0", n("WC0")), ("WC1", n("WC1")))

    with nc.Block() as block:
        for engname, starter in [("sync", block.sync), ("scalar", block.scalar),
                                 ("vector", block.vector), ("tensor", block.tensor),
                                 ("gpsimd", block.gpsimd)]:
            def run(eng, engname=engname):
                build_program(Ctx(engname, eng, semh))
            starter(run)

    for cm in sem_cms:
        cm.__exit__(None, None, None)
    arena_cm.__exit__(None, None, None)
    return nc


def _prep_weights(inputs):
    f32 = np.float32
    W = {}
    W["wv_s"] = lay_rhs_nh(np.asarray(inputs["L0_attn_wv"], f32), DC)
    W["wg_s"] = lay_rhs_nh(np.asarray(inputs["L0_attn_wg"], f32), DC)
    W["wq_s"] = lay_lhsT(np.asarray(inputs["L0_attn_wq"], f32), DC, KC)
    W["wk_s"] = lay_lhsT(np.asarray(inputs["L0_attn_wk"], f32), DC, KC)
    gk1 = np.asarray(inputs["L0_attn_gk_w1"], f32)
    W["gk1_s"] = np.ascontiguousarray(gk1.reshape(DC, 128, LR).transpose(1, 0, 2).reshape(128, DC * LR))
    W["wo_s"] = lay_lhsT(np.asarray(inputs["L0_attn_wo"], f32), VC, DC)
    W["fgu_s"] = lay_gu(np.asarray(inputs["L0_attn_w_gate"], f32),
                        np.asarray(inputs["L0_attn_w_up"], f32), DC, FC)
    W["fd_s"] = lay_lhsT(np.asarray(inputs["L0_attn_w_down"], f32), FC, DC)
    W["qgu_s"] = lay_gu(np.asarray(inputs["L0_ffn_w_gate"]),
                        np.asarray(inputs["L0_ffn_w_up"]), DC, QC)
    W["qd_s"] = lay_lhsT(np.asarray(inputs["L0_ffn_w_down"]), QC, DC)
    W["gk2_s"] = np.ascontiguousarray(np.asarray(inputs["L0_attn_gk_w2"], f32))
    W["nw_s"] = lay_vec(inputs["L0_attn_norm1_w"])
    W["fw_s"] = lay_vec(inputs["L0_attn_ffn_norm_w"])
    W["gb_s"] = lay_vec(inputs["L0_attn_gk_b"])
    W["sg_s"] = lay_vec(inputs["L0_ffn_s_gate"])
    W["su_s"] = lay_vec(inputs["L0_ffn_s_up"])
    W["sd_s"] = lay_vec(inputs["L0_ffn_s_down"])
    gnw = np.asarray(inputs["L0_attn_gnorm_w"], f32)
    W["gnw_s"] = np.ascontiguousarray(gnw.reshape(2, 128).T)
    cst = np.zeros((128, 192), dtype=ml_dtypes.bfloat16)
    cst[:, 0:128] = np.eye(128, dtype=ml_dtypes.bfloat16)
    um = np.triu(np.ones((64, 64), np.float32)).astype(ml_dtypes.bfloat16)
    cst[0:64, 128:192] = um
    cst[64:128, 128:192] = um
    W["cst_s"] = cst
    return W


def kernel(**inputs):
    x = np.asarray(inputs["L0_attn_x"], np.float32)
    B = x.shape[0]
    W = _prep_weights(inputs)
    nc = build_nc(debug=False)
    in_maps = []
    for b in range(B):
        m = {"x0": np.ascontiguousarray(x[b].T)}
        m.update(W)
        in_maps.append(m)
    res = run_bass_kernel_spmd(nc, in_maps, core_ids=list(range(B)))
    out = np.stack([np.ascontiguousarray(res.results[b]["y"].T) for b in range(B)], axis=0)
    return out


# version 11
# speedup vs baseline: 1.0628x; 1.0628x over previous
import math
import numpy as np
import ml_dtypes
import concourse.bass as bass
import concourse.mybir as mybir
from concourse.bass_utils import run_bass_kernel_spmd

F32 = mybir.dt.float32
BF16 = mybir.dt.bfloat16
I8 = mybir.dt.int8
U8 = mybir.dt.uint8
AF = mybir.ActivationFunctionType
ALU = mybir.AluOpType

D, KD, VD, H, HK, HV = 2048, 1024, 2048, 8, 128, 256
LR, FH, QH, CH = 16, 5504, 8192, 64
GN = 16.0
QSCALE_LOG = -0.5 * math.log(128.0)
T_TOTAL, TS = 4096, 1024
NS, NCH, NPAIR = T_TOTAL // TS, TS // CH, TS // CH // 2
DC, KC, VC, FC, QC = D // 128, KD // 128, VD // 128, FH // 128, QH // 128
SLAB = 2048
NRING = 2
EPS_RMS, EPS_LN = 1e-6, 1e-5
KB = 1024

SEMS = ("WL0", "WL1", "XC", "XL0", "XL1", "XL2", "XL3", "XW0", "XW1",
        "PE", "DV", "AC", "PL", "ST0", "ST1", "WC0", "WC1", "WP0", "WP1")
NOW = (None, 0)


def cdiv(a, b):
    return (a + b - 1) // b


def _wc_sections():
    raw = [("wv", 2 * DC * 1024), ("wg", 2 * DC * 1024),
           ("wq", KC * DC * 128), ("wk", KC * DC * 128),
           ("gk1", DC * LR), ("wo", DC * VC * 128),
           ("fgu", FC * DC * 256), ("fd", DC * FC * 128),
           ("qgu", QC * DC * 256), ("qd", DC * QC * 128)]
    out, col = {}, 0
    for name, cols in raw:
        out[name] = (col, cols)
        col += cdiv(cols, SLAB) * SLAB
    return out, col


WC_SECS, WC_COLS = _wc_sections()


def lay_lhsT(W, kc, oc):
    return np.ascontiguousarray(
        W.reshape(kc, 128, oc, 128).transpose(1, 2, 0, 3).reshape(128, oc * kc * 128))


def lay_gu(Wg, Wu, kc, oc):
    g = Wg.reshape(kc, 128, oc, 128).transpose(1, 2, 0, 3)
    u = Wu.reshape(kc, 128, oc, 128).transpose(1, 2, 0, 3)
    return np.ascontiguousarray(np.stack([g, u], axis=3).reshape(128, oc * kc * 256))


def lay_rhs_nh(W, kc):
    w = W.reshape(kc, 128, 2, 1024)
    return np.ascontiguousarray(w.transpose(1, 2, 0, 3).reshape(128, 2 * kc * 1024))


def lay_vec(v):
    n = v.shape[0] // 128
    return np.ascontiguousarray(np.asarray(v, np.float32).reshape(n, 128).T)


class Ctx:
    def __init__(self, engname, eng, semh):
        self.engname, self.eng, self.semh = engname, eng, semh
        self.s = {k: 0 for k in SEMS}

    def n(self, sem):
        return self.s[sem]

    def mark(self, sem):
        return (sem, self.s[sem])

    def op(self, engname, fn, inc=(), wait=()):
        if self.engname == engname:
            for s, v in wait:
                if s is not None and v > 0:
                    self.eng.wait_ge(self.semh[s], v)
            bi = fn(self.eng)
            for s, k in inc:
                bi.then_inc(self.semh[s], k)
        for s, k in inc:
            self.s[s] += k

    def wait(self, engname, *sv):
        if self.engname == engname:
            for s, v in sv:
                if s is not None and v > 0:
                    self.eng.wait_ge(self.semh[s], v)


class WStream:
    def __init__(self, wc_ap, ring_ap):
        self.wc, self.ring, self.i = wc_ap, ring_ap, 0
        self.slot_free = [(None, 0)] * NRING
        self.gate = []

    def load(self, ctx, col, ncols=SLAB, extra_wait=()):
        slot = self.i % NRING
        self.i += 1
        dst = self.ring[:, slot * SLAB:slot * SLAB + ncols]
        src_ = self.wc[:, col:col + ncols]
        sem = "WL%d" % slot
        w = list(extra_wait) + list(self.gate) + [self.slot_free[slot],
                                                  (sem, ctx.n(sem))]
        ctx.op("sync", lambda e: e.dma_start(out=dst, in_=src_), inc=[(sem, 16)], wait=w)
        return slot, ctx.mark(sem)

    def buf(self, slot, off=0, ncols=128):
        return self.ring[:, slot * SLAB + off:slot * SLAB + off + ncols]


def build_nc(debug=False):
    nc = bass.Bass("TRN2", target_bir_lowering=False, debug=False)

    x0_d = nc.dram_tensor("x0", [D, T_TOTAL], F32, kind="ExternalInput").ap()
    y_d = nc.dram_tensor("y", [D, T_TOTAL], F32, kind="ExternalOutput").ap()
    src = {}
    for name, shp, dt in [
        ("wv_s", [128, 2 * DC * 1024], F32), ("wg_s", [128, 2 * DC * 1024], F32),
        ("wq_s", [128, KC * DC * 128], F32), ("wk_s", [128, KC * DC * 128], F32),
        ("gk1_s", [128, DC * LR], F32), ("wo_s", [128, DC * VC * 128], F32),
        ("fgu_s", [128, FC * DC * 256], F32), ("fd_s", [128, DC * FC * 128], F32),
        ("qgu_s", [128, QC * DC * 256], I8), ("qd_s", [128, DC * QC * 128], I8),
        ("gk2_s", [16, KD], F32), ("nw_s", [128, 16], F32), ("fw_s", [128, 16], F32),
        ("gb_s", [128, 8], F32), ("sg_s", [128, 64], F32), ("su_s", [128, 64], F32),
        ("sd_s", [128, 16], F32), ("gnw_s", [128, 2], F32), ("cst_s", [128, 192], BF16),
    ]:
        src[name] = nc.dram_tensor(name, shp, dt, kind="ExternalInput").ap()
    wc = nc.dram_tensor("wc", [128, WC_COLS], BF16, kind="Internal").ap()

    dbg = {}
    if debug:
        for name, shp in [
            ("d_h", [128, DC * TS]), ("d_q", [128, KC * TS]), ("d_k", [128, KC * TS]),
            ("d_g", [128, KC * TS]), ("d_v", [128, NPAIR * VD]), ("d_sg", [128, NPAIR * VD]),
            ("d_o", [128, NPAIR * VD]), ("d_u", [128, NPAIR * VD]), ("d_x1", [128, DC * TS]),
            ("d_h2", [128, DC * TS]), ("d_x3", [128, DC * TS]),
        ]:
            dbg[name] = nc.dram_tensor(name, shp, BF16, kind="ExternalOutput").ap()
        dbg["d_S"] = nc.dram_tensor("d_S", [128, H * HV], F32, kind="ExternalOutput").ap()

    ARENA = 212800
    arena_cm = nc.sbuf_tensor("arena", [128, ARENA], U8)
    A = arena_cm.__enter__()

    def carve(off, p, cols, dt):
        bsz = {F32: 4, BF16: 2, I8: 1, U8: 1}[dt]
        assert off + cols * bsz <= ARENA, (off, cols, dt)
        return A[0:p, off:off + cols * bsz].bitcast(dt)

    V = {}
    V["ident"] = carve(0, 128, 128, BF16)
    V["umask"] = carve(256, 128, 64, BF16)
    V["ones_b"] = carve(384, 128, 64, BF16)
    V["ones_f"] = carve(512, 128, 128, F32)
    V["nw"] = carve(1024, 128, 16, F32)
    V["fw"] = carve(1088, 128, 16, F32)
    V["gb"] = carve(1152, 128, 8, F32)
    V["sg"] = carve(1184, 128, 64, F32)
    V["su"] = carve(1440, 128, 64, F32)
    V["sd"] = carve(1696, 128, 16, F32)
    V["gnw"] = carve(1760, 128, 2, F32)
    V["cb"] = carve(1768, 128, 4, F32)
    V["gk2"] = carve(1792, 16, KD, BF16)
    V["S32"] = carve(3840, 128, H * HV, F32)
    V["S16"] = carve(12032, 128, H * HV, BF16)
    V["b0"] = carve(16128, 128, 512, F32)
    V["b1"] = carve(18176, 128, 512, F32)
    V["d2"] = carve(20224, 128, 512, F32)
    V["ebq"] = carve(22272, 128, 512, BF16)
    V["emq"] = carve(23296, 128, 512, BF16)
    V["klq"] = carve(24320, 128, 512, BF16)
    V["qe"] = carve(25344, 128, 512, BF16)
    V["kd"] = carve(26368, 128, 512, BF16)
    V["kl"] = carve(27392, 128, 512, BF16)
    V["at"] = carve(28416, 128, 512, BF16)
    V["klt"] = carve(29440, 128, 1024, BF16)
    V["blv"] = carve(31488, 128, 8, F32)
    V["eblv"] = carve(31520, 128, 8, F32)
    V["ssq"] = carve(31552, 128, 64, F32)
    V["r_o"] = carve(31808, 128, 64, BF16)
    V["rfs"] = carve(31936, 1, 512, F32)
    V["sqscr"] = None
    V["sq2_0"] = carve(16128, 128, 1024, BF16)
    V["sq2_1"] = carve(18176, 128, 1024, BF16)
    V["mrow"] = carve(20224, 1, 1024, F32)
    V["vrow"] = carve(24320, 1, 1024, F32)
    V["ga1"] = carve(16128, 128, 1024, F32)
    V["ga2"] = carve(20224, 128, 1024, F32)
    V["ge1"] = carve(24320, 128, 1024, BF16)
    V["gl1"] = carve(26368, 128, 1024, BF16)
    V["t1sb"] = carve(33984, 16, 1024, BF16)
    V["trow"] = carve(33984, 1, 1024, F32)
    V["sqx0"] = carve(29440, 128, 512, BF16)
    V["sqx1"] = carve(30464, 128, 512, BF16)
    XS = 38080
    V["xr"] = [carve(XS + i * 2048, 128, 512, F32) for i in range(4)]
    V["sqscr"] = carve(XS, 128, VD, BF16)
    V["x0w0"] = carve(XS, 128, 1024, F32)
    V["x0w1"] = carve(XS + 4096, 128, 1024, F32)
    V["outb0"] = carve(XS, 128, 512, F32)
    V["outb1"] = carve(XS + 2048, 128, 512, F32)
    V["rsb2"] = carve(XS + 8192, 128, 512, BF16)
    V["nm2"] = carve(XS + 9216, 128, 512, BF16)
    RING_OFF = XS + 10240
    V["ring"] = carve(RING_OFF, 128, NRING * SLAB, BF16)
    M = RING_OFF + NRING * SLAB * 2
    V["h"] = carve(M, 128, DC * TS, BF16)
    V["v"] = carve(M + 32 * KB, 128, NPAIR * VD, BF16)
    V["sgt"] = carve(M + 64 * KB, 128, NPAIR * VD, BF16)
    V["q"] = carve(M + 96 * KB, 128, KC * TS, BF16)
    V["k"] = carve(M + 112 * KB, 128, KC * TS, BF16)
    V["g"] = carve(M + 128 * KB, 128, KC * TS, BF16)
    assert M + 144 * KB <= ARENA, M
    V["o"] = carve(M, 128, NPAIR * VD, BF16)
    V["u"] = carve(M + 32 * KB, 128, NPAIR * VD, BF16)
    V["ufm"] = carve(M, 128, VC * TS, BF16)
    V["x1"] = carve(M + 96 * KB, 128, DC * TS, BF16)
    V["h2"] = carve(M + 64 * KB, 128, DC * TS, BF16)
    V["mA"] = carve(M, 128, 32 * TS, BF16)
    V["mB"] = carve(M + 128 * KB, 128, 11 * TS, BF16)
    V["x3"] = carve(M + 64 * KB, 128, DC * TS, BF16)
    V["f"] = carve(M, 128, QC * 512, BF16)
    V["s2"] = carve(M + 96 * KB, 128, DC * TS, BF16)
    V["pstg"] = [carve(M + 128 * KB, 128, 8 * KB, U8), carve(M + 136 * KB, 128, 8 * KB, U8)]
    V["pout"] = [carve(M + 144 * KB, 128, SLAB, BF16), carve(M + 148 * KB, 128, SLAB, BF16)]

    def m_sl(j, lo, ncols):
        if j < 32:
            return V["mA"][:, j * TS + lo:j * TS + lo + ncols]
        return V["mB"][:, (j - 32) * TS + lo:(j - 32) * TS + lo + ncols]

    PA = nc.alloc_psum_tensor("PA", [128, 1024], F32)
    PC = nc.alloc_psum_tensor("PC", [128, 1024], F32)
    PB = nc.alloc_psum_tensor("PB", [128, 2048], F32)
    PBb = PB[:, 0:512].bitcast(BF16)
    PBb2 = PB[:, 512:1024].bitcast(BF16)
    PCb = PC[:, 512:1024].bitcast(BF16)

    nc.const_aps.register(F32, QSCALE_LOG, V["cb"][:, 3:4])
    nc.const_aps.register(F32, EPS_RMS, V["cb"][:, 0:1])
    nc.const_aps.register(F32, EPS_LN, V["cb"][:, 1:2])
    nc.const_aps.register(F32, 1.0, V["cb"][:, 2:3])
    sem_cms = [nc.semaphore(s) for s in SEMS]
    semh = {s: cm.__enter__() for s, cm in zip(SEMS, sem_cms)}

    def build_program(ctx: Ctx):
        op, wt, n, mark = ctx.op, ctx.wait, ctx.n, ctx.mark
        ws = WStream(wc, V["ring"])
        E = {}

        for dst, sname in [(V["nw"], "nw_s"), (V["fw"], "fw_s"), (V["gb"], "gb_s"),
                           (V["sg"], "sg_s"), (V["su"], "su_s"), (V["sd"], "sd_s"),
                           (V["gnw"], "gnw_s")]:
            op("scalar", lambda e, d=dst, s=sname: e.dma_start(out=d, in_=src[s]),
               inc=[("XC", 16)])
        op("scalar", lambda e: e.dma_start(out=A[0:128, 0:384].bitcast(BF16), in_=src["cst_s"]),
           inc=[("XC", 16)])
        op("scalar", lambda e: e.dma_start(out=V["ga1"][0:16, 0:1024], in_=src["gk2_s"]),
           inc=[("XC", 16)])
        nXC0 = n("XC")
        op("vector", lambda e: e.tensor_copy(out=V["gk2"], in_=V["ga1"][0:16, 0:1024]),
           wait=[("XC", nXC0)])
        op("vector", lambda e: e.memset(V["ones_b"], 1.0))
        op("vector", lambda e: e.memset(V["ones_f"], 1.0))
        op("vector", lambda e: e.memset(V["cb"][:, 0:1], EPS_RMS))
        op("vector", lambda e: e.memset(V["cb"][:, 1:2], EPS_LN))
        op("vector", lambda e: e.memset(V["cb"][:, 2:3], 1.0))
        op("vector", lambda e: e.memset(V["cb"][:, 3:4], QSCALE_LOG))
        op("vector", lambda e: e.memset(V["S32"], 0.0))
        op("vector", lambda e: e.memset(V["S16"], 0.0), inc=[("DV", 1)])
        E["s16_prev"] = mark("DV")
        E["s16_ac"] = mark("DV")
        E["init_dv"] = mark("DV")

        prep_marks = {}
        ps_ = {"si": 0, "load": {}, "cast": {}, "store": {}, "pend": []}
        p2items = []
        for sname, wname, dt in [("wv_s", "wv", F32), ("wg_s", "wg", F32),
                                 ("wq_s", "wq", F32), ("wk_s", "wk", F32),
                                 ("gk1_s", "gk1", F32), ("wo_s", "wo", F32),
                                 ("fgu_s", "fgu", F32), ("fd_s", "fd", F32)]:
            _, cols = WC_SECS[wname]
            nsl = cdiv(cols, SLAB)
            for j in range(nsl):
                p2items.append((sname, wname, dt, j, j == nsl - 1))
        p2i = [0]
        ceng_alt = [0]

        def next_ce(ceng):
            if ceng is not None:
                return ceng
            ceng_alt[0] += 1
            return "vector" if ceng_alt[0] % 2 == 0 else "scalar"

        def prep_load(extra=()):
            sname, wname, dt, j, last = p2items[p2i[0]]
            p2i[0] += 1
            si = ps_["si"]
            ps_["si"] = si + 1
            st = si % 2
            _, cols = WC_SECS[wname]
            ncols = min(SLAB, cols - j * SLAB)
            bsz = 4 if dt == F32 else 1
            stage = V["pstg"][st][:, 0:ncols * bsz].bitcast(dt)
            sem = "WP%d" % st
            op("sync" if st == 0 else "scalar",
               lambda e, sv=stage, sp=src[sname], c0=j * SLAB, nco=ncols:
               e.dma_start(out=sv, in_=sp[:, c0:c0 + nco]),
               inc=[(sem, 16)],
               wait=[ps_["cast"].get(si - 2, NOW), (sem, n(sem))] + list(extra))
            ps_["load"][si] = mark(sem)
            ps_["pend"].append((wname, dt, j, last, si, ncols, stage))

        def prep_cast_store(ceng):
            wname, dt, j, last, si, ncols, stage = ps_["pend"].pop(0)
            st = si % 2
            col0, _ = WC_SECS[wname]
            cwait = [ps_["load"][si], ps_["store"].get(si - 2, NOW)]
            if wname == "wo":
                ntile = ncols // 128
                for tti in range(ntile):
                    gt = (j * SLAB) // 128 + tti
                    kk2 = (gt % VC) % 2
                    op("vector", lambda e, o=V["pout"][st][:, tti * 128:(tti + 1) * 128],
                       s_=stage.bitcast(F32)[:, tti * 128:(tti + 1) * 128], k2=kk2:
                       e.tensor_scalar(out=o, in0=s_, scalar1=V["gnw"][:, k2:k2 + 1],
                                       scalar2=None, op0=ALU.mult),
                       inc=[("DV", 1)] if tti == ntile - 1 else [],
                       wait=cwait if tti == 0 else [])
                ps_["cast"][si] = mark("DV")
            elif dt != F32 or ceng == "vector":
                op("vector", lambda e, o=V["pout"][st][:, 0:ncols], s_=stage:
                   e.tensor_copy(out=o, in_=s_), inc=[("DV", 1)], wait=cwait)
                ps_["cast"][si] = mark("DV")
            else:
                op("scalar", lambda e, o=V["pout"][st][:, 0:ncols], s_=stage:
                   e.activation(out=o, in_=s_, func=AF.Copy),
                   inc=[("AC", 1)], wait=cwait)
                ps_["cast"][si] = mark("AC")
            wcsem = "WC%d" % st
            op("gpsimd", lambda e, o=V["pout"][st][:, 0:ncols], c0=col0 + j * SLAB, nco=ncols:
               e.dma_start(out=wc[:, c0:c0 + nco], in_=o),
               inc=[(wcsem, 16)], wait=[ps_["cast"][si]])
            ps_["store"][si] = mark(wcsem)
            if last:
                prep_marks[wname] = [mark("WC0"), mark("WC1")]

        def drain2(k, extra=(), ceng=None):
            while k > 0 and p2i[0] < len(p2items):
                if len(ps_["pend"]) >= 2:
                    prep_cast_store(next_ce(ceng))
                prep_load(list(extra))
                k -= 1

        def drain_flush(ceng=None):
            while ps_["pend"]:
                prep_cast_store(next_ce(ceng))

        def prep_quiesce():
            ms = []
            for d in (ps_["load"], ps_["cast"], ps_["store"]):
                for si in (ps_["si"] - 1, ps_["si"] - 2):
                    if si in d:
                        ms.append(d[si])
            return ms

        qf_items = []
        for sname, wname in (("qgu_s", "qgu"), ("qd_s", "qd")):
            col0, cols = WC_SECS[wname]
            nsl = cdiv(cols, SLAB)
            for j in range(nsl):
                qf_items.append((sname, wname, j * SLAB, min(SLAB, cols - j * SLAB),
                                 col0 + j * SLAB, j == nsl - 1))
        qf_di = [0]
        dstate = {"pend": [], "slot_store": [NOW] * NRING}

        def qf_prefetch(extra=()):
            if qf_di[0] >= len(qf_items):
                return
            sname, wname, c0s, ncols, wcc, last = qf_items[qf_di[0]]
            qf_di[0] += 1
            si = ps_["si"]
            ps_["si"] = si + 1
            st = si % 2
            stage = V["pstg"][st][:, 0:ncols].bitcast(I8)
            sem = "WP%d" % st
            op("sync" if st == 0 else "scalar",
               lambda e, sv=stage, sp=src[sname], c0=c0s, nco=ncols:
               e.dma_start(out=sv, in_=sp[:, c0:c0 + nco]),
               inc=[(sem, 16)],
               wait=[ps_["cast"].get(si - 2, NOW), (sem, n(sem))] + list(extra))
            ps_["load"][si] = mark(sem)
            dstate["pend"].append((wname, last, si, ncols, wcc, stage))

        def qdirect(col):
            wname, last, si, ncols, wcc, stage = dstate["pend"].pop(0)
            assert wcc == col, (wcc, col)
            st = si % 2
            slot = ws.i % NRING
            ws.i += 1
            op("vector", lambda e, o=ws.buf(slot, 0, ncols), s_=stage:
               e.tensor_copy(out=o, in_=s_), inc=[("DV", 1)],
               wait=[ps_["load"][si], ws.slot_free[slot], dstate["slot_store"][slot]])
            ps_["cast"][si] = mark("DV")
            wcsem = "WC%d" % st
            op("gpsimd", lambda e, o=ws.buf(slot, 0, ncols), c0=wcc, nco=ncols:
               e.dma_start(out=wc[:, c0:c0 + nco], in_=o),
               inc=[(wcsem, 16)], wait=[ps_["cast"][si]])
            ps_["store"][si] = mark(wcsem)
            dstate["slot_store"][slot] = ps_["store"][si]
            if last:
                prep_marks[wname] = [mark("WC0"), mark("WC1")]
            qf_prefetch()
            return slot, ps_["cast"][si]

        dump_marks = {}

        def dump(name, view, s, *waits):
            if not debug or s != 0:
                return NOW
            op("gpsimd", lambda e: e.dma_start(out=dbg[name], in_=view),
               inc=[("ST0", 16)],
               wait=[("ST0", n("ST0"))] + list(waits))
            return mark("ST0")

        xri = [0]

        def xload(c, t_lo, war):
            slot = xri[0] % 4
            xri[0] += 1
            buf = V["xr"][slot]
            sem = "XL%d" % slot
            op("scalar", lambda e: e.dma_start(
                out=buf, in_=x0_d[c * 128:c * 128 + 128, t_lo:t_lo + 512]),
                inc=[(sem, 16)], wait=[war])
            return buf, mark(sem), slot

        E["h_region"] = NOW
        E["v_region"] = NOW
        E["sg_region"] = NOW
        E["qk_region"] = NOW
        E["g_region"] = [NOW]
        E["pa_free"] = NOW
        E["pc_free"] = NOW
        E["pb_free"] = NOW
        E["xr_war"] = [NOW] * 4
        E["sqx_war"] = {0: NOW, 1: NOW}
        E["sq2_war"] = {0: NOW, 1: NOW}
        def sec_gate(name, s, extra=()):
            if s != 0:
                ws.gate = []
                return
            while name not in prep_marks:
                if len(ps_["pend"]) >= 2 or p2i[0] >= len(p2items):
                    prep_cast_store(next_ce(None))
                else:
                    prep_load(list(extra))
            ws.gate = prep_marks[name]

        for s in range(NS):
            t0 = s * TS

            sumA = {}
            for grp in range(2):
                tok0 = t0 + grp * 512
                for c in range(DC):
                    buf, xl, slot = xload(c, tok0, E["xr_war"][xri[0] % 4])
                    sqx = V["sqx0"] if c % 2 == 0 else V["sqx1"]
                    w_ = [xl, E["sqx_war"][c % 2]]
                    if s == 0 and grp == 0 and c == 0:
                        w_.append(E["init_dv"])
                    op("scalar", lambda e, o=sqx, i=buf: e.activation(out=o, in_=i, func=AF.Square),
                       inc=[("AC", 1)], wait=w_)
                    sq_ac = mark("AC")
                    E["xr_war"][slot] = sq_ac
                    w_ = [sq_ac]
                    if c == 0:
                        w_.append(E["pc_free"])
                        if s == 0 and grp == 0:
                            w_.append(E["init_dv"])
                    op("tensor", lambda e, r=sqx, cc=c, gg=grp:
                       e.matmul(out=PC[0:1, gg * 512:gg * 512 + 512], lhsT=V["ones_b"][:, 0:1],
                                rhs=r, start=(cc == 0), stop=(cc == DC - 1)),
                       inc=[("PE", 1)], wait=w_)
                    E["sqx_war"][c % 2] = mark("PE")
                    if s == 0:
                        drain2(1, ceng="vector")
                sumA[grp] = mark("PE")
            rb_m = {}
            for grp in range(2):
                op("scalar", lambda e, gg=grp:
                   e.activation(out=V["mrow"][0:1, gg * 512:gg * 512 + 512],
                                in_=PC[0:1, gg * 512:gg * 512 + 512],
                                func=AF.Ln, bias=EPS_RMS, scale=1.0 / D),
                   inc=[("AC", 1)], wait=[sumA[grp], E.get("mrow_war", NOW)])
                op("scalar", lambda e, gg=grp:
                   e.activation(out=V["mrow"][0:1, gg * 512:gg * 512 + 512],
                                in_=V["mrow"][0:1, gg * 512:gg * 512 + 512],
                                func=AF.Exp, scale=-0.5),
                   inc=[("AC", 1)], wait=[mark("AC")])
                rfs_m = mark("AC")
                op("tensor", lambda e, gg=grp:
                   e.matmul(out=PA[:, gg * 512:gg * 512 + 512], lhsT=V["ones_f"][0:1, :],
                            rhs=V["mrow"][0:1, gg * 512:gg * 512 + 512],
                            start=True, stop=True),
                   inc=[("PE", 1)], wait=[rfs_m] + ([E["pa_free"]] if grp == 0 else []))
                rb_m[grp] = mark("PE")
                op("scalar", lambda e, gg=grp, d=(V["rsb2"] if grp == 0 else V["nm2"]):
                   e.activation(out=d, in_=PA[:, gg * 512:gg * 512 + 512], func=AF.Copy),
                   inc=[("AC", 1)], wait=[rb_m[grp], E.get("rnb_war", NOW)])
                rb_m[grp] = mark("AC")
            E["pc_free"] = mark("AC")
            E["pa_free"] = mark("AC")
            E["mrow_war"] = rb_m[1]
            h_ch = {}
            tmp_war = {0: E.get("b0_war", NOW), 1: E.get("b0_war", NOW)}
            for c in range(DC):
                for grp in range(2):
                    rbuf = V["rsb2"] if grp == 0 else V["nm2"]
                    buf, xl, slot = xload(c, t0 + grp * 512, E["xr_war"][xri[0] % 4])
                    tmp = V["b0"] if grp == 0 else V["b1"]
                    w_ = [xl, tmp_war[grp]]
                    if c == 0 and grp == 0:
                        w_.append(E["h_region"])
                    op("gpsimd", lambda e, i=buf, cc=c, t_=tmp:
                       e.tensor_tensor(out=t_, in0=i,
                                       in1=V["nw"][:, cc:cc + 1].broadcast_to((128, 512)),
                                       op=ALU.mult),
                       inc=[("PL", 1)], wait=w_)
                    E["xr_war"][slot] = mark("PL")
                    t_pl = mark("PL")
                    op("gpsimd", lambda e, cc=c, gg=grp, rb=rbuf, t_=tmp:
                       e.tensor_tensor(
                           out=V["h"][:, cc * TS + gg * 512:cc * TS + gg * 512 + 512],
                           in0=t_, in1=rb, op=ALU.mult),
                       inc=[("PL", 1)], wait=[t_pl, rb_m[grp]])
                    tmp_war[grp] = mark("PL")
                    if s == 0 and grp == 0:
                        drain2(1, ceng="vector")
                h_ch[c] = mark("PL")
            hdone_grp = [mark("PL"), mark("PL")]
            E["rnb_war"] = mark("PL")
            E["b0_war"] = mark("PL")
            h_done = mark("PL")

            v_done = {}
            prev_evac = {"wv": E["v_region"], "wg": E["sg_region"]}
            vg_psum_war = NOW
            for mat, outv in (("wv", "v"), ("wg", "sgt")):
                col0, _ = WC_SECS[mat]
                eng, sem = ("scalar", "AC")
                pe_war = vg_psum_war
                sec_gate(mat, s)
                for nh in range(2):
                    for tcg in range(2):
                        for kk in range(DC):
                            if kk % 2 == 0:
                                cur_v = ws.load(ctx, col0 + (nh * DC + kk) * 1024)
                                if s == 0 and kk % 4 == 0:
                                    drain2(1)
                            slot, wl_t = cur_v
                            ro = (kk % 2) * 1024
                            for j in range(4):
                                tc = tcg * 4 + j
                                psl = [PA[:], PC[:], PB[:, 0:1024], PB[:, 1024:2048]][j]
                                lhs = V["h"][:, kk * TS + tc * 128:kk * TS + (tc + 1) * 128]
                                for nn2 in range(2):
                                    w_ = []
                                    if kk == 0 and nn2 == 0 and j == 0:
                                        w_ = [wl_t, pe_war, h_ch[kk + 1]]
                                        if mat == "wv" and nh == 0 and tcg == 0:
                                            w_ += [E["pa_free"], E["pc_free"], E["pb_free"]]
                                        if nh == 0 and tcg == 0:
                                            w_.append(prev_evac[mat])
                                    elif kk % 2 == 0 and nn2 == 0 and j == 0:
                                        w_ = [wl_t, h_ch[min(kk + 1, DC - 1)]]
                                    inc_ = []
                                    if kk % 2 == 1 and nn2 == 1 and j == 3:
                                        inc_.append(("PE", 1))
                                    op("tensor", lambda e, p=psl, l=lhs, sl=slot, r2=ro, n2=nn2, k2=kk:
                                       e.matmul(out=p[:, n2 * 512:(n2 + 1) * 512], lhsT=l,
                                                rhs=ws.buf(sl, r2 + n2 * 512, 512),
                                                start=(k2 == 0), stop=(k2 == DC - 1)),
                                       inc=inc_, wait=w_)
                                    if kk % 2 == 1 and nn2 == 1 and j == 3:
                                        ws.slot_free[slot] = mark("PE")
                        grp_pe = mark("PE")
                        for j in range(4):
                            tc = tcg * 4 + j
                            psl = [PA[:], PC[:], PB[:, 0:1024], PB[:, 1024:2048]][j]
                            dst = V[outv][:, tc * VD + nh * 1024:tc * VD + nh * 1024 + 1024]
                            fn_ = AF.Copy if mat == "wv" else AF.Silu
                            op(eng, lambda e, d=dst, p=psl, f_=fn_:
                               e.activation(out=d, in_=p, func=f_),
                               inc=[(sem, 1)], wait=[grp_pe] + ([prev_evac[mat]] if (nh == 0 and tcg == 0 and j == 0) else []))
                        pe_war = mark(sem)
                vg_psum_war = pe_war
                v_done[outv] = mark(sem)
            E["pa_free"] = v_done["sgt"]
            E["pc_free"] = v_done["sgt"]
            E["pb_free"] = v_done["sgt"]

            qk_ps_free = {0: E["pa_free"], 1: E["pc_free"]}
            for mat, outv in (("wq", "q"), ("wk", "k")):
                col0, _ = WC_SECS[mat]
                sec_gate(mat, s)
                region_free = E["qk_region"] if mat == "wq" else NOW
                for o in range(KC):
                    slot, wl_t = ws.load(ctx, col0 + o * DC * 128)
                    if s == 0:
                        drain2(1)
                    ps = PA[:] if o % 2 == 0 else PC[:]
                    for kk in range(DC):
                        for nn2 in range(2):
                            w_ = []
                            if kk == 0 and nn2 == 0:
                                w_ = [wl_t, qk_ps_free[o % 2], h_done]
                            op("tensor", lambda e, p=ps, sl=slot, k2=kk, n2=nn2:
                               e.matmul(out=p[:, n2 * 512:(n2 + 1) * 512],
                                        lhsT=ws.buf(sl, k2 * 128, 128),
                                        rhs=V["h"][:, k2 * TS + n2 * 512:k2 * TS + n2 * 512 + 512],
                                        start=(k2 == 0), stop=(k2 == DC - 1)),
                               inc=([("PE", 1)] if (kk == DC - 1 and nn2 == 1) else []),
                               wait=w_)
                    ws.slot_free[slot] = mark("PE")
                    pe_m = mark("PE")
                    op("scalar", lambda e, oo=o, p=ps:
                       e.activation(out=V[outv][:, oo * TS:(oo + 1) * TS], in_=p, func=AF.Copy),
                       inc=[("AC", 1)], wait=[pe_m, region_free])
                    region_free = NOW
                    qk_ps_free[o % 2] = mark("AC")
            qk_done = mark("AC")

            col0, _ = WC_SECS["gk1"]
            sec_gate("gk1", s)
            slot, wl_t = ws.load(ctx, col0, DC * LR)
            for kk in range(DC):
                for nn2 in range(2):
                    w_ = [wl_t, E["pb_free"], h_done] if (kk == 0 and nn2 == 0) else []
                    op("tensor", lambda e, sl=slot, k2=kk, n2=nn2:
                       e.matmul(out=PB[0:16, n2 * 512:(n2 + 1) * 512],
                                lhsT=ws.buf(sl, k2 * LR, LR),
                                rhs=V["h"][:, k2 * TS + n2 * 512:k2 * TS + n2 * 512 + 512],
                                start=(k2 == 0), stop=(k2 == DC - 1)),
                       inc=([("PE", 1)] if (kk == DC - 1 and nn2 == 1) else []),
                       wait=w_)
            ws.slot_free[slot] = mark("PE")
            t1_pe = mark("PE")
            op("vector", lambda e: e.tensor_copy(out=V["t1sb"], in_=PB[0:16, 0:1024]),
               inc=[("DV", 1)], wait=[t1_pe])
            t1_dv = mark("DV")
            E["pb_free"] = t1_dv
            if s == 0:
                drain2(2)
                drain_flush()
                E["g_quiesce"] = prep_quiesce()
            g_ps_free = dict(qk_ps_free)
            prev_e1, prev_l1, prev_gs = NOW, NOW, NOW
            for o in range(KC):
                ps = PA[:] if o % 2 == 0 else PC[:]
                for nn2 in range(2):
                    w_ = [t1_dv, g_ps_free[o % 2]] if nn2 == 0 else []
                    op("tensor", lambda e, oo=o, p=ps, n2=nn2:
                       e.matmul(out=p[:, n2 * 512:(n2 + 1) * 512],
                                lhsT=V["gk2"][:, oo * 128:(oo + 1) * 128],
                                rhs=V["t1sb"][:, n2 * 512:n2 * 512 + 512],
                                start=True, stop=True),
                       inc=[("PE", 1)] if nn2 == 1 else [], wait=w_)
                t2_pe = mark("PE")
                op("vector", lambda e, oo=o, p=ps:
                   e.tensor_scalar_add(out=V["ga2"], in0=p, scalar1=V["gb"][:, oo:oo + 1]),
                   inc=[("DV", 1)], wait=[t2_pe, prev_gs, prev_e1])
                g_ps_free[o % 2] = mark("DV")
                t3_dv = mark("DV")
                op("scalar", lambda e: e.activation(out=V["ga1"], in_=V["ga2"], func=AF.Abs),
                   inc=[("AC", 1)], wait=[t3_dv, prev_e1])
                a1_ac = mark("AC")
                op("scalar", lambda e: e.activation(out=V["ge1"], in_=V["ga1"],
                                                    func=AF.Exp, scale=-1.0),
                   inc=[("AC", 1)], wait=[a1_ac, prev_l1])
                e1_ac = mark("AC")
                prev_e1 = e1_ac
                op("scalar", lambda e: e.activation(out=V["gl1"], in_=V["ge1"],
                                                    func=AF.Ln, bias=1.0),
                   inc=[("AC", 1)], wait=[e1_ac, prev_gs])
                l1_ac = mark("AC")
                prev_l1 = l1_ac
                w_ = [l1_ac]
                if o == 0:
                    w_ += E["g_region"]
                    if s == 0:
                        w_ += E["g_quiesce"]
                op("vector", lambda e, oo=o:
                   e.scalar_tensor_tensor(out=V["g"][:, oo * TS:(oo + 1) * TS],
                                          in0=V["ga2"], scalar=0.0, in1=V["gl1"],
                                          op0=ALU.min, op1=ALU.subtract),
                   inc=[("DV", 1)], wait=w_)
                prev_gs = mark("DV")
            g_done = mark("DV")
            E["pa_free"] = g_ps_free[0]
            E["pc_free"] = g_ps_free[1]
            dump_marks["h"] = dump("d_h", V["h"], s, h_done)
            dump_marks["q"] = dump("d_q", V["q"], s, qk_done)
            dump_marks["k"] = dump("d_k", V["k"], s, qk_done)
            dump_marks["g"] = dump("d_g", V["g"], s, g_done)
            dump_marks["v"] = dump("d_v", V["v"], s, v_done["v"])
            dump_marks["sg"] = dump("d_sg", V["sgt"], s, v_done["sgt"])

            ch = {"b0_free": NOW, "b1_free": NOW, "d2_free": NOW, "ebq_free": NOW,
                  "emq_free": NOW, "klq_free": NOW, "eblv_free": NOW,
                  "qe_free": NOW, "kd_free": NOW, "kl_free": NOW,
                  "at_ps": E["pc_free"], "at_free": NOW, "o_ps": E["pb_free"],
                  "klt_ps": E["pc_free"], "klt_free": NOW,
                  "su_ps": [E["pa_free"], E["pa_free"]], "s16": E["s16_ac"],
                  "sq_free": NOW}
            o_evacs = []
            g_read_pl = NOW
            for c in range(NCH):
                pair, base = c // 2, 64 * (c % 2)
                co = c * CH
                g3 = V["g"].rearrange("p (hh t) -> p hh t", hh=KC)[:, :, co:co + CH]
                q3 = V["q"].rearrange("p (hh t) -> p hh t", hh=KC)[:, :, co:co + CH]
                k3 = V["k"].rearrange("p (hh t) -> p hh t", hh=KC)[:, :, co:co + CH]
                op("gpsimd", lambda e, g_=g3: e.tensor_copy(out=V["b0"], in_=g_),
                   inc=[("PL", 1)], wait=[g_done if c == 0 else NOW, ch["b0_free"]])
                b0_pl = mark("PL")
                g_read_pl = b0_pl
                for hh in range(H):
                    op("vector", lambda e, h2=hh:
                       e.tensor_tensor_scan(out=V["b1"][:, h2 * CH:(h2 + 1) * CH],
                                            data0=V["b0"][:, h2 * CH:(h2 + 1) * CH],
                                            data1=V["ones_f"][:, 0:CH], initial=0.0,
                                            op0=ALU.add, op1=ALU.mult),
                       inc=[("DV", 1)] if hh == H - 1 else [],
                       wait=[b0_pl, ch["b1_free"]] if hh == 0 else [])
                ch["b0_free"] = mark("DV")
                scan_dv = mark("DV")
                op("vector", lambda e:
                   e.tensor_copy(out=V["blv"],
                                 in_=V["b1"].rearrange("p (hh t) -> p hh t", hh=H)[:, :, CH - 1:CH].squeeze(2)),
                   inc=[("DV", 1)], wait=[scan_dv])
                blv_dv = mark("DV")
                op("scalar", lambda e: e.activation(out=V["ebq"], in_=V["b1"], func=AF.Exp,
                                                    scale=1.0 / GN, bias=QSCALE_LOG),
                   inc=[("AC", 1)], wait=[scan_dv, ch["ebq_free"]])
                op("scalar", lambda e: e.activation(out=V["emq"], in_=V["b1"], func=AF.Exp,
                                                    scale=-1.0 / GN),
                   inc=[("AC", 1)], wait=[ch["emq_free"]])
                em_ac = mark("AC")
                op("vector", lambda e:
                   e.tensor_tensor(out=V["d2"], in0=V["b1"],
                                   in1=V["blv"].unsqueeze(2).broadcast_to((128, H, CH)),
                                   op=ALU.subtract),
                   inc=[("DV", 1)], wait=[blv_dv, ch["d2_free"]])
                ch["b1_free"] = mark("DV")
                d2_dv = mark("DV")
                op("scalar", lambda e: e.activation(out=V["klq"], in_=V["d2"], func=AF.Exp,
                                                    scale=-1.0 / GN),
                   inc=[("AC", 1)], wait=[d2_dv, ch["klq_free"]])
                ch["d2_free"] = mark("AC")
                op("scalar", lambda e: e.activation(out=V["eblv"], in_=V["blv"], func=AF.Exp,
                                                    scale=1.0 / GN),
                   inc=[("AC", 1)], wait=[ch["eblv_free"]])
                exps_ac = mark("AC")
                op("gpsimd", lambda e, q_=q3:
                   e.tensor_tensor(out=V["qe"], in0=q_,
                                   in1=V["ebq"].rearrange("p (hh t) -> p hh t", hh=H), op=ALU.mult),
                   inc=[("PL", 1)], wait=[em_ac, ch["qe_free"], qk_done if c == 0 else NOW])
                op("gpsimd", lambda e, k_=k3:
                   e.tensor_tensor(out=V["kd"], in0=k_,
                                   in1=V["emq"].rearrange("p (hh t) -> p hh t", hh=H), op=ALU.mult),
                   inc=[("PL", 1)], wait=[ch["kd_free"]])
                ch["ebq_free"] = mark("PL")
                ch["emq_free"] = mark("PL")
                kd_pl = mark("PL")
                op("gpsimd", lambda e, k_=k3:
                   e.tensor_tensor(out=V["kl"], in0=k_,
                                   in1=V["klq"].rearrange("p (hh t) -> p hh t", hh=H), op=ALU.mult),
                   inc=[("PL", 1)], wait=[exps_ac, ch["kl_free"]])
                ch["klq_free"] = mark("PL")
                kl_pl = mark("PL")
                for hh in range(H):
                    w_ = [kd_pl, ch["at_ps"]] if hh == 0 else []
                    op("tensor", lambda e, h2=hh:
                       e.matmul(out=PC[base:base + 64, h2 * 64:(h2 + 1) * 64],
                                lhsT=V["kd"][:, h2 * 64:(h2 + 1) * 64],
                                rhs=V["qe"][:, h2 * 64:(h2 + 1) * 64],
                                start=True, stop=True),
                       inc=[("PE", 1)] if hh == H - 1 else [], wait=w_)
                ch["kd_free"] = mark("PE")
                at_pe = mark("PE")
                op("vector", lambda e:
                   e.tensor_tensor(out=V["at"][base:base + 64, :],
                                   in0=PC[base:base + 64, 0:512],
                                   in1=V["umask"][base:base + 64, :]
                                   .unsqueeze(1).broadcast_to((64, H, CH)),
                                   op=ALU.mult),
                   inc=[("DV", 1)], wait=[at_pe, ch["at_free"]])
                ch["at_ps"] = mark("DV")
                at_dv = mark("DV")
                for hh in range(H):
                    w_ = [ch["s16"], ch["o_ps"]] if hh == 0 else []
                    op("tensor", lambda e, h2=hh:
                       e.matmul(out=PB[base:base + 64, h2 * 256:(h2 + 1) * 256],
                                lhsT=V["qe"][:, h2 * 64:(h2 + 1) * 64],
                                rhs=V["S16"][:, h2 * 256:(h2 + 1) * 256],
                                start=(h2 % 2 == 0), stop=False), wait=w_)
                for hh in range(H):
                    w_ = [kl_pl, ch["klt_ps"]] if hh == 0 else []
                    op("tensor", lambda e, h2=hh:
                       e.transpose(out=PCb[base:base + 64, h2 * 128:(h2 + 1) * 128],
                                   in_=V["kl"][:, h2 * 64:(h2 + 1) * 64],
                                   identity=V["ident"]),
                       inc=[("PE", 1)] if hh == H - 1 else [], wait=w_)
                ch["kl_free"] = mark("PE")
                kt_pe = mark("PE")
                op("vector", lambda e:
                   e.tensor_copy(out=V["klt"][base:base + 64, :], in_=PCb[base:base + 64, :]),
                   inc=[("DV", 1)], wait=[kt_pe, ch["klt_free"]])
                ch["klt_ps"] = mark("DV")
                kt_dv = mark("DV")
                for hh in range(H):
                    w_ = [at_dv, v_done["v"] if c == 0 else NOW] if hh == 0 else []
                    op("tensor", lambda e, h2=hh, p=pair:
                       e.matmul(out=PB[base:base + 64, h2 * 256:(h2 + 1) * 256],
                                lhsT=V["at"][base:base + 64, h2 * 64:(h2 + 1) * 64],
                                rhs=V["v"][base:base + 64, p * VD + h2 * 256:p * VD + (h2 + 1) * 256],
                                start=False, stop=(h2 % 2 == 1)),
                       inc=[("PE", 1)] if hh == H - 1 else [], wait=w_)
                ch["at_free"] = mark("PE")
                o_pe = mark("PE")
                op("scalar", lambda e, p=pair:
                   e.activation(out=V["o"][base:base + 64, p * VD:(p + 1) * VD],
                                in_=PB[base:base + 64, 0:2048], func=AF.Copy),
                   inc=[("AC", 1)],
                   wait=[o_pe] + ([dump_marks.get("h", NOW)] if c == 0 else []))
                ch["o_ps"] = mark("AC")
                o_evacs.append(mark("AC"))
                su_dv = [NOW, NOW]
                for hh in range(H):
                    w_ = [kt_dv] if hh == 0 else []
                    w_.append(ch["su_ps"][hh % 2] if hh < 2 else su_dv[hh % 2])
                    op("tensor", lambda e, h2=hh, p=pair:
                       e.matmul(out=PA[:, (h2 % 2) * 512:(h2 % 2) * 512 + 256],
                                lhsT=V["klt"][base:base + 64, h2 * 128:(h2 + 1) * 128],
                                rhs=V["v"][base:base + 64, p * VD + h2 * 256:p * VD + (h2 + 1) * 256],
                                start=True, stop=True),
                       inc=[("PE", 1)], wait=w_)
                    su_pe = mark("PE")
                    op("vector", lambda e, h2=hh:
                       e.scalar_tensor_tensor(out=V["S32"][:, h2 * 256:(h2 + 1) * 256],
                                              in0=V["S32"][:, h2 * 256:(h2 + 1) * 256],
                                              scalar=V["eblv"][:, h2:h2 + 1],
                                              in1=PA[:, (h2 % 2) * 512:(h2 % 2) * 512 + 256],
                                              op0=ALU.mult, op1=ALU.add),
                       inc=[("DV", 1)],
                       wait=[su_pe, exps_ac] + ([E["s16_prev"], E.pop("sdump", NOW)] if hh == 0 else []))
                    su_dv[hh % 2] = mark("DV")
                ch["klt_free"] = mark("PE")
                ch["su_ps"] = [su_dv[0], su_dv[1]]
                ch["eblv_free"] = mark("DV")
                su_all = mark("DV")
                op("scalar", lambda e: e.activation(out=V["S16"], in_=V["S32"], func=AF.Copy),
                   inc=[("AC", 1)], wait=[su_all])
                ch["s16"] = mark("AC")
                E["s16_prev"] = ch["s16"]
                ch["qe_free"] = o_pe
                if c % 2 == 1:
                    p = pair
                    su_pe_pair = mark("PE")
                    op("scalar", lambda e, pp=p:
                       e.activation(out=V["sqscr"], in_=V["o"][:, pp * VD:(pp + 1) * VD],
                                    func=AF.Square),
                       inc=[("AC", 1)],
                       wait=[o_evacs[-1], ch["sq_free"]] +
                            (list(E["xr_war"]) if p == 0 else []))
                    sq_ac = mark("AC")
                    op("vector", lambda e, pp=p:
                       e.tensor_reduce(out=V["ssq"][:, pp * 8:(pp + 1) * 8],
                                       in_=V["sqscr"].rearrange("pt (hh vv) -> pt hh vv", hh=H),
                                       axis=mybir.AxisListType.X, op=ALU.add),
                       inc=[("DV", 1)], wait=[sq_ac])
                    red_dv = mark("DV")
                    op("scalar", lambda e, pp=p:
                       e.activation(out=V["ssq"][:, pp * 8:(pp + 1) * 8],
                                    in_=V["ssq"][:, pp * 8:(pp + 1) * 8],
                                    func=AF.Ln, bias=EPS_RMS, scale=1.0 / HV),
                       inc=[("AC", 1)], wait=[red_dv])
                    op("scalar", lambda e, pp=p:
                       e.activation(out=V["r_o"][:, pp * 8:(pp + 1) * 8],
                                    in_=V["ssq"][:, pp * 8:(pp + 1) * 8],
                                    func=AF.Exp, scale=-0.5),
                       inc=[("AC", 1)], wait=[mark("AC")])
                    r_ac = mark("AC")
                    op("vector", lambda e, pp=p:
                       e.tensor_tensor(out=V["sqscr"],
                                       in0=V["o"][:, pp * VD:(pp + 1) * VD]
                                       .rearrange("pt (hh vv) -> pt hh vv", hh=H),
                                       in1=V["r_o"][:, pp * 8:(pp + 1) * 8]
                                       .unsqueeze(2).broadcast_to((128, H, HV)),
                                       op=ALU.mult),
                       inc=[("DV", 1)], wait=[r_ac])
                    t2_dv = mark("DV")
                    w_ = [t2_dv, su_pe_pair]
                    if p == 0:
                        w_ += [v_done["sgt"], dump_marks.get("v", NOW)]
                    op("gpsimd", lambda e, pp=p:
                       e.tensor_tensor(out=V["u"][:, pp * VD:(pp + 1) * VD], in0=V["sqscr"],
                                       in1=V["sgt"][:, pp * VD:(pp + 1) * VD], op=ALU.mult),
                       inc=[("PL", 1)], wait=w_)
                    ch["sq_free"] = mark("PL")
            u_done = mark("PL")
            gla_pe = mark("PE")
            dump_marks["o"] = dump("d_o", V["o"], s, ("AC", n("AC")))
            dump_marks["u"] = dump("d_u", V["u"], s, u_done)
            E["s16_ac"] = ch["s16"]
            E["pa_free"] = ch["su_ps"][1]
            E["pc_free"] = ch["klt_ps"]
            E["pb_free"] = ch["o_ps"]

            tp_ps_free = {0: E["pb_free"], 1: E["pb_free"]}
            for kk in range(VC):
                pbb = PBb if kk % 2 == 0 else PBb2
                for p in range(NPAIR):
                    w_ = []
                    if p == 0:
                        w_ = [tp_ps_free[kk % 2]]
                        if kk == 0:
                            w_.append(u_done)
                    op("tensor", lambda e, pp=p, k2=kk, pb=pbb:
                       e.transpose(out=pb[:, pp * 128:(pp + 1) * 128],
                                   in_=V["u"][:, pp * VD + k2 * 128:pp * VD + (k2 + 1) * 128],
                                   identity=V["ident"]),
                       inc=[("PE", 1)] if p == NPAIR - 1 else [], wait=w_)
                tp_pe = mark("PE")
                w_ = [tp_pe]
                if kk == 0:
                    w_.append(o_evacs[-1])
                    w_.append(u_done)
                    w_.append(dump_marks.get("o", NOW))
                op("vector", lambda e, k2=kk, pb=pbb:
                   e.tensor_copy(out=V["ufm"][:, k2 * TS:(k2 + 1) * TS], in_=pb),
                   inc=[("DV", 1)], wait=w_)
                tp_ps_free[kk % 2] = mark("DV")
            ufm_done = mark("DV")
            E["pb_free"] = ufm_done
            col0, _ = WC_SECS["wo"]
            sec_gate("wo", s, [g_read_pl])
            wo_ps_free = {0: E["pa_free"], 1: E["pc_free"]}
            x0w_free = {0: u_done, 1: u_done}
            x1_marks = {}
            for o in range(DC):
                slot, wl_t = ws.load(ctx, col0 + o * VC * 128)
                ps = PA[:] if o % 2 == 0 else PC[:]
                xw = V["x0w0"] if o % 2 == 0 else V["x0w1"]
                op("scalar", lambda e, d=xw, oo=o:
                   e.dma_start(out=d, in_=x0_d[oo * 128:(oo + 1) * 128, t0:t0 + TS]),
                   inc=[("XW%d" % (o % 2), 16)], wait=[x0w_free[o % 2]])
                xw_xl = mark("XW%d" % (o % 2))
                for kk in range(VC):
                    for nn2 in range(2):
                        w_ = []
                        if kk == 0 and nn2 == 0:
                            w_ = [wl_t, ufm_done, wo_ps_free[o % 2]]
                        op("tensor", lambda e, p=ps, sl=slot, k2=kk, n2=nn2:
                           e.matmul(out=p[:, n2 * 512:(n2 + 1) * 512],
                                    lhsT=ws.buf(sl, k2 * 128, 128),
                                    rhs=V["ufm"][:, k2 * TS + n2 * 512:k2 * TS + n2 * 512 + 512],
                                    start=(k2 == 0), stop=(k2 == VC - 1)),
                           inc=([("PE", 1)] if (kk == VC - 1 and nn2 == 1) else []),
                           wait=w_)
                ws.slot_free[slot] = mark("PE")
                wo_pe = mark("PE")
                w_ = [wo_pe, xw_xl]
                if o == 0:
                    w_ += [u_done, dump_marks.get("q", NOW), dump_marks.get("k", NOW)]
                op("vector", lambda e, oo=o, p=ps, x=xw:
                   e.scalar_tensor_tensor(out=V["x1"][:, oo * TS:(oo + 1) * TS],
                                          in0=p, scalar=1.0, in1=x,
                                          op0=ALU.mult, op1=ALU.add),
                   inc=[("DV", 1)], wait=w_)
                wo_ps_free[o % 2] = mark("DV")
                x0w_free[o % 2] = mark("DV")
                x1_marks[o] = mark("DV")
                if s == 0:
                    drain2(3, extra=[g_read_pl])
            E["pa_free"] = wo_ps_free[0]
            E["pc_free"] = wo_ps_free[1]
            dump_marks["x1"] = dump("d_x1", V["x1"], s, ("DV", n("DV")))

            rb2 = {}
            sq2war = dict(E["sq2_war"])
            for c in range(DC):
                sq2 = V["sq2_0"] if c % 2 == 0 else V["sq2_1"]
                op("scalar", lambda e, o_=sq2, cc=c:
                   e.activation(out=o_, in_=V["x1"][:, cc * TS:(cc + 1) * TS], func=AF.Square),
                   inc=[("AC", 1)], wait=[x1_marks[c], sq2war[c % 2]])
                sq_ac = mark("AC")
                for gg in range(2):
                    w_ = [sq_ac] if gg == 0 else []
                    if c == 0 and gg == 0:
                        w_.append(E["pb_free"])
                    op("tensor", lambda e, r=sq2, cc=c, g2=gg:
                       e.matmul(out=PB[0:1, g2 * 512:g2 * 512 + 512], lhsT=V["ones_b"][:, 0:1],
                                rhs=r[:, g2 * 512:g2 * 512 + 512],
                                start=(cc == 0), stop=(cc == DC - 1)),
                       inc=[("PE", 1)], wait=w_)
                sq2war[c % 2] = mark("PE")
                if s == 0:
                    drain2(3, extra=[g_read_pl])
            E["sq2_war"] = sq2war
            sum_pe = mark("PE")
            for grp in range(2):
                op("scalar", lambda e, gg=grp:
                   e.activation(out=V["rfs"], in_=PB[0:1, gg * 512:gg * 512 + 512],
                                func=AF.Ln, bias=EPS_RMS, scale=1.0 / D),
                   inc=[("AC", 1)], wait=[sum_pe] + ([rb2[0]] if grp == 1 else []))
                op("scalar", lambda e: e.activation(out=V["rfs"], in_=V["rfs"],
                                                    func=AF.Exp, scale=-0.5),
                   inc=[("AC", 1)], wait=[mark("AC")])
                r2_ac = mark("AC")
                op("tensor", lambda e, gg=grp:
                   e.matmul(out=PB[:, 1024 + gg * 512:1024 + gg * 512 + 512],
                            lhsT=V["ones_f"][0:1, :], rhs=V["rfs"], start=True, stop=True),
                   inc=[("PE", 1)], wait=[r2_ac])
                rb2[grp] = mark("PE")
            for grp in range(2):
                for c in range(DC):
                    w_ = [rb2[grp]]
                    if c == 0 and grp == 0:
                        w_ += [v_done["sgt"], u_done, dump_marks.get("sg", NOW)]
                    op("vector", lambda e, cc=c, gg=grp:
                       e.scalar_tensor_tensor(
                           out=V["h2"][:, cc * TS + gg * 512:cc * TS + gg * 512 + 512],
                           in0=V["x1"][:, cc * TS + gg * 512:cc * TS + gg * 512 + 512],
                           scalar=V["fw"][:, cc:cc + 1],
                           in1=PB[:, 1024 + gg * 512:1024 + gg * 512 + 512],
                           op0=ALU.mult, op1=ALU.mult),
                       inc=[("DV", 1)], wait=w_)
            h2_done = mark("DV")
            E["pb_free"] = h2_done
            dump_marks["h2"] = dump("d_h2", V["h2"], s, h2_done)

            col0, _ = WC_SECS["fgu"]
            sec_gate("fgu", s, [g_read_pl])
            ffn_ps_free = {0: [E["pa_free"], E["pc_free"]], 1: [E["pb_free"]]}
            fsc_free = NOW
            cur = None
            for m in range(FC):
                for kk in range(DC):
                    cl = m * DC * 256 + kk * 256
                    if cl % SLAB == 0:
                        cur = ws.load(ctx, col0 + cl)
                    sl, wlt = cur
                    off = cl % SLAB
                    if m % 2 == 0:
                        psG, psU = PA[:], PC[:]
                    else:
                        psG, psU = PB[:, 0:1024], PB[:, 1024:2048]
                    for gu in range(2):
                        pp = psG if gu == 0 else psU
                        for nn2 in range(2):
                            w_ = []
                            if kk == 0 and gu == 0 and nn2 == 0:
                                w_ = [wlt, h2_done] + ffn_ps_free[m % 2]
                            elif off == 0 and gu == 0 and nn2 == 0:
                                w_ = [wlt]
                            retire = (((cl + 256) % SLAB == 0 or (m == FC - 1 and kk == DC - 1))
                                      and gu == 1 and nn2 == 1)
                            inc_ = [("PE", 1)] if (retire or (kk == DC - 1 and gu == 1 and nn2 == 1)) else []
                            op("tensor", lambda e, p=pp, s_=sl, of=off, g2=gu, k2=kk, n2=nn2:
                               e.matmul(out=p[:, n2 * 512:(n2 + 1) * 512],
                                        lhsT=ws.buf(s_, of + g2 * 128, 128),
                                        rhs=V["h2"][:, k2 * TS + n2 * 512:k2 * TS + n2 * 512 + 512],
                                        start=(k2 == 0), stop=(k2 == DC - 1)),
                               inc=inc_, wait=w_)
                            if retire:
                                ws.slot_free[sl] = mark("PE")
                mm_pe = mark("PE")
                if m % 2 == 0:
                    psG, psU = PA[:], PC[:]
                else:
                    psG, psU = PB[:, 0:1024], PB[:, 1024:2048]
                op("scalar", lambda e, p=psG: e.activation(out=V["klt"], in_=p, func=AF.Silu),
                   inc=[("AC", 1)], wait=[mm_pe, fsc_free])
                sil_ac = mark("AC")
                if s == 0:
                    if m == 32:
                        drain2(len(p2items), extra=[g_read_pl])
                        drain_flush()
                        E["p2_done"] = prep_quiesce()
                    elif m < 32:
                        drain2(2, extra=[g_read_pl])
                w_ = [sil_ac]
                if m == 0:
                    w_ += [u_done, ufm_done, x1_marks[DC - 1],
                           dump_marks.get("u", NOW)]
                if m == 32:
                    w_ += [g_read_pl, dump_marks.get("g", NOW)]
                    if s == 0:
                        w_ += E["p2_done"]
                op("vector", lambda e, mm=m, p=psU:
                   e.tensor_tensor(out=m_sl(mm, 0, TS), in0=V["klt"], in1=p, op=ALU.mult),
                   inc=[("DV", 1)], wait=w_)
                fsc_free = mark("DV")
                ffn_ps_free[m % 2] = [mark("DV")]
            m_done = mark("DV")
            E["pa_free"] = m_done
            E["pc_free"] = m_done
            E["pb_free"] = m_done

            col0, _ = WC_SECS["fd"]
            sec_gate("fd", s, [g_read_pl])
            fd_ps_free = {0: E["pa_free"], 1: E["pc_free"]}
            fsc2_free = fsc_free
            s_marks = {}
            cur = None
            for o in range(DC):
                ps = PA[:] if o % 2 == 0 else PC[:]
                xw = V["x0w0"] if o % 2 == 0 else V["x0w1"]
                op("scalar", lambda e, d=xw, oo=o:
                   e.dma_start(out=d, in_=x0_d[oo * 128:(oo + 1) * 128, t0:t0 + TS]),
                   inc=[("XW%d" % (o % 2), 16)], wait=[x0w_free[o % 2]])
                xw_xl = mark("XW%d" % (o % 2))
                for kk in range(FC):
                    cl = (o * FC + kk) * 128
                    if cl % SLAB == 0:
                        cur = ws.load(ctx, col0 + cl)
                    sl, wlt = cur
                    off = cl % SLAB
                    for nn2 in range(2):
                        w_ = []
                        if off == 0 and nn2 == 0:
                            w_.append(wlt)
                        if kk == 0 and nn2 == 0:
                            w_ += [m_done, fd_ps_free[o % 2]]
                        retire = (((cl + 128) % SLAB == 0 or (o == DC - 1 and kk == FC - 1))
                                  and nn2 == 1)
                        inc_ = [("PE", 1)] if (retire or (kk == FC - 1 and nn2 == 1)) else []
                        op("tensor", lambda e, p=ps, s_=sl, of=off, n2=nn2, k2=kk:
                           e.matmul(out=p[:, n2 * 512:(n2 + 1) * 512],
                                    lhsT=ws.buf(s_, of, 128),
                                    rhs=m_sl(k2, n2 * 512, 512),
                                    start=(k2 == 0), stop=(k2 == FC - 1)),
                           inc=inc_, wait=w_)
                        if retire:
                            ws.slot_free[s_ if False else sl] = mark("PE")
                fd_pe = mark("PE")
                op("vector", lambda e, oo=o, p=ps:
                   e.scalar_tensor_tensor(out=V["klt"], in0=p, scalar=1.0,
                                          in1=V["x1"][:, oo * TS:(oo + 1) * TS],
                                          op0=ALU.mult, op1=ALU.add),
                   inc=[("DV", 1)], wait=[fd_pe, fsc2_free])
                fd_ps_free[o % 2] = mark("DV")
                t_dv = mark("DV")
                op("vector", lambda e, oo=o, x=xw:
                   e.tensor_tensor(out=V["x1"][:, oo * TS:(oo + 1) * TS], in0=V["klt"],
                                   in1=x, op=ALU.add),
                   inc=[("DV", 1)],
                   wait=[xw_xl, t_dv] + ([dump_marks.get("x1", NOW)] if o == 0 else []))
                fsc2_free = mark("DV")
                x0w_free[o % 2] = mark("DV")
                s_dv = mark("DV")
                s_marks[o] = s_dv
                sq2 = V["sq2_0"] if o % 2 == 0 else V["sq2_1"]
                op("scalar", lambda e, oo=o, q_=sq2:
                   e.activation(out=q_, in_=V["x1"][:, oo * TS:(oo + 1) * TS], func=AF.Square),
                   inc=[("AC", 1)], wait=[s_dv, E["sq2_war"][o % 2]])
                sq_ac = mark("AC")
                for nn2 in range(2):
                    w_ = [s_dv] if nn2 == 0 else []
                    if o == 0 and nn2 == 0:
                        w_.append(E["pb_free"])
                    op("tensor", lambda e, oo=o, n2=nn2:
                       e.matmul(out=PB[0:1, n2 * 512:n2 * 512 + 512], lhsT=V["ones_b"][:, 0:1],
                                rhs=V["x1"][:, oo * TS + n2 * 512:oo * TS + n2 * 512 + 512],
                                start=(oo == 0), stop=(oo == DC - 1)), wait=w_)
                    op("tensor", lambda e, q_=sq2, n2=nn2, oo=o:
                       e.matmul(out=PB[0:1, 1024 + n2 * 512:1024 + n2 * 512 + 512],
                                lhsT=V["ones_b"][:, 0:1], rhs=q_[:, n2 * 512:n2 * 512 + 512],
                                start=(oo == 0), stop=(oo == DC - 1)),
                       inc=[("PE", 1)],
                       wait=[sq_ac] if nn2 == 0 else [])
                    E["sq2_war"][o % 2] = mark("PE")
            s_done = mark("DV")
            fd_all_pe = mark("PE")

            def emit_ln(sum_pe_mark, s_view, ph, out_view, fsc_war, region_war,
                        store_slices=None, store_war=NOW, row_war=NOW):
                tokens = 512
                pso = ph * 512
                op("scalar", lambda e: e.activation(out=V["mrow"][0:1, 0:tokens],
                                                    in_=PB[0:1, pso:pso + tokens],
                                                    func=AF.Copy, scale=1.0 / D),
                   inc=[("AC", 1)], wait=[sum_pe_mark, row_war])
                op("scalar", lambda e: e.activation(out=V["vrow"][0:1, 0:tokens],
                                                    in_=PB[0:1, 1024 + pso:1024 + pso + tokens],
                                                    func=AF.Copy, scale=1.0 / D),
                   inc=[("AC", 1)])
                mv_ac = mark("AC")
                op("vector", lambda e: e.tensor_tensor(out=V["trow"][0:1, 0:tokens],
                                                       in0=V["mrow"][0:1, 0:tokens],
                                                       in1=V["mrow"][0:1, 0:tokens], op=ALU.mult),
                   inc=[("DV", 1)], wait=[mv_ac])
                op("vector", lambda e: e.tensor_tensor(out=V["vrow"][0:1, 0:tokens],
                                                       in0=V["vrow"][0:1, 0:tokens],
                                                       in1=V["trow"][0:1, 0:tokens],
                                                       op=ALU.subtract),
                   inc=[("DV", 1)], wait=[mark("DV")])
                var_dv = mark("DV")
                op("scalar", lambda e: e.activation(out=V["vrow"][0:1, 0:tokens],
                                                    in_=V["vrow"][0:1, 0:tokens],
                                                    func=AF.Ln, bias=EPS_LN),
                   inc=[("AC", 1)], wait=[var_dv])
                op("scalar", lambda e: e.activation(out=V["vrow"][0:1, 0:tokens],
                                                    in_=V["vrow"][0:1, 0:tokens],
                                                    func=AF.Exp, scale=-0.5),
                   inc=[("AC", 1)], wait=[mark("AC")])
                r_ac = mark("AC")
                op("vector", lambda e:
                   e.scalar_tensor_tensor(out=V["mrow"][0:1, 0:tokens],
                                          in0=V["mrow"][0:1, 0:tokens], scalar=-1.0,
                                          in1=V["vrow"][0:1, 0:tokens],
                                          op0=ALU.mult, op1=ALU.mult),
                   inc=[("DV", 1)], wait=[r_ac])
                nm_dv = mark("DV")
                op("tensor", lambda e:
                   e.matmul(out=PB[:, 1024 + pso:1024 + pso + 512],
                            lhsT=V["ones_f"][0:1, :],
                            rhs=V["vrow"][0:1, 0:512],
                            start=True, stop=True), wait=[r_ac])
                op("tensor", lambda e:
                   e.matmul(out=PB[:, pso:pso + 512],
                            lhsT=V["ones_f"][0:1, :],
                            rhs=V["mrow"][0:1, 0:512],
                            start=True, stop=True),
                   inc=[("PE", 1)], wait=[nm_dv])
                bc_pe = mark("PE")
                emit_ln.last_bc = bc_pe
                op("vector", lambda e: e.tensor_copy(out=V["rsb2"][:, 0:tokens],
                                                     in_=PB[:, 1024 + pso:1024 + pso + tokens]),
                   inc=[("DV", 1)], wait=[bc_pe])
                op("vector", lambda e: e.tensor_copy(out=V["nm2"][:, 0:tokens],
                                                     in_=PB[:, pso:pso + tokens]),
                   inc=[("DV", 1)])
                bc_dv = mark("DV")
                outs, st_marks = {}, {}
                fw_ = fsc_war
                ob_war = dict(store_war) if isinstance(store_war, dict) else {0: store_war, 1: store_war}
                for c in range(DC):
                    w_ = [bc_dv, fw_]
                    if c == 0:
                        w_ += list(region_war) if isinstance(region_war, list) else [region_war]
                    op("gpsimd", lambda e, cc=c:
                       e.tensor_tensor(out=V["klt"][:, 0:tokens], in0=s_view(cc),
                                       in1=V["rsb2"][:, 0:tokens], op=ALU.mult),
                       inc=[("PL", 1)], wait=w_)
                    t_pl = mark("PL")
                    w_ = [t_pl]
                    if store_slices is not None:
                        w_.append(ob_war[c % 2])
                    op("vector", lambda e, cc=c:
                       e.tensor_tensor(out=out_view(cc), in0=V["klt"][:, 0:tokens],
                                       in1=V["nm2"][:, 0:tokens], op=ALU.add),
                       inc=[("DV", 1)], wait=w_)
                    fw_ = mark("DV")
                    outs[c] = mark("DV")
                    if store_slices is not None:
                        stsem = "ST%d" % (c % 2)
                        op("gpsimd", lambda e, cc=c:
                           e.dma_start(out=store_slices(cc), in_=out_view(cc)),
                           inc=[(stsem, 16)], wait=[outs[c], ob_war[c % 2]])
                        ob_war[c % 2] = mark(stsem)
                        st_marks[c] = mark(stsem)
                return outs, mark("DV"), st_marks, fw_

            fsc_after = fsc2_free
            row_war = E.get("ln_row_war", NOW)
            x3h = {}
            for ph in range(2):
                _, x3_done, _, fsc_after = emit_ln(
                    fd_all_pe,
                    lambda c, p_=ph: V["x1"][:, c * TS + p_ * 512:c * TS + p_ * 512 + 512], ph,
                    lambda c, p_=ph: V["x3"][:, c * TS + p_ * 512:c * TS + p_ * 512 + 512],
                    fsc_after, [m_done, dump_marks.get("h2", NOW)] if ph == 0 else NOW,
                    row_war=row_war)
                row_war = emit_ln.last_bc
                x3h[ph] = x3_done
            E["pb_free"] = x3_done
            dump_marks["x3"] = dump("d_x3", V["x3"], s, x3_done)

            qf_ps_free = {0: E["pa_free"], 1: E["pc_free"]}
            fsc_war = fsc_after
            f_region_war = fd_all_pe
            st_war = E.get("st_war", {0: NOW, 1: NOW})
            s2_read_done = NOW
            for half in range(2):
                th0 = half * 512
                col0, _ = WC_SECS["qgu"]
                direct = (s == 0 and half == 0)
                if direct:
                    qf_prefetch([fd_all_pe])
                    qf_prefetch([fd_all_pe])
                else:
                    sec_gate("qgu", s)
                    if s == 0:
                        ws.gate = list(ws.gate) + list(dstate["slot_store"])
                cur = None
                for m in range(QC):
                    for kk in range(DC):
                        cl = m * DC * 256 + kk * 256
                        if cl % SLAB == 0:
                            cur = qdirect(col0 + cl) if direct else ws.load(ctx, col0 + cl)
                        sl, wlt = cur
                        off = cl % SLAB
                        psG = PA[:, 0:512] if m % 2 == 0 else PC[:, 0:512]
                        psU = PA[:, 512:1024] if m % 2 == 0 else PC[:, 512:1024]
                        for gu in range(2):
                            pp = psG if gu == 0 else psU
                            w_ = []
                            if kk == 0 and gu == 0:
                                w_ = [x3h[half], qf_ps_free[m % 2]]
                                if off == 0:
                                    w_.append(wlt)
                            elif off == 0 and gu == 0 and kk % (SLAB // 256) == 0:
                                w_ = [wlt]
                            retire = (((cl + 256) % SLAB == 0 or (m == QC - 1 and kk == DC - 1))
                                      and gu == 1)
                            inc_ = [("PE", 1)] if (retire or (kk == DC - 1 and gu == 1)) else []
                            op("tensor", lambda e, p=pp, s_=sl, of=off, g2=gu, k2=kk:
                               e.matmul(out=p,
                                        lhsT=ws.buf(s_, of + g2 * 128, 128),
                                        rhs=V["x3"][:, k2 * TS + th0:k2 * TS + th0 + 512],
                                        start=(k2 == 0), stop=(k2 == DC - 1)),
                               inc=inc_, wait=w_)
                            if retire:
                                ws.slot_free[sl] = mark("PE")
                    gu_pe = mark("PE")
                    psG = PA[:, 0:512] if m % 2 == 0 else PC[:, 0:512]
                    psU = PA[:, 512:1024] if m % 2 == 0 else PC[:, 512:1024]
                    op("scalar", lambda e, p=psG, mm=m:
                       e.activation(out=V["klt"][:, 0:512], in_=p, func=AF.Silu,
                                    scale=V["sg"][:, mm:mm + 1]),
                       inc=[("AC", 1)], wait=[gu_pe, fsc_war])
                    op("scalar", lambda e, p=psU, mm=m:
                       e.activation(out=V["klt"][:, 512:1024], in_=p, func=AF.Copy,
                                    scale=V["su"][:, mm:mm + 1]),
                       inc=[("AC", 1)])
                    sil_ac = mark("AC")
                    w_ = [sil_ac]
                    if m == 0:
                        w_.append(f_region_war)
                    op("vector", lambda e, mm=m:
                       e.tensor_tensor(out=V["f"][:, mm * 512:(mm + 1) * 512],
                                       in0=V["klt"][:, 0:512], in1=V["klt"][:, 512:1024],
                                       op=ALU.mult),
                       inc=[("DV", 1)], wait=w_)
                    fsc_war = mark("DV")
                    qf_ps_free[m % 2] = mark("DV")
                f_done = mark("DV")
                col0d, _ = WC_SECS["qd"]
                if not direct:
                    sec_gate("qd", s)
                cur = None
                for o in range(DC):
                    ps = PA[:, 0:512] if o % 2 == 0 else PC[:, 0:512]
                    for kk in range(QC):
                        cl = (o * QC + kk) * 128
                        if cl % SLAB == 0:
                            cur = qdirect(col0d + cl) if direct else ws.load(ctx, col0d + cl)
                        sl, wlt = cur
                        off = cl % SLAB
                        w_ = []
                        if off == 0:
                            w_.append(wlt)
                        if kk == 0:
                            w_ += [f_done, qf_ps_free[o % 2]]
                        retire = ((cl + 128) % SLAB == 0 or (o == DC - 1 and kk == QC - 1))
                        inc_ = [("PE", 1)] if (retire or kk == QC - 1) else []
                        op("tensor", lambda e, p=ps, s_=sl, of=off, k2=kk:
                           e.matmul(out=p, lhsT=ws.buf(s_, of, 128),
                                    rhs=V["f"][:, k2 * 512:(k2 + 1) * 512],
                                    start=(k2 == 0), stop=(k2 == QC - 1)),
                           inc=inc_, wait=w_)
                        if retire:
                            ws.slot_free[sl] = mark("PE")
                    qd_pe = mark("PE")
                    w_ = [qd_pe]
                    if o == 0 and half == 0:
                        w_.append(x3_done)
                    op("vector", lambda e, oo=o, p=ps:
                       e.scalar_tensor_tensor(
                           out=V["s2"][:, oo * TS + th0:oo * TS + th0 + 512],
                           in0=p, scalar=V["sd"][:, oo:oo + 1],
                           in1=V["x3"][:, oo * TS + th0:oo * TS + th0 + 512],
                           op0=ALU.mult, op1=ALU.add),
                       inc=[("DV", 1)], wait=w_)
                    qf_ps_free[o % 2] = mark("DV")
                    s2_dv = mark("DV")
                    sq2 = V["sq2_0"] if o % 2 == 0 else V["sq2_1"]
                    op("scalar", lambda e, oo=o, q_=sq2:
                       e.activation(out=q_[:, 0:512],
                                    in_=V["s2"][:, oo * TS + th0:oo * TS + th0 + 512],
                                    func=AF.Square),
                       inc=[("AC", 1)], wait=[s2_dv, E["sq2_war"][o % 2]])
                    sq_ac = mark("AC")
                    w0 = [s2_dv] + ([E["pb_free"]] if o == 0 else [])
                    op("tensor", lambda e, oo=o:
                       e.matmul(out=PB[0:1, 0:512], lhsT=V["ones_b"][:, 0:1],
                                rhs=V["s2"][:, oo * TS + th0:oo * TS + th0 + 512],
                                start=(oo == 0), stop=(oo == DC - 1)), wait=w0)
                    op("tensor", lambda e, q_=sq2, oo=o:
                       e.matmul(out=PB[0:1, 1024:1536], lhsT=V["ones_b"][:, 0:1],
                                rhs=q_[:, 0:512], start=(oo == 0), stop=(oo == DC - 1)),
                       inc=[("PE", 1)], wait=[sq_ac])
                    E["sq2_war"][o % 2] = mark("PE")
                f_region_war = mark("PE")
                qsum_pe = mark("PE")
                outs, fin, st_marks, fsc_after2 = emit_ln(
                    qsum_pe,
                    lambda c: V["s2"][:, c * TS + th0:c * TS + th0 + 512], 0,
                    lambda c: (V["outb0"] if c % 2 == 0 else V["outb1"]),
                    fsc_war,
                    x0w_free[0] if half == 0 else NOW,
                    store_slices=lambda c: y_d[c * 128:(c + 1) * 128, t0 + th0:t0 + th0 + 512],
                    store_war=st_war, row_war=row_war)
                row_war = emit_ln.last_bc
                E["ln_row_war"] = row_war
                fsc_war = fsc_after2
                st_war = {0: mark("ST0"), 1: mark("ST1")}
                E["st_war"] = st_war
                last_st0, last_st1 = mark("ST0"), mark("ST1")
                E["pb_free"] = fin
                s2_read_done = fin
            E["h_region"] = f_region_war
            E["v_region"] = f_region_war
            E["sg_region"] = s2_read_done
            E["qk_region"] = s2_read_done
            E["g_region"] = [fd_all_pe] + (prep_quiesce() if s == 0 else [])
            E["pa_free"] = qf_ps_free[0]
            E["pc_free"] = qf_ps_free[1]
            E["xr_war"] = [last_st0, last_st1, x0w_free[1], x0w_free[1]]
            E["sqx_war"] = {0: s2_read_done, 1: s2_read_done}

            if debug and s == 0:
                op("gpsimd", lambda e: e.dma_start(out=dbg["d_S"], in_=V["S32"]),
                   inc=[("ST0", 16)],
                   wait=[("ST0", n("ST0")), ("DV", n("DV"))])
                E["sdump"] = mark("ST0")

        wt("gpsimd", ("ST0", n("ST0")), ("ST1", n("ST1")),
           ("WC0", n("WC0")), ("WC1", n("WC1")))

    with nc.Block() as block:
        for engname, starter in [("sync", block.sync), ("scalar", block.scalar),
                                 ("vector", block.vector), ("tensor", block.tensor),
                                 ("gpsimd", block.gpsimd)]:
            def run(eng, engname=engname):
                build_program(Ctx(engname, eng, semh))
            starter(run)

    for cm in sem_cms:
        cm.__exit__(None, None, None)
    arena_cm.__exit__(None, None, None)
    return nc


def _prep_weights(inputs):
    f32 = np.float32
    W = {}
    W["wv_s"] = lay_rhs_nh(np.asarray(inputs["L0_attn_wv"], f32), DC)
    W["wg_s"] = lay_rhs_nh(np.asarray(inputs["L0_attn_wg"], f32), DC)
    W["wq_s"] = lay_lhsT(np.asarray(inputs["L0_attn_wq"], f32), DC, KC)
    W["wk_s"] = lay_lhsT(np.asarray(inputs["L0_attn_wk"], f32), DC, KC)
    gk1 = np.asarray(inputs["L0_attn_gk_w1"], f32)
    W["gk1_s"] = np.ascontiguousarray(gk1.reshape(DC, 128, LR).transpose(1, 0, 2).reshape(128, DC * LR))
    W["wo_s"] = lay_lhsT(np.asarray(inputs["L0_attn_wo"], f32), VC, DC)
    W["fgu_s"] = lay_gu(np.asarray(inputs["L0_attn_w_gate"], f32),
                        np.asarray(inputs["L0_attn_w_up"], f32), DC, FC)
    W["fd_s"] = lay_lhsT(np.asarray(inputs["L0_attn_w_down"], f32), FC, DC)
    W["qgu_s"] = lay_gu(np.asarray(inputs["L0_ffn_w_gate"]),
                        np.asarray(inputs["L0_ffn_w_up"]), DC, QC)
    W["qd_s"] = lay_lhsT(np.asarray(inputs["L0_ffn_w_down"]), QC, DC)
    W["gk2_s"] = np.ascontiguousarray(np.asarray(inputs["L0_attn_gk_w2"], f32))
    W["nw_s"] = lay_vec(inputs["L0_attn_norm1_w"])
    W["fw_s"] = lay_vec(inputs["L0_attn_ffn_norm_w"])
    W["gb_s"] = lay_vec(inputs["L0_attn_gk_b"])
    W["sg_s"] = lay_vec(inputs["L0_ffn_s_gate"])
    W["su_s"] = lay_vec(inputs["L0_ffn_s_up"])
    W["sd_s"] = lay_vec(inputs["L0_ffn_s_down"])
    gnw = np.asarray(inputs["L0_attn_gnorm_w"], f32)
    W["gnw_s"] = np.ascontiguousarray(gnw.reshape(2, 128).T)
    cst = np.zeros((128, 192), dtype=ml_dtypes.bfloat16)
    cst[:, 0:128] = np.eye(128, dtype=ml_dtypes.bfloat16)
    um = np.triu(np.ones((64, 64), np.float32)).astype(ml_dtypes.bfloat16)
    cst[0:64, 128:192] = um
    cst[64:128, 128:192] = um
    W["cst_s"] = cst
    return W


def kernel(**inputs):
    x = np.asarray(inputs["L0_attn_x"], np.float32)
    B = x.shape[0]
    W = _prep_weights(inputs)
    nc = build_nc(debug=False)
    in_maps = []
    for b in range(B):
        m = {"x0": np.ascontiguousarray(x[b].T)}
        m.update(W)
        in_maps.append(m)
    res = run_bass_kernel_spmd(nc, in_maps, core_ids=list(range(B)))
    out = np.stack([np.ascontiguousarray(res.results[b]["y"].T) for b in range(B)], axis=0)
    return out


# version 12 (final)
# speedup vs baseline: 1.1460x; 1.0783x over previous
import math
import numpy as np
import ml_dtypes
import concourse.bass as bass
import concourse.mybir as mybir
from concourse.bass_utils import run_bass_kernel_spmd

F32 = mybir.dt.float32
BF16 = mybir.dt.bfloat16
I8 = mybir.dt.int8
U8 = mybir.dt.uint8
AF = mybir.ActivationFunctionType
ALU = mybir.AluOpType

D, KD, VD, H, HK, HV = 2048, 1024, 2048, 8, 128, 256
LR, FH, QH, CH = 16, 5504, 8192, 64
GN = 16.0
QSCALE_LOG = -0.5 * math.log(128.0)
T_TOTAL, TS = 4096, 1024
NS, NCH, NPAIR = T_TOTAL // TS, TS // CH, TS // CH // 2
DC, KC, VC, FC, QC = D // 128, KD // 128, VD // 128, FH // 128, QH // 128
SLAB = 2048
NRING = 2
EPS_RMS, EPS_LN = 1e-6, 1e-5
KB = 1024

SEMS = ("WL0", "WL1", "XC", "XL0", "XL1", "XL2", "XL3", "XW0", "XW1",
        "PE", "DV", "AC", "PL", "ST0", "ST1", "WC0", "WC1", "WP0", "WP1")
NOW = (None, 0)


def cdiv(a, b):
    return (a + b - 1) // b


def _wc_sections():
    raw = [("wv", 2 * DC * 1024), ("wg", 2 * DC * 1024),
           ("wq", KC * DC * 128), ("wk", KC * DC * 128),
           ("gk1", DC * LR), ("wo", DC * VC * 128),
           ("fgu", FC * DC * 256), ("fd", DC * FC * 128),
           ("qgu", QC * DC * 256), ("qd", DC * QC * 128)]
    out, col = {}, 0
    for name, cols in raw:
        out[name] = (col, cols)
        col += cdiv(cols, SLAB) * SLAB
    return out, col


WC_SECS, WC_COLS = _wc_sections()


def lay_lhsT(W, kc, oc):
    return np.ascontiguousarray(
        W.reshape(kc, 128, oc, 128).transpose(1, 2, 0, 3).reshape(128, oc * kc * 128))


def lay_gu(Wg, Wu, kc, oc):
    g = Wg.reshape(kc, 128, oc, 128).transpose(1, 2, 0, 3)
    u = Wu.reshape(kc, 128, oc, 128).transpose(1, 2, 0, 3)
    return np.ascontiguousarray(np.stack([g, u], axis=3).reshape(128, oc * kc * 256))


def lay_rhs_nh(W, kc):
    w = W.reshape(kc, 128, 2, 1024)
    return np.ascontiguousarray(w.transpose(1, 2, 0, 3).reshape(128, 2 * kc * 1024))


def lay_vec(v):
    n = v.shape[0] // 128
    return np.ascontiguousarray(np.asarray(v, np.float32).reshape(n, 128).T)


class Ctx:
    def __init__(self, engname, eng, semh):
        self.engname, self.eng, self.semh = engname, eng, semh
        self.s = {k: 0 for k in SEMS}

    def n(self, sem):
        return self.s[sem]

    def mark(self, sem):
        return (sem, self.s[sem])

    def op(self, engname, fn, inc=(), wait=()):
        if self.engname == engname:
            for s, v in wait:
                if s is not None and v > 0:
                    self.eng.wait_ge(self.semh[s], v)
            bi = fn(self.eng)
            for s, k in inc:
                bi.then_inc(self.semh[s], k)
        for s, k in inc:
            self.s[s] += k

    def wait(self, engname, *sv):
        if self.engname == engname:
            for s, v in sv:
                if s is not None and v > 0:
                    self.eng.wait_ge(self.semh[s], v)


class WStream:
    def __init__(self, wc_ap, ring_ap):
        self.wc, self.ring, self.i = wc_ap, ring_ap, 0
        self.slot_free = [(None, 0)] * NRING
        self.gate = []

    def load(self, ctx, col, ncols=SLAB, extra_wait=()):
        slot = self.i % NRING
        self.i += 1
        dst = self.ring[:, slot * SLAB:slot * SLAB + ncols]
        src_ = self.wc[:, col:col + ncols]
        sem = "WL%d" % slot
        w = list(extra_wait) + list(self.gate) + [self.slot_free[slot],
                                                  (sem, ctx.n(sem))]
        ctx.op("sync", lambda e: e.dma_start(out=dst, in_=src_), inc=[(sem, 16)], wait=w)
        return slot, ctx.mark(sem)

    def buf(self, slot, off=0, ncols=128):
        return self.ring[:, slot * SLAB + off:slot * SLAB + off + ncols]


def build_nc(debug=False):
    nc = bass.Bass("TRN2", target_bir_lowering=False, debug=False)

    x0_d = nc.dram_tensor("x0", [D, T_TOTAL], F32, kind="ExternalInput").ap()
    y_d = nc.dram_tensor("y", [D, T_TOTAL], F32, kind="ExternalOutput").ap()
    src = {}
    for name, shp, dt in [
        ("wv_s", [128, 2 * DC * 1024], F32), ("wg_s", [128, 2 * DC * 1024], F32),
        ("wq_s", [128, KC * DC * 128], F32), ("wk_s", [128, KC * DC * 128], F32),
        ("gk1_s", [128, DC * LR], F32), ("wo_s", [128, DC * VC * 128], F32),
        ("fgu_s", [128, FC * DC * 256], F32), ("fd_s", [128, DC * FC * 128], F32),
        ("qgu_s", [128, QC * DC * 256], I8), ("qd_s", [128, DC * QC * 128], I8),
        ("gk2_s", [16, KD], F32), ("nw_s", [128, 16], F32), ("fw_s", [128, 16], F32),
        ("gb_s", [128, 8], F32), ("sg_s", [128, 64], F32), ("su_s", [128, 64], F32),
        ("sd_s", [128, 16], F32), ("gnw_s", [128, 2], F32), ("cst_s", [128, 192], BF16),
    ]:
        src[name] = nc.dram_tensor(name, shp, dt, kind="ExternalInput").ap()
    wc = nc.dram_tensor("wc", [128, WC_COLS], BF16, kind="Internal").ap()

    dbg = {}
    if debug:
        for name, shp in [
            ("d_h", [128, DC * TS]), ("d_q", [128, KC * TS]), ("d_k", [128, KC * TS]),
            ("d_g", [128, KC * TS]), ("d_v", [128, NPAIR * VD]), ("d_sg", [128, NPAIR * VD]),
            ("d_o", [128, NPAIR * VD]), ("d_u", [128, NPAIR * VD]), ("d_x1", [128, DC * TS]),
            ("d_h2", [128, DC * TS]), ("d_x3", [128, DC * TS]),
        ]:
            dbg[name] = nc.dram_tensor(name, shp, BF16, kind="ExternalOutput").ap()
        dbg["d_S"] = nc.dram_tensor("d_S", [128, H * HV], F32, kind="ExternalOutput").ap()

    ARENA = 212800
    arena_cm = nc.sbuf_tensor("arena", [128, ARENA], U8)
    A = arena_cm.__enter__()

    def carve(off, p, cols, dt):
        bsz = {F32: 4, BF16: 2, I8: 1, U8: 1}[dt]
        assert off + cols * bsz <= ARENA, (off, cols, dt)
        return A[0:p, off:off + cols * bsz].bitcast(dt)

    V = {}
    V["ident"] = carve(0, 128, 128, BF16)
    V["umask"] = carve(256, 128, 64, BF16)
    V["ones_b"] = carve(384, 128, 64, BF16)
    V["ones_f"] = carve(512, 128, 128, F32)
    V["nw"] = carve(1024, 128, 16, F32)
    V["fw"] = carve(1088, 128, 16, F32)
    V["gb"] = carve(1152, 128, 8, F32)
    V["sg"] = carve(1184, 128, 64, F32)
    V["su"] = carve(1440, 128, 64, F32)
    V["sd"] = carve(1696, 128, 16, F32)
    V["gnw"] = carve(1760, 128, 2, F32)
    V["cb"] = carve(1768, 128, 4, F32)
    V["gk2"] = carve(1792, 16, KD, BF16)
    V["S32"] = carve(3840, 128, H * HV, F32)
    V["S16"] = carve(12032, 128, H * HV, BF16)
    V["b0"] = carve(16128, 128, 512, F32)
    V["b1"] = carve(18176, 128, 512, F32)
    V["d2"] = carve(20224, 128, 512, F32)
    V["ebq"] = carve(22272, 128, 512, BF16)
    V["emq"] = carve(23296, 128, 512, BF16)
    V["klq"] = carve(24320, 128, 512, BF16)
    V["qe"] = carve(25344, 128, 512, BF16)
    V["kd"] = carve(26368, 128, 512, BF16)
    V["kl"] = carve(27392, 128, 512, BF16)
    V["at"] = carve(28416, 128, 512, BF16)
    V["klt"] = carve(29440, 128, 1024, BF16)
    V["blv"] = carve(31488, 128, 8, F32)
    V["eblv"] = carve(31520, 128, 8, F32)
    V["ssq"] = carve(31552, 128, 64, F32)
    V["r_o"] = carve(31808, 128, 64, BF16)
    V["rfs"] = carve(31936, 1, 512, F32)
    V["sqscr"] = None
    V["sq2_0"] = carve(16128, 128, 1024, BF16)
    V["sq2_1"] = carve(18176, 128, 1024, BF16)
    V["mrow"] = carve(20224, 1, 1024, F32)
    V["vrow"] = carve(24320, 1, 1024, F32)
    V["ga1"] = carve(16128, 128, 1024, F32)
    V["ga2"] = carve(20224, 128, 1024, F32)
    V["ge1"] = carve(24320, 128, 1024, BF16)
    V["gl1"] = carve(26368, 128, 1024, BF16)
    V["t1sb"] = carve(33984, 16, 1024, BF16)
    V["trow"] = carve(33984, 1, 1024, F32)
    V["sqx0"] = carve(29440, 128, 512, BF16)
    V["sqx1"] = carve(30464, 128, 512, BF16)
    XS = 38080
    V["xr"] = [carve(XS + i * 2048, 128, 512, F32) for i in range(4)]
    V["sqscr"] = carve(XS, 128, VD, BF16)
    V["x0w0"] = carve(XS, 128, 1024, F32)
    V["x0w1"] = carve(XS + 4096, 128, 1024, F32)
    V["outb0"] = carve(XS, 128, 512, F32)
    V["outb1"] = carve(XS + 2048, 128, 512, F32)
    V["rsb2"] = carve(XS + 8192, 128, 512, BF16)
    V["nm2"] = carve(XS + 9216, 128, 512, BF16)
    RING_OFF = XS + 10240
    V["ring"] = carve(RING_OFF, 128, NRING * SLAB, BF16)
    M = RING_OFF + NRING * SLAB * 2
    V["h"] = carve(M, 128, DC * TS, BF16)
    V["v"] = carve(M + 32 * KB, 128, NPAIR * VD, BF16)
    V["sgt"] = carve(M + 64 * KB, 128, NPAIR * VD, BF16)
    V["q"] = carve(M + 96 * KB, 128, KC * TS, BF16)
    V["k"] = carve(M + 112 * KB, 128, KC * TS, BF16)
    V["g"] = carve(M + 128 * KB, 128, KC * TS, BF16)
    assert M + 144 * KB <= ARENA, M
    V["o"] = carve(M, 128, NPAIR * VD, BF16)
    V["u"] = carve(M + 32 * KB, 128, NPAIR * VD, BF16)
    V["ufm"] = carve(M, 128, VC * TS, BF16)
    V["x1"] = carve(M + 96 * KB, 128, DC * TS, BF16)
    V["h2"] = carve(M + 64 * KB, 128, DC * TS, BF16)
    V["mA"] = carve(M, 128, 32 * TS, BF16)
    V["mB"] = carve(M + 128 * KB, 128, 11 * TS, BF16)
    V["x3"] = carve(M + 64 * KB, 128, DC * TS, BF16)
    V["f"] = carve(M, 128, QC * 512, BF16)
    V["s2"] = carve(M + 96 * KB, 128, DC * TS, BF16)
    V["pstg"] = [carve(M + 128 * KB, 128, 8 * KB, U8), carve(M + 136 * KB, 128, 8 * KB, U8)]
    V["pout"] = [carve(M + 144 * KB, 128, SLAB, BF16), carve(M + 148 * KB, 128, SLAB, BF16)]

    def m_sl(j, lo, ncols):
        if j < 32:
            return V["mA"][:, j * TS + lo:j * TS + lo + ncols]
        return V["mB"][:, (j - 32) * TS + lo:(j - 32) * TS + lo + ncols]

    PA = nc.alloc_psum_tensor("PA", [128, 1024], F32)
    PC = nc.alloc_psum_tensor("PC", [128, 1024], F32)
    PB = nc.alloc_psum_tensor("PB", [128, 2048], F32)
    PBb = PB[:, 0:512].bitcast(BF16)
    PBb2 = PB[:, 512:1024].bitcast(BF16)
    PCb = PC[:, 512:1024].bitcast(BF16)

    nc.const_aps.register(F32, QSCALE_LOG, V["cb"][:, 3:4])
    nc.const_aps.register(F32, EPS_RMS, V["cb"][:, 0:1])
    nc.const_aps.register(F32, EPS_LN, V["cb"][:, 1:2])
    nc.const_aps.register(F32, 1.0, V["cb"][:, 2:3])
    sem_cms = [nc.semaphore(s) for s in SEMS]
    semh = {s: cm.__enter__() for s, cm in zip(SEMS, sem_cms)}

    def build_program(ctx: Ctx):
        op, wt, n, mark = ctx.op, ctx.wait, ctx.n, ctx.mark
        ws = WStream(wc, V["ring"])
        E = {}

        for dst, sname in [(V["nw"], "nw_s"), (V["fw"], "fw_s"), (V["gb"], "gb_s"),
                           (V["sg"], "sg_s"), (V["su"], "su_s"), (V["sd"], "sd_s"),
                           (V["gnw"], "gnw_s")]:
            op("scalar", lambda e, d=dst, s=sname: e.dma_start(out=d, in_=src[s]),
               inc=[("XC", 16)])
        op("scalar", lambda e: e.dma_start(out=A[0:128, 0:384].bitcast(BF16), in_=src["cst_s"]),
           inc=[("XC", 16)])
        op("scalar", lambda e: e.dma_start(out=V["ga1"][0:16, 0:1024], in_=src["gk2_s"]),
           inc=[("XC", 16)])
        nXC0 = n("XC")
        op("vector", lambda e: e.tensor_copy(out=V["gk2"], in_=V["ga1"][0:16, 0:1024]),
           wait=[("XC", nXC0)])
        op("vector", lambda e: e.memset(V["ones_b"], 1.0))
        op("vector", lambda e: e.memset(V["ones_f"], 1.0))
        op("vector", lambda e: e.memset(V["cb"][:, 0:1], EPS_RMS))
        op("vector", lambda e: e.memset(V["cb"][:, 1:2], EPS_LN))
        op("vector", lambda e: e.memset(V["cb"][:, 2:3], 1.0))
        op("vector", lambda e: e.memset(V["cb"][:, 3:4], QSCALE_LOG))
        op("vector", lambda e: e.memset(V["S32"], 0.0))
        op("vector", lambda e: e.memset(V["S16"], 0.0), inc=[("DV", 1)])
        E["s16_prev"] = mark("DV")
        E["s16_ac"] = mark("DV")
        E["init_dv"] = mark("DV")

        prep_marks = {}
        ps_ = {"si": 0, "load": {}, "cast": {}, "store": {}, "pend": []}
        p2items = []
        for sname, wname, dt in [("wv_s", "wv", F32), ("wg_s", "wg", F32),
                                 ("wq_s", "wq", F32), ("wk_s", "wk", F32),
                                 ("gk1_s", "gk1", F32), ("wo_s", "wo", F32),
                                 ("fgu_s", "fgu", F32), ("fd_s", "fd", F32)]:
            _, cols = WC_SECS[wname]
            nsl = cdiv(cols, SLAB)
            for j in range(nsl):
                p2items.append((sname, wname, dt, j, j == nsl - 1))
        p2i = [0]
        ceng_alt = [0]

        def next_ce(ceng):
            if ceng is not None:
                return ceng
            ceng_alt[0] += 1
            return "vector" if ceng_alt[0] % 2 == 0 else "scalar"

        def prep_load(extra=()):
            sname, wname, dt, j, last = p2items[p2i[0]]
            p2i[0] += 1
            si = ps_["si"]
            ps_["si"] = si + 1
            st = si % 2
            _, cols = WC_SECS[wname]
            ncols = min(SLAB, cols - j * SLAB)
            bsz = 4 if dt == F32 else 1
            stage = V["pstg"][st][:, 0:ncols * bsz].bitcast(dt)
            sem = "WP%d" % st
            op("sync" if st == 0 else "scalar",
               lambda e, sv=stage, sp=src[sname], c0=j * SLAB, nco=ncols:
               e.dma_start(out=sv, in_=sp[:, c0:c0 + nco]),
               inc=[(sem, 16)],
               wait=[ps_["cast"].get(si - 2, NOW), (sem, n(sem))] + list(extra))
            ps_["load"][si] = mark(sem)
            ps_["pend"].append((wname, dt, j, last, si, ncols, stage))

        def prep_cast_store(ceng):
            wname, dt, j, last, si, ncols, stage = ps_["pend"].pop(0)
            st = si % 2
            col0, _ = WC_SECS[wname]
            cwait = [ps_["load"][si], ps_["store"].get(si - 2, NOW)]
            if wname == "wo":
                ntile = ncols // 128
                for tti in range(ntile):
                    gt = (j * SLAB) // 128 + tti
                    kk2 = (gt % VC) % 2
                    op("vector", lambda e, o=V["pout"][st][:, tti * 128:(tti + 1) * 128],
                       s_=stage.bitcast(F32)[:, tti * 128:(tti + 1) * 128], k2=kk2:
                       e.tensor_scalar(out=o, in0=s_, scalar1=V["gnw"][:, k2:k2 + 1],
                                       scalar2=None, op0=ALU.mult),
                       inc=[("DV", 1)] if tti == ntile - 1 else [],
                       wait=cwait if tti == 0 else [])
                ps_["cast"][si] = mark("DV")
            elif dt != F32 or ceng == "vector":
                op("vector", lambda e, o=V["pout"][st][:, 0:ncols], s_=stage:
                   e.tensor_copy(out=o, in_=s_), inc=[("DV", 1)], wait=cwait)
                ps_["cast"][si] = mark("DV")
            else:
                op("scalar", lambda e, o=V["pout"][st][:, 0:ncols], s_=stage:
                   e.activation(out=o, in_=s_, func=AF.Copy),
                   inc=[("AC", 1)], wait=cwait)
                ps_["cast"][si] = mark("AC")
            wcsem = "WC%d" % st
            op("gpsimd", lambda e, o=V["pout"][st][:, 0:ncols], c0=col0 + j * SLAB, nco=ncols:
               e.dma_start(out=wc[:, c0:c0 + nco], in_=o),
               inc=[(wcsem, 16)], wait=[ps_["cast"][si]])
            ps_["store"][si] = mark(wcsem)
            if last:
                prep_marks[wname] = [mark("WC0"), mark("WC1")]

        def drain2(k, extra=(), ceng=None):
            while k > 0 and p2i[0] < len(p2items):
                if len(ps_["pend"]) >= 2:
                    prep_cast_store(next_ce(ceng))
                prep_load(list(extra))
                k -= 1

        def drain_flush(ceng=None):
            while ps_["pend"]:
                prep_cast_store(next_ce(ceng))

        def prep_quiesce():
            ms = []
            for d in (ps_["load"], ps_["cast"], ps_["store"]):
                for si in (ps_["si"] - 1, ps_["si"] - 2):
                    if si in d:
                        ms.append(d[si])
            return ms

        qf_items = []
        for sname, wname in (("qgu_s", "qgu"), ("qd_s", "qd")):
            col0, cols = WC_SECS[wname]
            nsl = cdiv(cols, SLAB)
            for j in range(nsl):
                qf_items.append((sname, wname, j * SLAB, min(SLAB, cols - j * SLAB),
                                 col0 + j * SLAB, j == nsl - 1))
        qf_di = [0]
        dstate = {"pend": [], "slot_store": [NOW] * NRING}

        def qf_prefetch(extra=()):
            if qf_di[0] >= len(qf_items):
                return
            sname, wname, c0s, ncols, wcc, last = qf_items[qf_di[0]]
            qf_di[0] += 1
            si = ps_["si"]
            ps_["si"] = si + 1
            st = si % 2
            stage = V["pstg"][st][:, 0:ncols].bitcast(I8)
            sem = "WP%d" % st
            op("sync" if st == 0 else "scalar",
               lambda e, sv=stage, sp=src[sname], c0=c0s, nco=ncols:
               e.dma_start(out=sv, in_=sp[:, c0:c0 + nco]),
               inc=[(sem, 16)],
               wait=[ps_["cast"].get(si - 2, NOW), (sem, n(sem))] + list(extra))
            ps_["load"][si] = mark(sem)
            dstate["pend"].append((wname, last, si, ncols, wcc, stage))

        def qdirect(col):
            wname, last, si, ncols, wcc, stage = dstate["pend"].pop(0)
            assert wcc == col, (wcc, col)
            st = si % 2
            slot = ws.i % NRING
            ws.i += 1
            op("vector", lambda e, o=ws.buf(slot, 0, ncols), s_=stage:
               e.tensor_copy(out=o, in_=s_), inc=[("DV", 1)],
               wait=[ps_["load"][si], ws.slot_free[slot], dstate["slot_store"][slot]])
            ps_["cast"][si] = mark("DV")
            wcsem = "WC%d" % st
            op("gpsimd", lambda e, o=ws.buf(slot, 0, ncols), c0=wcc, nco=ncols:
               e.dma_start(out=wc[:, c0:c0 + nco], in_=o),
               inc=[(wcsem, 16)], wait=[ps_["cast"][si]])
            ps_["store"][si] = mark(wcsem)
            dstate["slot_store"][slot] = ps_["store"][si]
            if last:
                prep_marks[wname] = [mark("WC0"), mark("WC1")]
            qf_prefetch()
            return slot, ps_["cast"][si]

        dump_marks = {}

        def dump(name, view, s, *waits):
            if not debug or s != 0:
                return NOW
            op("gpsimd", lambda e: e.dma_start(out=dbg[name], in_=view),
               inc=[("ST0", 16)],
               wait=[("ST0", n("ST0"))] + list(waits))
            return mark("ST0")

        xri = [0]

        def xload(c, t_lo, war):
            slot = xri[0] % 4
            xri[0] += 1
            buf = V["xr"][slot]
            sem = "XL%d" % slot
            op("scalar" if slot % 2 else "sync", lambda e: e.dma_start(
                out=buf, in_=x0_d[c * 128:c * 128 + 128, t_lo:t_lo + 512]),
                inc=[(sem, 16)], wait=[war])
            return buf, mark(sem), slot

        E["h_region"] = NOW
        E["v_region"] = NOW
        E["sg_region"] = NOW
        E["qk_region"] = NOW
        E["g_region"] = [NOW]
        E["pa_free"] = NOW
        E["pc_free"] = NOW
        E["pb_free"] = NOW
        E["xr_war"] = [NOW] * 4
        E["sqx_war"] = {0: NOW, 1: NOW}
        E["sq2_war"] = {0: NOW, 1: NOW}
        def sec_gate(name, s, extra=()):
            if s != 0:
                ws.gate = []
                return
            while name not in prep_marks:
                if len(ps_["pend"]) >= 2 or p2i[0] >= len(p2items):
                    prep_cast_store(next_ce(None))
                else:
                    prep_load(list(extra))
            ws.gate = prep_marks[name]

        for s in range(NS):
            t0 = s * TS
            if s:
                xri[0] += (2 - xri[0] % 4) % 4

            sumA = {}
            for grp in range(2):
                tok0 = t0 + grp * 512
                for c in range(DC):
                    buf, xl, slot = xload(c, tok0, E["xr_war"][xri[0] % 4])
                    sqx = V["sqx0"] if c % 2 == 0 else V["sqx1"]
                    w_ = [xl, E["sqx_war"][c % 2]]
                    if s == 0 and grp == 0 and c == 0:
                        w_.append(E["init_dv"])
                    op("scalar", lambda e, o=sqx, i=buf: e.activation(out=o, in_=i, func=AF.Square),
                       inc=[("AC", 1)], wait=w_)
                    sq_ac = mark("AC")
                    E["xr_war"][slot] = sq_ac
                    w_ = [sq_ac]
                    if c == 0:
                        w_.append(E["pc_free"])
                        if s == 0 and grp == 0:
                            w_.append(E["init_dv"])
                    op("tensor", lambda e, r=sqx, cc=c, gg=grp:
                       e.matmul(out=PC[0:1, gg * 512:gg * 512 + 512], lhsT=V["ones_b"][:, 0:1],
                                rhs=r, start=(cc == 0), stop=(cc == DC - 1)),
                       inc=[("PE", 1)], wait=w_)
                    E["sqx_war"][c % 2] = mark("PE")
                    if s == 0:
                        drain2(1, ceng="vector")
                sumA[grp] = mark("PE")
            rb_m = {}
            for grp in range(2):
                op("scalar", lambda e, gg=grp:
                   e.activation(out=V["mrow"][0:1, gg * 512:gg * 512 + 512],
                                in_=PC[0:1, gg * 512:gg * 512 + 512],
                                func=AF.Ln, bias=EPS_RMS, scale=1.0 / D),
                   inc=[("AC", 1)], wait=[sumA[grp], E.get("mrow_war", NOW)])
                op("scalar", lambda e, gg=grp:
                   e.activation(out=V["mrow"][0:1, gg * 512:gg * 512 + 512],
                                in_=V["mrow"][0:1, gg * 512:gg * 512 + 512],
                                func=AF.Exp, scale=-0.5),
                   inc=[("AC", 1)], wait=[mark("AC")])
                rfs_m = mark("AC")
                op("tensor", lambda e, gg=grp:
                   e.matmul(out=PA[:, gg * 512:gg * 512 + 512], lhsT=V["ones_f"][0:1, :],
                            rhs=V["mrow"][0:1, gg * 512:gg * 512 + 512],
                            start=True, stop=True),
                   inc=[("PE", 1)], wait=[rfs_m] + ([E["pa_free"]] if grp == 0 else []))
                rb_m[grp] = mark("PE")
                op("scalar", lambda e, gg=grp, d=(V["rsb2"] if grp == 0 else V["nm2"]):
                   e.activation(out=d, in_=PA[:, gg * 512:gg * 512 + 512], func=AF.Copy),
                   inc=[("AC", 1)], wait=[rb_m[grp], E.get("rnb_war", NOW)])
                rb_m[grp] = mark("AC")
            E["pc_free"] = mark("AC")
            E["pa_free"] = mark("AC")
            E["mrow_war"] = rb_m[1]
            h_ch = {}
            tmp_war = {0: E.get("b0_war", NOW), 1: E.get("b0_war", NOW)}
            for c in range(DC):
                for grp in range(2):
                    rbuf = V["rsb2"] if grp == 0 else V["nm2"]
                    buf, xl, slot = xload(c, t0 + grp * 512, E["xr_war"][xri[0] % 4])
                    tmp = V["b0"] if grp == 0 else V["b1"]
                    w_ = [xl, tmp_war[grp]]
                    if c == 0 and grp == 0:
                        w_.append(E["h_region"])
                    op("gpsimd", lambda e, i=buf, cc=c, t_=tmp:
                       e.tensor_tensor(out=t_, in0=i,
                                       in1=V["nw"][:, cc:cc + 1].broadcast_to((128, 512)),
                                       op=ALU.mult),
                       inc=[("PL", 1)], wait=w_)
                    E["xr_war"][slot] = mark("PL")
                    t_pl = mark("PL")
                    op("gpsimd", lambda e, cc=c, gg=grp, rb=rbuf, t_=tmp:
                       e.tensor_tensor(
                           out=V["h"][:, cc * TS + gg * 512:cc * TS + gg * 512 + 512],
                           in0=t_, in1=rb, op=ALU.mult),
                       inc=[("PL", 1)], wait=[t_pl, rb_m[grp]])
                    tmp_war[grp] = mark("PL")
                    if s == 0 and grp == 0:
                        drain2(1, ceng="vector")
                h_ch[c] = mark("PL")
            hdone_grp = [mark("PL"), mark("PL")]
            E["rnb_war"] = mark("PL")
            E["b0_war"] = mark("PL")
            h_done = mark("PL")

            v_done = {}
            prev_evac = {"wv": E["v_region"], "wg": E["sg_region"]}
            vg_psum_war = NOW
            for mat, outv in (("wv", "v"), ("wg", "sgt")):
                col0, _ = WC_SECS[mat]
                eng, sem = ("scalar", "AC")
                pe_war = vg_psum_war
                sec_gate(mat, s)
                for nh in range(2):
                    for tcg in range(2):
                        for kk in range(DC):
                            if kk % 2 == 0:
                                cur_v = ws.load(ctx, col0 + (nh * DC + kk) * 1024)
                                if s == 0 and kk % 4 == 0:
                                    drain2(1)
                            slot, wl_t = cur_v
                            ro = (kk % 2) * 1024
                            for j in range(4):
                                tc = tcg * 4 + j
                                psl = [PA[:], PC[:], PB[:, 0:1024], PB[:, 1024:2048]][j]
                                lhs = V["h"][:, kk * TS + tc * 128:kk * TS + (tc + 1) * 128]
                                for nn2 in range(2):
                                    w_ = []
                                    if kk == 0 and nn2 == 0 and j == 0:
                                        w_ = [wl_t, pe_war, h_ch[kk + 1]]
                                        if mat == "wv" and nh == 0 and tcg == 0:
                                            w_ += [E["pa_free"], E["pc_free"], E["pb_free"]]
                                        if nh == 0 and tcg == 0:
                                            w_.append(prev_evac[mat])
                                    elif kk % 2 == 0 and nn2 == 0 and j == 0:
                                        w_ = [wl_t, h_ch[min(kk + 1, DC - 1)]]
                                    inc_ = []
                                    if kk % 2 == 1 and nn2 == 1 and j == 3:
                                        inc_.append(("PE", 1))
                                    op("tensor", lambda e, p=psl, l=lhs, sl=slot, r2=ro, n2=nn2, k2=kk:
                                       e.matmul(out=p[:, n2 * 512:(n2 + 1) * 512], lhsT=l,
                                                rhs=ws.buf(sl, r2 + n2 * 512, 512),
                                                start=(k2 == 0), stop=(k2 == DC - 1)),
                                       inc=inc_, wait=w_)
                                    if kk % 2 == 1 and nn2 == 1 and j == 3:
                                        ws.slot_free[slot] = mark("PE")
                        grp_pe = mark("PE")
                        for j in range(4):
                            tc = tcg * 4 + j
                            psl = [PA[:], PC[:], PB[:, 0:1024], PB[:, 1024:2048]][j]
                            dst = V[outv][:, tc * VD + nh * 1024:tc * VD + nh * 1024 + 1024]
                            fn_ = AF.Copy if mat == "wv" else AF.Silu
                            op(eng, lambda e, d=dst, p=psl, f_=fn_:
                               e.activation(out=d, in_=p, func=f_),
                               inc=[(sem, 1)], wait=[grp_pe] + ([prev_evac[mat]] if (nh == 0 and tcg == 0 and j == 0) else []))
                        pe_war = mark(sem)
                vg_psum_war = pe_war
                v_done[outv] = mark(sem)
            E["pa_free"] = v_done["sgt"]
            E["pc_free"] = v_done["sgt"]
            E["pb_free"] = v_done["sgt"]

            qk_ps_free = {0: E["pa_free"], 1: E["pc_free"]}
            for mat, outv in (("wq", "q"), ("wk", "k")):
                col0, _ = WC_SECS[mat]
                sec_gate(mat, s)
                region_free = E["qk_region"] if mat == "wq" else NOW
                for o in range(KC):
                    slot, wl_t = ws.load(ctx, col0 + o * DC * 128)
                    if s == 0:
                        drain2(1)
                    ps = PA[:] if o % 2 == 0 else PC[:]
                    for kk in range(DC):
                        for nn2 in range(2):
                            w_ = []
                            if kk == 0 and nn2 == 0:
                                w_ = [wl_t, qk_ps_free[o % 2], h_done]
                            op("tensor", lambda e, p=ps, sl=slot, k2=kk, n2=nn2:
                               e.matmul(out=p[:, n2 * 512:(n2 + 1) * 512],
                                        lhsT=ws.buf(sl, k2 * 128, 128),
                                        rhs=V["h"][:, k2 * TS + n2 * 512:k2 * TS + n2 * 512 + 512],
                                        start=(k2 == 0), stop=(k2 == DC - 1)),
                               inc=([("PE", 1)] if (kk == DC - 1 and nn2 == 1) else []),
                               wait=w_)
                    ws.slot_free[slot] = mark("PE")
                    pe_m = mark("PE")
                    op("scalar", lambda e, oo=o, p=ps:
                       e.activation(out=V[outv][:, oo * TS:(oo + 1) * TS], in_=p, func=AF.Copy),
                       inc=[("AC", 1)], wait=[pe_m, region_free])
                    region_free = NOW
                    qk_ps_free[o % 2] = mark("AC")
            qk_done = mark("AC")

            col0, _ = WC_SECS["gk1"]
            sec_gate("gk1", s)
            slot, wl_t = ws.load(ctx, col0, DC * LR)
            for kk in range(DC):
                for nn2 in range(2):
                    w_ = [wl_t, E["pb_free"], h_done] if (kk == 0 and nn2 == 0) else []
                    op("tensor", lambda e, sl=slot, k2=kk, n2=nn2:
                       e.matmul(out=PB[0:16, n2 * 512:(n2 + 1) * 512],
                                lhsT=ws.buf(sl, k2 * LR, LR),
                                rhs=V["h"][:, k2 * TS + n2 * 512:k2 * TS + n2 * 512 + 512],
                                start=(k2 == 0), stop=(k2 == DC - 1)),
                       inc=([("PE", 1)] if (kk == DC - 1 and nn2 == 1) else []),
                       wait=w_)
            ws.slot_free[slot] = mark("PE")
            t1_pe = mark("PE")
            op("vector", lambda e: e.tensor_copy(out=V["t1sb"], in_=PB[0:16, 0:1024]),
               inc=[("DV", 1)], wait=[t1_pe])
            t1_dv = mark("DV")
            E["pb_free"] = t1_dv
            if s == 0:
                drain2(2)
                drain_flush()
                E["g_quiesce"] = prep_quiesce()
            g_ps_free = dict(qk_ps_free)
            prev_e1, prev_l1, prev_gs = NOW, NOW, NOW
            for o in range(KC):
                ps = PA[:] if o % 2 == 0 else PC[:]
                for nn2 in range(2):
                    w_ = [t1_dv, g_ps_free[o % 2]] if nn2 == 0 else []
                    op("tensor", lambda e, oo=o, p=ps, n2=nn2:
                       e.matmul(out=p[:, n2 * 512:(n2 + 1) * 512],
                                lhsT=V["gk2"][:, oo * 128:(oo + 1) * 128],
                                rhs=V["t1sb"][:, n2 * 512:n2 * 512 + 512],
                                start=True, stop=True),
                       inc=[("PE", 1)] if nn2 == 1 else [], wait=w_)
                t2_pe = mark("PE")
                op("vector", lambda e, oo=o, p=ps:
                   e.tensor_scalar_add(out=V["ga2"], in0=p, scalar1=V["gb"][:, oo:oo + 1]),
                   inc=[("DV", 1)], wait=[t2_pe, prev_gs, prev_e1])
                g_ps_free[o % 2] = mark("DV")
                t3_dv = mark("DV")
                op("scalar", lambda e: e.activation(out=V["ga1"], in_=V["ga2"], func=AF.Abs),
                   inc=[("AC", 1)], wait=[t3_dv, prev_e1])
                a1_ac = mark("AC")
                op("scalar", lambda e: e.activation(out=V["ge1"], in_=V["ga1"],
                                                    func=AF.Exp, scale=-1.0),
                   inc=[("AC", 1)], wait=[a1_ac, prev_l1])
                e1_ac = mark("AC")
                prev_e1 = e1_ac
                op("scalar", lambda e: e.activation(out=V["gl1"], in_=V["ge1"],
                                                    func=AF.Ln, bias=1.0),
                   inc=[("AC", 1)], wait=[e1_ac, prev_gs])
                l1_ac = mark("AC")
                prev_l1 = l1_ac
                w_ = [l1_ac]
                if o == 0:
                    w_ += E["g_region"]
                    if s == 0:
                        w_ += E["g_quiesce"]
                op("vector", lambda e, oo=o:
                   e.scalar_tensor_tensor(out=V["g"][:, oo * TS:(oo + 1) * TS],
                                          in0=V["ga2"], scalar=0.0, in1=V["gl1"],
                                          op0=ALU.min, op1=ALU.subtract),
                   inc=[("DV", 1)], wait=w_)
                prev_gs = mark("DV")
            g_done = mark("DV")
            E["pa_free"] = g_ps_free[0]
            E["pc_free"] = g_ps_free[1]
            dump_marks["h"] = dump("d_h", V["h"], s, h_done)
            dump_marks["q"] = dump("d_q", V["q"], s, qk_done)
            dump_marks["k"] = dump("d_k", V["k"], s, qk_done)
            dump_marks["g"] = dump("d_g", V["g"], s, g_done)
            dump_marks["v"] = dump("d_v", V["v"], s, v_done["v"])
            dump_marks["sg"] = dump("d_sg", V["sgt"], s, v_done["sgt"])

            ch = {"b0_free": NOW, "b1_free": NOW, "d2_free": NOW, "ebq_free": NOW,
                  "emq_free": NOW, "klq_free": NOW, "eblv_free": NOW,
                  "qe_free": NOW, "kd_free": NOW, "kl_free": NOW,
                  "at_ps": E["pc_free"], "at_free": NOW, "o_ps": E["pb_free"],
                  "klt_ps": E["pc_free"], "klt_free": NOW,
                  "su_ps": [E["pa_free"], E["pa_free"]], "s16": E["s16_ac"],
                  "sq_free": NOW}
            o_evacs = []
            g_read_pl = NOW
            for c in range(NCH):
                pair, base = c // 2, 64 * (c % 2)
                co = c * CH
                g3 = V["g"].rearrange("p (hh t) -> p hh t", hh=KC)[:, :, co:co + CH]
                q3 = V["q"].rearrange("p (hh t) -> p hh t", hh=KC)[:, :, co:co + CH]
                k3 = V["k"].rearrange("p (hh t) -> p hh t", hh=KC)[:, :, co:co + CH]
                op("gpsimd", lambda e, g_=g3: e.tensor_copy(out=V["b0"], in_=g_),
                   inc=[("PL", 1)], wait=[g_done if c == 0 else NOW, ch["b0_free"]])
                b0_pl = mark("PL")
                g_read_pl = b0_pl
                for hh in range(H):
                    op("vector", lambda e, h2=hh:
                       e.tensor_tensor_scan(out=V["b1"][:, h2 * CH:(h2 + 1) * CH],
                                            data0=V["b0"][:, h2 * CH:(h2 + 1) * CH],
                                            data1=V["ones_f"][:, 0:CH], initial=0.0,
                                            op0=ALU.add, op1=ALU.mult),
                       inc=[("DV", 1)] if hh == H - 1 else [],
                       wait=[b0_pl, ch["b1_free"]] if hh == 0 else [])
                ch["b0_free"] = mark("DV")
                scan_dv = mark("DV")
                op("vector", lambda e:
                   e.tensor_copy(out=V["blv"],
                                 in_=V["b1"].rearrange("p (hh t) -> p hh t", hh=H)[:, :, CH - 1:CH].squeeze(2)),
                   inc=[("DV", 1)], wait=[scan_dv])
                blv_dv = mark("DV")
                op("scalar", lambda e: e.activation(out=V["ebq"], in_=V["b1"], func=AF.Exp,
                                                    scale=1.0 / GN, bias=QSCALE_LOG),
                   inc=[("AC", 1)], wait=[scan_dv, ch["ebq_free"]])
                op("scalar", lambda e: e.activation(out=V["emq"], in_=V["b1"], func=AF.Exp,
                                                    scale=-1.0 / GN),
                   inc=[("AC", 1)], wait=[ch["emq_free"]])
                em_ac = mark("AC")
                op("vector", lambda e:
                   e.tensor_tensor(out=V["d2"], in0=V["b1"],
                                   in1=V["blv"].unsqueeze(2).broadcast_to((128, H, CH)),
                                   op=ALU.subtract),
                   inc=[("DV", 1)], wait=[blv_dv, ch["d2_free"]])
                ch["b1_free"] = mark("DV")
                d2_dv = mark("DV")
                op("scalar", lambda e: e.activation(out=V["klq"], in_=V["d2"], func=AF.Exp,
                                                    scale=-1.0 / GN),
                   inc=[("AC", 1)], wait=[d2_dv, ch["klq_free"]])
                ch["d2_free"] = mark("AC")
                op("scalar", lambda e: e.activation(out=V["eblv"], in_=V["blv"], func=AF.Exp,
                                                    scale=1.0 / GN),
                   inc=[("AC", 1)], wait=[ch["eblv_free"]])
                exps_ac = mark("AC")
                op("gpsimd", lambda e, q_=q3:
                   e.tensor_tensor(out=V["qe"], in0=q_,
                                   in1=V["ebq"].rearrange("p (hh t) -> p hh t", hh=H), op=ALU.mult),
                   inc=[("PL", 1)], wait=[em_ac, ch["qe_free"], qk_done if c == 0 else NOW])
                op("gpsimd", lambda e, k_=k3:
                   e.tensor_tensor(out=V["kd"], in0=k_,
                                   in1=V["emq"].rearrange("p (hh t) -> p hh t", hh=H), op=ALU.mult),
                   inc=[("PL", 1)], wait=[ch["kd_free"]])
                ch["ebq_free"] = mark("PL")
                ch["emq_free"] = mark("PL")
                kd_pl = mark("PL")
                op("gpsimd", lambda e, k_=k3:
                   e.tensor_tensor(out=V["kl"], in0=k_,
                                   in1=V["klq"].rearrange("p (hh t) -> p hh t", hh=H), op=ALU.mult),
                   inc=[("PL", 1)], wait=[exps_ac, ch["kl_free"]])
                ch["klq_free"] = mark("PL")
                kl_pl = mark("PL")
                for hh in range(H):
                    w_ = [kd_pl, ch["at_ps"]] if hh == 0 else []
                    op("tensor", lambda e, h2=hh:
                       e.matmul(out=PC[base:base + 64, h2 * 64:(h2 + 1) * 64],
                                lhsT=V["kd"][:, h2 * 64:(h2 + 1) * 64],
                                rhs=V["qe"][:, h2 * 64:(h2 + 1) * 64],
                                start=True, stop=True),
                       inc=[("PE", 1)] if hh == H - 1 else [], wait=w_)
                ch["kd_free"] = mark("PE")
                at_pe = mark("PE")
                op("vector", lambda e:
                   e.tensor_tensor(out=V["at"][base:base + 64, :],
                                   in0=PC[base:base + 64, 0:512],
                                   in1=V["umask"][base:base + 64, :]
                                   .unsqueeze(1).broadcast_to((64, H, CH)),
                                   op=ALU.mult),
                   inc=[("DV", 1)], wait=[at_pe, ch["at_free"]])
                ch["at_ps"] = mark("DV")
                at_dv = mark("DV")
                for hh in range(H):
                    w_ = [ch["s16"], ch["o_ps"]] if hh == 0 else []
                    op("tensor", lambda e, h2=hh:
                       e.matmul(out=PB[base:base + 64, h2 * 256:(h2 + 1) * 256],
                                lhsT=V["qe"][:, h2 * 64:(h2 + 1) * 64],
                                rhs=V["S16"][:, h2 * 256:(h2 + 1) * 256],
                                start=(h2 % 2 == 0), stop=False), wait=w_)
                for hh in range(H):
                    w_ = [kl_pl, ch["klt_ps"]] if hh == 0 else []
                    op("tensor", lambda e, h2=hh:
                       e.transpose(out=PCb[base:base + 64, h2 * 128:(h2 + 1) * 128],
                                   in_=V["kl"][:, h2 * 64:(h2 + 1) * 64],
                                   identity=V["ident"]),
                       inc=[("PE", 1)] if hh == H - 1 else [], wait=w_)
                ch["kl_free"] = mark("PE")
                kt_pe = mark("PE")
                op("vector", lambda e:
                   e.tensor_copy(out=V["klt"][base:base + 64, :], in_=PCb[base:base + 64, :]),
                   inc=[("DV", 1)], wait=[kt_pe, ch["klt_free"]])
                ch["klt_ps"] = mark("DV")
                kt_dv = mark("DV")
                for hh in range(H):
                    w_ = [at_dv, v_done["v"] if c == 0 else NOW] if hh == 0 else []
                    op("tensor", lambda e, h2=hh, p=pair:
                       e.matmul(out=PB[base:base + 64, h2 * 256:(h2 + 1) * 256],
                                lhsT=V["at"][base:base + 64, h2 * 64:(h2 + 1) * 64],
                                rhs=V["v"][base:base + 64, p * VD + h2 * 256:p * VD + (h2 + 1) * 256],
                                start=False, stop=(h2 % 2 == 1)),
                       inc=[("PE", 1)] if hh == H - 1 else [], wait=w_)
                ch["at_free"] = mark("PE")
                o_pe = mark("PE")
                op("scalar", lambda e, p=pair:
                   e.activation(out=V["o"][base:base + 64, p * VD:(p + 1) * VD],
                                in_=PB[base:base + 64, 0:2048], func=AF.Copy),
                   inc=[("AC", 1)],
                   wait=[o_pe] + ([dump_marks.get("h", NOW)] if c == 0 else []))
                ch["o_ps"] = mark("AC")
                o_evacs.append(mark("AC"))
                su_dv = [NOW, NOW]
                for hh in range(H):
                    w_ = [kt_dv] if hh == 0 else []
                    w_.append(ch["su_ps"][hh % 2] if hh < 2 else su_dv[hh % 2])
                    op("tensor", lambda e, h2=hh, p=pair:
                       e.matmul(out=PA[:, (h2 % 2) * 512:(h2 % 2) * 512 + 256],
                                lhsT=V["klt"][base:base + 64, h2 * 128:(h2 + 1) * 128],
                                rhs=V["v"][base:base + 64, p * VD + h2 * 256:p * VD + (h2 + 1) * 256],
                                start=True, stop=True),
                       inc=[("PE", 1)], wait=w_)
                    su_pe = mark("PE")
                    op("vector", lambda e, h2=hh:
                       e.scalar_tensor_tensor(out=V["S32"][:, h2 * 256:(h2 + 1) * 256],
                                              in0=V["S32"][:, h2 * 256:(h2 + 1) * 256],
                                              scalar=V["eblv"][:, h2:h2 + 1],
                                              in1=PA[:, (h2 % 2) * 512:(h2 % 2) * 512 + 256],
                                              op0=ALU.mult, op1=ALU.add),
                       inc=[("DV", 1)],
                       wait=[su_pe, exps_ac] + ([E["s16_prev"], E.pop("sdump", NOW)] if hh == 0 else []))
                    su_dv[hh % 2] = mark("DV")
                ch["klt_free"] = mark("PE")
                ch["su_ps"] = [su_dv[0], su_dv[1]]
                ch["eblv_free"] = mark("DV")
                su_all = mark("DV")
                op("scalar", lambda e: e.activation(out=V["S16"], in_=V["S32"], func=AF.Copy),
                   inc=[("AC", 1)], wait=[su_all])
                ch["s16"] = mark("AC")
                E["s16_prev"] = ch["s16"]
                ch["qe_free"] = o_pe
                if c % 2 == 1:
                    p = pair
                    su_pe_pair = mark("PE")
                    op("scalar", lambda e, pp=p:
                       e.activation(out=V["sqscr"], in_=V["o"][:, pp * VD:(pp + 1) * VD],
                                    func=AF.Square),
                       inc=[("AC", 1)],
                       wait=[o_evacs[-1], ch["sq_free"]] +
                            (list(E["xr_war"]) if p == 0 else []))
                    sq_ac = mark("AC")
                    op("vector", lambda e, pp=p:
                       e.tensor_reduce(out=V["ssq"][:, pp * 8:(pp + 1) * 8],
                                       in_=V["sqscr"].rearrange("pt (hh vv) -> pt hh vv", hh=H),
                                       axis=mybir.AxisListType.X, op=ALU.add),
                       inc=[("DV", 1)], wait=[sq_ac])
                    red_dv = mark("DV")
                    op("scalar", lambda e, pp=p:
                       e.activation(out=V["ssq"][:, pp * 8:(pp + 1) * 8],
                                    in_=V["ssq"][:, pp * 8:(pp + 1) * 8],
                                    func=AF.Ln, bias=EPS_RMS, scale=1.0 / HV),
                       inc=[("AC", 1)], wait=[red_dv])
                    op("scalar", lambda e, pp=p:
                       e.activation(out=V["r_o"][:, pp * 8:(pp + 1) * 8],
                                    in_=V["ssq"][:, pp * 8:(pp + 1) * 8],
                                    func=AF.Exp, scale=-0.5),
                       inc=[("AC", 1)], wait=[mark("AC")])
                    r_ac = mark("AC")
                    op("vector", lambda e, pp=p:
                       e.tensor_tensor(out=V["sqscr"],
                                       in0=V["o"][:, pp * VD:(pp + 1) * VD]
                                       .rearrange("pt (hh vv) -> pt hh vv", hh=H),
                                       in1=V["r_o"][:, pp * 8:(pp + 1) * 8]
                                       .unsqueeze(2).broadcast_to((128, H, HV)),
                                       op=ALU.mult),
                       inc=[("DV", 1)], wait=[r_ac])
                    t2_dv = mark("DV")
                    w_ = [t2_dv, su_pe_pair]
                    if p == 0:
                        w_ += [v_done["sgt"], dump_marks.get("v", NOW)]
                    op("gpsimd", lambda e, pp=p:
                       e.tensor_tensor(out=V["u"][:, pp * VD:(pp + 1) * VD], in0=V["sqscr"],
                                       in1=V["sgt"][:, pp * VD:(pp + 1) * VD], op=ALU.mult),
                       inc=[("PL", 1)], wait=w_)
                    ch["sq_free"] = mark("PL")
            u_done = mark("PL")
            gla_pe = mark("PE")
            dump_marks["o"] = dump("d_o", V["o"], s, ("AC", n("AC")))
            dump_marks["u"] = dump("d_u", V["u"], s, u_done)
            E["s16_ac"] = ch["s16"]
            E["pa_free"] = ch["su_ps"][1]
            E["pc_free"] = ch["klt_ps"]
            E["pb_free"] = ch["o_ps"]

            tp_ps_free = {0: E["pb_free"], 1: E["pb_free"]}
            for kk in range(VC):
                pbb = PBb if kk % 2 == 0 else PBb2
                for p in range(NPAIR):
                    w_ = []
                    if p == 0:
                        w_ = [tp_ps_free[kk % 2]]
                        if kk == 0:
                            w_.append(u_done)
                    op("tensor", lambda e, pp=p, k2=kk, pb=pbb:
                       e.transpose(out=pb[:, pp * 128:(pp + 1) * 128],
                                   in_=V["u"][:, pp * VD + k2 * 128:pp * VD + (k2 + 1) * 128],
                                   identity=V["ident"]),
                       inc=[("PE", 1)] if p == NPAIR - 1 else [], wait=w_)
                tp_pe = mark("PE")
                w_ = [tp_pe]
                if kk == 0:
                    w_.append(o_evacs[-1])
                    w_.append(u_done)
                    w_.append(dump_marks.get("o", NOW))
                op("vector", lambda e, k2=kk, pb=pbb:
                   e.tensor_copy(out=V["ufm"][:, k2 * TS:(k2 + 1) * TS], in_=pb),
                   inc=[("DV", 1)], wait=w_)
                tp_ps_free[kk % 2] = mark("DV")
            ufm_done = mark("DV")
            E["pb_free"] = ufm_done
            col0, _ = WC_SECS["wo"]
            sec_gate("wo", s, [g_read_pl])
            wo_ps_free = {0: E["pa_free"], 1: E["pc_free"]}
            x0w_free = {0: u_done, 1: u_done}
            x1_marks = {}
            for o in range(DC):
                slot, wl_t = ws.load(ctx, col0 + o * VC * 128)
                ps = PA[:] if o % 2 == 0 else PC[:]
                xw = V["x0w0"] if o % 2 == 0 else V["x0w1"]
                op("scalar", lambda e, d=xw, oo=o:
                   e.dma_start(out=d, in_=x0_d[oo * 128:(oo + 1) * 128, t0:t0 + TS]),
                   inc=[("XW%d" % (o % 2), 16)], wait=[x0w_free[o % 2]])
                xw_xl = mark("XW%d" % (o % 2))
                for kk in range(VC):
                    for nn2 in range(2):
                        w_ = []
                        if kk == 0 and nn2 == 0:
                            w_ = [wl_t, ufm_done, wo_ps_free[o % 2]]
                        op("tensor", lambda e, p=ps, sl=slot, k2=kk, n2=nn2:
                           e.matmul(out=p[:, n2 * 512:(n2 + 1) * 512],
                                    lhsT=ws.buf(sl, k2 * 128, 128),
                                    rhs=V["ufm"][:, k2 * TS + n2 * 512:k2 * TS + n2 * 512 + 512],
                                    start=(k2 == 0), stop=(k2 == VC - 1)),
                           inc=([("PE", 1)] if (kk == VC - 1 and nn2 == 1) else []),
                           wait=w_)
                ws.slot_free[slot] = mark("PE")
                wo_pe = mark("PE")
                w_ = [wo_pe, xw_xl]
                if o == 0:
                    w_ += [u_done, dump_marks.get("q", NOW), dump_marks.get("k", NOW)]
                op("vector", lambda e, oo=o, p=ps, x=xw:
                   e.scalar_tensor_tensor(out=V["x1"][:, oo * TS:(oo + 1) * TS],
                                          in0=p, scalar=1.0, in1=x,
                                          op0=ALU.mult, op1=ALU.add),
                   inc=[("DV", 1)], wait=w_)
                wo_ps_free[o % 2] = mark("DV")
                x0w_free[o % 2] = mark("DV")
                x1_marks[o] = mark("DV")
                if s == 0:
                    drain2(3, extra=[g_read_pl])
            E["pa_free"] = wo_ps_free[0]
            E["pc_free"] = wo_ps_free[1]
            dump_marks["x1"] = dump("d_x1", V["x1"], s, ("DV", n("DV")))

            rb2 = {}
            sq2war = dict(E["sq2_war"])
            for c in range(DC):
                sq2 = V["sq2_0"] if c % 2 == 0 else V["sq2_1"]
                op("scalar", lambda e, o_=sq2, cc=c:
                   e.activation(out=o_, in_=V["x1"][:, cc * TS:(cc + 1) * TS], func=AF.Square),
                   inc=[("AC", 1)], wait=[x1_marks[c], sq2war[c % 2]])
                sq_ac = mark("AC")
                for gg in range(2):
                    w_ = [sq_ac] if gg == 0 else []
                    if c == 0 and gg == 0:
                        w_.append(E["pb_free"])
                    op("tensor", lambda e, r=sq2, cc=c, g2=gg:
                       e.matmul(out=PB[0:1, g2 * 512:g2 * 512 + 512], lhsT=V["ones_b"][:, 0:1],
                                rhs=r[:, g2 * 512:g2 * 512 + 512],
                                start=(cc == 0), stop=(cc == DC - 1)),
                       inc=[("PE", 1)], wait=w_)
                sq2war[c % 2] = mark("PE")
                if s == 0:
                    drain2(3, extra=[g_read_pl])
            E["sq2_war"] = sq2war
            sum_pe = mark("PE")
            for grp in range(2):
                op("scalar", lambda e, gg=grp:
                   e.activation(out=V["rfs"], in_=PB[0:1, gg * 512:gg * 512 + 512],
                                func=AF.Ln, bias=EPS_RMS, scale=1.0 / D),
                   inc=[("AC", 1)], wait=[sum_pe] + ([rb2[0]] if grp == 1 else []))
                op("scalar", lambda e: e.activation(out=V["rfs"], in_=V["rfs"],
                                                    func=AF.Exp, scale=-0.5),
                   inc=[("AC", 1)], wait=[mark("AC")])
                r2_ac = mark("AC")
                op("tensor", lambda e, gg=grp:
                   e.matmul(out=PB[:, 1024 + gg * 512:1024 + gg * 512 + 512],
                            lhsT=V["ones_f"][0:1, :], rhs=V["rfs"], start=True, stop=True),
                   inc=[("PE", 1)], wait=[r2_ac])
                rb2[grp] = mark("PE")
            h2m = {}
            for c in range(DC):
                for grp in range(2):
                    w_ = [rb2[grp]] if c == 0 else []
                    if c == 0 and grp == 0:
                        w_ += [v_done["sgt"], u_done, dump_marks.get("sg", NOW)]
                    op("vector", lambda e, cc=c, gg=grp:
                       e.scalar_tensor_tensor(
                           out=V["h2"][:, cc * TS + gg * 512:cc * TS + gg * 512 + 512],
                           in0=V["x1"][:, cc * TS + gg * 512:cc * TS + gg * 512 + 512],
                           scalar=V["fw"][:, cc:cc + 1],
                           in1=PB[:, 1024 + gg * 512:1024 + gg * 512 + 512],
                           op0=ALU.mult, op1=ALU.mult),
                       inc=[("DV", 1)], wait=w_)
                h2m[c] = mark("DV")
            h2_done = mark("DV")
            E["pb_free"] = h2_done
            dump_marks["h2"] = dump("d_h2", V["h2"], s, h2_done)

            col0, _ = WC_SECS["fgu"]
            sec_gate("fgu", s, [g_read_pl])
            ffn_ps_free = {0: [E["pa_free"], E["pc_free"]], 1: [E["pb_free"]]}
            fsc_free = NOW
            cur = None
            for m in range(FC):
                for kk in range(DC):
                    cl = m * DC * 256 + kk * 256
                    if cl % SLAB == 0:
                        cur = ws.load(ctx, col0 + cl)
                    sl, wlt = cur
                    off = cl % SLAB
                    if m % 2 == 0:
                        psG, psU = PA[:], PC[:]
                    else:
                        psG, psU = PB[:, 0:1024], PB[:, 1024:2048]
                    for gu in range(2):
                        pp = psG if gu == 0 else psU
                        for nn2 in range(2):
                            w_ = []
                            if kk == 0 and gu == 0 and nn2 == 0:
                                w_ = [wlt] + ffn_ps_free[m % 2]
                            elif off == 0 and gu == 0 and nn2 == 0:
                                w_ = [wlt]
                            if m == 0 and gu == 0 and nn2 == 0:
                                w_.append(h2m[kk])
                            retire = (((cl + 256) % SLAB == 0 or (m == FC - 1 and kk == DC - 1))
                                      and gu == 1 and nn2 == 1)
                            inc_ = [("PE", 1)] if (retire or (kk == DC - 1 and gu == 1 and nn2 == 1)) else []
                            op("tensor", lambda e, p=pp, s_=sl, of=off, g2=gu, k2=kk, n2=nn2:
                               e.matmul(out=p[:, n2 * 512:(n2 + 1) * 512],
                                        lhsT=ws.buf(s_, of + g2 * 128, 128),
                                        rhs=V["h2"][:, k2 * TS + n2 * 512:k2 * TS + n2 * 512 + 512],
                                        start=(k2 == 0), stop=(k2 == DC - 1)),
                               inc=inc_, wait=w_)
                            if retire:
                                ws.slot_free[sl] = mark("PE")
                mm_pe = mark("PE")
                if m % 2 == 0:
                    psG, psU = PA[:], PC[:]
                else:
                    psG, psU = PB[:, 0:1024], PB[:, 1024:2048]
                op("scalar", lambda e, p=psG: e.activation(out=V["klt"], in_=p, func=AF.Silu),
                   inc=[("AC", 1)], wait=[mm_pe, fsc_free])
                sil_ac = mark("AC")
                if s == 0:
                    if m == 32:
                        drain2(len(p2items), extra=[g_read_pl])
                        drain_flush()
                        E["p2_done"] = prep_quiesce()
                    elif m < 32:
                        drain2(2, extra=[g_read_pl])
                w_ = [sil_ac]
                if m == 0:
                    w_ += [u_done, ufm_done, x1_marks[DC - 1],
                           dump_marks.get("u", NOW)]
                if m == 32:
                    w_ += [g_read_pl, dump_marks.get("g", NOW)]
                    if s == 0:
                        w_ += E["p2_done"]
                op("vector", lambda e, mm=m, p=psU:
                   e.tensor_tensor(out=m_sl(mm, 0, TS), in0=V["klt"], in1=p, op=ALU.mult),
                   inc=[("DV", 1)], wait=w_)
                fsc_free = mark("DV")
                ffn_ps_free[m % 2] = [mark("DV")]
            m_done = mark("DV")
            E["pa_free"] = m_done
            E["pc_free"] = m_done
            E["pb_free"] = m_done

            col0, _ = WC_SECS["fd"]
            sec_gate("fd", s, [g_read_pl])
            fd_ps_free = {0: E["pa_free"], 1: E["pc_free"]}
            fsc2_free = fsc_free
            s_marks = {}
            cur = None
            for o in range(DC):
                ps = PA[:] if o % 2 == 0 else PC[:]
                xw = V["x0w0"] if o % 2 == 0 else V["x0w1"]
                op("scalar", lambda e, d=xw, oo=o:
                   e.dma_start(out=d, in_=x0_d[oo * 128:(oo + 1) * 128, t0:t0 + TS]),
                   inc=[("XW%d" % (o % 2), 16)], wait=[x0w_free[o % 2]])
                xw_xl = mark("XW%d" % (o % 2))
                for kk in range(FC):
                    cl = (o * FC + kk) * 128
                    if cl % SLAB == 0:
                        cur = ws.load(ctx, col0 + cl)
                    sl, wlt = cur
                    off = cl % SLAB
                    for nn2 in range(2):
                        w_ = []
                        if off == 0 and nn2 == 0:
                            w_.append(wlt)
                        if kk == 0 and nn2 == 0:
                            w_ += [m_done, fd_ps_free[o % 2]]
                        retire = (((cl + 128) % SLAB == 0 or (o == DC - 1 and kk == FC - 1))
                                  and nn2 == 1)
                        inc_ = [("PE", 1)] if (retire or (kk == FC - 1 and nn2 == 1)) else []
                        op("tensor", lambda e, p=ps, s_=sl, of=off, n2=nn2, k2=kk:
                           e.matmul(out=p[:, n2 * 512:(n2 + 1) * 512],
                                    lhsT=ws.buf(s_, of, 128),
                                    rhs=m_sl(k2, n2 * 512, 512),
                                    start=(k2 == 0), stop=(k2 == FC - 1)),
                           inc=inc_, wait=w_)
                        if retire:
                            ws.slot_free[s_ if False else sl] = mark("PE")
                fd_pe = mark("PE")
                op("vector", lambda e, oo=o, p=ps:
                   e.scalar_tensor_tensor(out=V["klt"], in0=p, scalar=1.0,
                                          in1=V["x1"][:, oo * TS:(oo + 1) * TS],
                                          op0=ALU.mult, op1=ALU.add),
                   inc=[("DV", 1)], wait=[fd_pe, fsc2_free])
                fd_ps_free[o % 2] = mark("DV")
                t_dv = mark("DV")
                op("vector", lambda e, oo=o, x=xw:
                   e.tensor_tensor(out=V["x1"][:, oo * TS:(oo + 1) * TS], in0=V["klt"],
                                   in1=x, op=ALU.add),
                   inc=[("DV", 1)],
                   wait=[xw_xl, t_dv] + ([dump_marks.get("x1", NOW)] if o == 0 else []))
                fsc2_free = mark("DV")
                x0w_free[o % 2] = mark("DV")
                s_dv = mark("DV")
                s_marks[o] = s_dv
                sq2 = V["sq2_0"] if o % 2 == 0 else V["sq2_1"]
                op("scalar", lambda e, oo=o, q_=sq2:
                   e.activation(out=q_, in_=V["x1"][:, oo * TS:(oo + 1) * TS], func=AF.Square),
                   inc=[("AC", 1)], wait=[s_dv, E["sq2_war"][o % 2]])
                sq_ac = mark("AC")
                for nn2 in range(2):
                    w_ = [s_dv] if nn2 == 0 else []
                    if o == 0 and nn2 == 0:
                        w_.append(E["pb_free"])
                    op("tensor", lambda e, oo=o, n2=nn2:
                       e.matmul(out=PB[0:1, n2 * 512:n2 * 512 + 512], lhsT=V["ones_b"][:, 0:1],
                                rhs=V["x1"][:, oo * TS + n2 * 512:oo * TS + n2 * 512 + 512],
                                start=(oo == 0), stop=(oo == DC - 1)), wait=w_)
                    op("tensor", lambda e, q_=sq2, n2=nn2, oo=o:
                       e.matmul(out=PB[0:1, 1024 + n2 * 512:1024 + n2 * 512 + 512],
                                lhsT=V["ones_b"][:, 0:1], rhs=q_[:, n2 * 512:n2 * 512 + 512],
                                start=(oo == 0), stop=(oo == DC - 1)),
                       inc=[("PE", 1)],
                       wait=[sq_ac] if nn2 == 0 else [])
                    E["sq2_war"][o % 2] = mark("PE")
            s_done = mark("DV")
            fd_all_pe = mark("PE")

            def emit_ln(sum_pe_mark, s_view, ph, out_view, fsc_war, region_war,
                        store_slices=None, store_war=NOW, row_war=NOW):
                tokens = 512
                pso = ph * 512
                op("scalar", lambda e: e.activation(out=V["mrow"][0:1, 0:tokens],
                                                    in_=PB[0:1, pso:pso + tokens],
                                                    func=AF.Copy, scale=1.0 / D),
                   inc=[("AC", 1)], wait=[sum_pe_mark, row_war])
                op("scalar", lambda e: e.activation(out=V["vrow"][0:1, 0:tokens],
                                                    in_=PB[0:1, 1024 + pso:1024 + pso + tokens],
                                                    func=AF.Copy, scale=1.0 / D),
                   inc=[("AC", 1)])
                mv_ac = mark("AC")
                op("vector", lambda e: e.tensor_tensor(out=V["trow"][0:1, 0:tokens],
                                                       in0=V["mrow"][0:1, 0:tokens],
                                                       in1=V["mrow"][0:1, 0:tokens], op=ALU.mult),
                   inc=[("DV", 1)], wait=[mv_ac])
                op("vector", lambda e: e.tensor_tensor(out=V["vrow"][0:1, 0:tokens],
                                                       in0=V["vrow"][0:1, 0:tokens],
                                                       in1=V["trow"][0:1, 0:tokens],
                                                       op=ALU.subtract),
                   inc=[("DV", 1)], wait=[mark("DV")])
                var_dv = mark("DV")
                op("scalar", lambda e: e.activation(out=V["vrow"][0:1, 0:tokens],
                                                    in_=V["vrow"][0:1, 0:tokens],
                                                    func=AF.Ln, bias=EPS_LN),
                   inc=[("AC", 1)], wait=[var_dv])
                op("scalar", lambda e: e.activation(out=V["vrow"][0:1, 0:tokens],
                                                    in_=V["vrow"][0:1, 0:tokens],
                                                    func=AF.Exp, scale=-0.5),
                   inc=[("AC", 1)], wait=[mark("AC")])
                r_ac = mark("AC")
                op("vector", lambda e:
                   e.scalar_tensor_tensor(out=V["mrow"][0:1, 0:tokens],
                                          in0=V["mrow"][0:1, 0:tokens], scalar=-1.0,
                                          in1=V["vrow"][0:1, 0:tokens],
                                          op0=ALU.mult, op1=ALU.mult),
                   inc=[("DV", 1)], wait=[r_ac])
                nm_dv = mark("DV")
                op("tensor", lambda e:
                   e.matmul(out=PB[:, 1024 + pso:1024 + pso + 512],
                            lhsT=V["ones_f"][0:1, :],
                            rhs=V["vrow"][0:1, 0:512],
                            start=True, stop=True), wait=[r_ac])
                op("tensor", lambda e:
                   e.matmul(out=PB[:, pso:pso + 512],
                            lhsT=V["ones_f"][0:1, :],
                            rhs=V["mrow"][0:1, 0:512],
                            start=True, stop=True),
                   inc=[("PE", 1)], wait=[nm_dv])
                bc_pe = mark("PE")
                emit_ln.last_bc = bc_pe
                op("vector", lambda e: e.tensor_copy(out=V["rsb2"][:, 0:tokens],
                                                     in_=PB[:, 1024 + pso:1024 + pso + tokens]),
                   inc=[("DV", 1)], wait=[bc_pe])
                op("vector", lambda e: e.tensor_copy(out=V["nm2"][:, 0:tokens],
                                                     in_=PB[:, pso:pso + tokens]),
                   inc=[("DV", 1)])
                bc_dv = mark("DV")
                outs, st_marks = {}, {}
                fw_ = fsc_war
                ob_war = dict(store_war) if isinstance(store_war, dict) else {0: store_war, 1: store_war}
                for c in range(DC):
                    w_ = [bc_dv, fw_]
                    if c == 0:
                        w_ += list(region_war) if isinstance(region_war, list) else [region_war]
                    op("gpsimd", lambda e, cc=c:
                       e.tensor_tensor(out=V["klt"][:, 0:tokens], in0=s_view(cc),
                                       in1=V["rsb2"][:, 0:tokens], op=ALU.mult),
                       inc=[("PL", 1)], wait=w_)
                    t_pl = mark("PL")
                    w_ = [t_pl]
                    if store_slices is not None:
                        w_.append(ob_war[c % 2])
                    op("vector", lambda e, cc=c:
                       e.tensor_tensor(out=out_view(cc), in0=V["klt"][:, 0:tokens],
                                       in1=V["nm2"][:, 0:tokens], op=ALU.add),
                       inc=[("DV", 1)], wait=w_)
                    fw_ = mark("DV")
                    outs[c] = mark("DV")
                    if store_slices is not None:
                        stsem = "ST%d" % (c % 2)
                        op("gpsimd", lambda e, cc=c:
                           e.dma_start(out=store_slices(cc), in_=out_view(cc)),
                           inc=[(stsem, 16)], wait=[outs[c], ob_war[c % 2]])
                        ob_war[c % 2] = mark(stsem)
                        st_marks[c] = mark(stsem)
                return outs, mark("DV"), st_marks, fw_

            fsc_after = fsc2_free
            row_war = E.get("ln_row_war", NOW)
            x3h = {}
            x3o = {}
            for ph in range(2):
                x3o[ph], x3_done, _, fsc_after = emit_ln(
                    fd_all_pe,
                    lambda c, p_=ph: V["x1"][:, c * TS + p_ * 512:c * TS + p_ * 512 + 512], ph,
                    lambda c, p_=ph: V["x3"][:, c * TS + p_ * 512:c * TS + p_ * 512 + 512],
                    fsc_after, [m_done, dump_marks.get("h2", NOW)] if ph == 0 else NOW,
                    row_war=row_war)
                row_war = emit_ln.last_bc
                x3h[ph] = x3_done
            E["pb_free"] = x3_done
            dump_marks["x3"] = dump("d_x3", V["x3"], s, x3_done)

            qf_ps_free = {0: E["pa_free"], 1: E["pc_free"]}
            fsc_war = fsc_after
            f_region_war = fd_all_pe
            st_war = E.get("st_war", {0: NOW, 1: NOW})
            s2_read_done = NOW
            for half in range(2):
                th0 = half * 512
                col0, _ = WC_SECS["qgu"]
                direct = (s == 0 and half == 0)
                if direct:
                    qf_prefetch([fd_all_pe])
                    qf_prefetch([fd_all_pe])
                else:
                    sec_gate("qgu", s)
                    if s == 0:
                        ws.gate = list(ws.gate) + list(dstate["slot_store"])
                cur = None
                for m in range(QC):
                    for kk in range(DC):
                        cl = m * DC * 256 + kk * 256
                        if cl % SLAB == 0:
                            cur = qdirect(col0 + cl) if direct else ws.load(ctx, col0 + cl)
                        sl, wlt = cur
                        off = cl % SLAB
                        psG = PA[:, 0:512] if m % 2 == 0 else PC[:, 0:512]
                        psU = PA[:, 512:1024] if m % 2 == 0 else PC[:, 512:1024]
                        for gu in range(2):
                            pp = psG if gu == 0 else psU
                            w_ = []
                            if kk == 0 and gu == 0:
                                w_ = [qf_ps_free[m % 2]]
                                if off == 0:
                                    w_.append(wlt)
                            elif off == 0 and gu == 0 and kk % (SLAB // 256) == 0:
                                w_ = [wlt]
                            if m == 0 and gu == 0:
                                w_.append(x3o[half][kk])
                            retire = (((cl + 256) % SLAB == 0 or (m == QC - 1 and kk == DC - 1))
                                      and gu == 1)
                            inc_ = [("PE", 1)] if (retire or (kk == DC - 1 and gu == 1)) else []
                            op("tensor", lambda e, p=pp, s_=sl, of=off, g2=gu, k2=kk:
                               e.matmul(out=p,
                                        lhsT=ws.buf(s_, of + g2 * 128, 128),
                                        rhs=V["x3"][:, k2 * TS + th0:k2 * TS + th0 + 512],
                                        start=(k2 == 0), stop=(k2 == DC - 1)),
                               inc=inc_, wait=w_)
                            if retire:
                                ws.slot_free[sl] = mark("PE")
                    gu_pe = mark("PE")
                    psG = PA[:, 0:512] if m % 2 == 0 else PC[:, 0:512]
                    psU = PA[:, 512:1024] if m % 2 == 0 else PC[:, 512:1024]
                    op("scalar", lambda e, p=psG, mm=m:
                       e.activation(out=V["klt"][:, 0:512], in_=p, func=AF.Silu,
                                    scale=V["sg"][:, mm:mm + 1]),
                       inc=[("AC", 1)], wait=[gu_pe, fsc_war])
                    op("scalar", lambda e, p=psU, mm=m:
                       e.activation(out=V["klt"][:, 512:1024], in_=p, func=AF.Copy,
                                    scale=V["su"][:, mm:mm + 1]),
                       inc=[("AC", 1)])
                    sil_ac = mark("AC")
                    w_ = [sil_ac]
                    if m == 0:
                        w_.append(f_region_war)
                    op("vector", lambda e, mm=m:
                       e.tensor_tensor(out=V["f"][:, mm * 512:(mm + 1) * 512],
                                       in0=V["klt"][:, 0:512], in1=V["klt"][:, 512:1024],
                                       op=ALU.mult),
                       inc=[("DV", 1)], wait=w_)
                    fsc_war = mark("DV")
                    qf_ps_free[m % 2] = mark("DV")
                f_done = mark("DV")
                col0d, _ = WC_SECS["qd"]
                if not direct:
                    sec_gate("qd", s)
                cur = None
                for o in range(DC):
                    ps = PA[:, 0:512] if o % 2 == 0 else PC[:, 0:512]
                    for kk in range(QC):
                        cl = (o * QC + kk) * 128
                        if cl % SLAB == 0:
                            cur = qdirect(col0d + cl) if direct else ws.load(ctx, col0d + cl)
                        sl, wlt = cur
                        off = cl % SLAB
                        w_ = []
                        if off == 0:
                            w_.append(wlt)
                        if kk == 0:
                            w_ += [f_done, qf_ps_free[o % 2]]
                        retire = ((cl + 128) % SLAB == 0 or (o == DC - 1 and kk == QC - 1))
                        inc_ = [("PE", 1)] if (retire or kk == QC - 1) else []
                        op("tensor", lambda e, p=ps, s_=sl, of=off, k2=kk:
                           e.matmul(out=p, lhsT=ws.buf(s_, of, 128),
                                    rhs=V["f"][:, k2 * 512:(k2 + 1) * 512],
                                    start=(k2 == 0), stop=(k2 == QC - 1)),
                           inc=inc_, wait=w_)
                        if retire:
                            ws.slot_free[sl] = mark("PE")
                    qd_pe = mark("PE")
                    w_ = [qd_pe]
                    if o == 0 and half == 0:
                        w_.append(x3_done)
                    op("vector", lambda e, oo=o, p=ps:
                       e.scalar_tensor_tensor(
                           out=V["s2"][:, oo * TS + th0:oo * TS + th0 + 512],
                           in0=p, scalar=V["sd"][:, oo:oo + 1],
                           in1=V["x3"][:, oo * TS + th0:oo * TS + th0 + 512],
                           op0=ALU.mult, op1=ALU.add),
                       inc=[("DV", 1)], wait=w_)
                    qf_ps_free[o % 2] = mark("DV")
                    s2_dv = mark("DV")
                    sq2 = V["sq2_0"] if o % 2 == 0 else V["sq2_1"]
                    op("scalar", lambda e, oo=o, q_=sq2:
                       e.activation(out=q_[:, 0:512],
                                    in_=V["s2"][:, oo * TS + th0:oo * TS + th0 + 512],
                                    func=AF.Square),
                       inc=[("AC", 1)], wait=[s2_dv, E["sq2_war"][o % 2]])
                    sq_ac = mark("AC")
                    w0 = [s2_dv] + ([E["pb_free"]] if o == 0 else [])
                    op("tensor", lambda e, oo=o:
                       e.matmul(out=PB[0:1, 0:512], lhsT=V["ones_b"][:, 0:1],
                                rhs=V["s2"][:, oo * TS + th0:oo * TS + th0 + 512],
                                start=(oo == 0), stop=(oo == DC - 1)), wait=w0)
                    op("tensor", lambda e, q_=sq2, oo=o:
                       e.matmul(out=PB[0:1, 1024:1536], lhsT=V["ones_b"][:, 0:1],
                                rhs=q_[:, 0:512], start=(oo == 0), stop=(oo == DC - 1)),
                       inc=[("PE", 1)], wait=[sq_ac])
                    E["sq2_war"][o % 2] = mark("PE")
                f_region_war = mark("PE")
                qsum_pe = mark("PE")
                outs, fin, st_marks, fsc_after2 = emit_ln(
                    qsum_pe,
                    lambda c: V["s2"][:, c * TS + th0:c * TS + th0 + 512], 0,
                    lambda c: (V["outb0"] if c % 2 == 0 else V["outb1"]),
                    fsc_war,
                    x0w_free[0] if half == 0 else NOW,
                    store_slices=lambda c: y_d[c * 128:(c + 1) * 128, t0 + th0:t0 + th0 + 512],
                    store_war=st_war, row_war=row_war)
                row_war = emit_ln.last_bc
                E["ln_row_war"] = row_war
                fsc_war = fsc_after2
                st_war = {0: mark("ST0"), 1: mark("ST1")}
                E["st_war"] = st_war
                last_st0, last_st1 = mark("ST0"), mark("ST1")
                E["pb_free"] = fin
                s2_read_done = fin
            E["h_region"] = f_region_war
            E["v_region"] = f_region_war
            E["sg_region"] = s2_read_done
            E["qk_region"] = s2_read_done
            E["g_region"] = [fd_all_pe] + (prep_quiesce() if s == 0 else [])
            E["pa_free"] = qf_ps_free[0]
            E["pc_free"] = qf_ps_free[1]
            E["xr_war"] = [last_st0, last_st1, x0w_free[1], x0w_free[1]]
            E["sqx_war"] = {0: s2_read_done, 1: s2_read_done}

            if debug and s == 0:
                op("gpsimd", lambda e: e.dma_start(out=dbg["d_S"], in_=V["S32"]),
                   inc=[("ST0", 16)],
                   wait=[("ST0", n("ST0")), ("DV", n("DV"))])
                E["sdump"] = mark("ST0")

        wt("gpsimd", ("ST0", n("ST0")), ("ST1", n("ST1")),
           ("WC0", n("WC0")), ("WC1", n("WC1")))

    with nc.Block() as block:
        for engname, starter in [("sync", block.sync), ("scalar", block.scalar),
                                 ("vector", block.vector), ("tensor", block.tensor),
                                 ("gpsimd", block.gpsimd)]:
            def run(eng, engname=engname):
                build_program(Ctx(engname, eng, semh))
            starter(run)

    for cm in sem_cms:
        cm.__exit__(None, None, None)
    arena_cm.__exit__(None, None, None)
    return nc


def _prep_weights(inputs):
    f32 = np.float32
    W = {}
    W["wv_s"] = lay_rhs_nh(np.asarray(inputs["L0_attn_wv"], f32), DC)
    W["wg_s"] = lay_rhs_nh(np.asarray(inputs["L0_attn_wg"], f32), DC)
    W["wq_s"] = lay_lhsT(np.asarray(inputs["L0_attn_wq"], f32), DC, KC)
    W["wk_s"] = lay_lhsT(np.asarray(inputs["L0_attn_wk"], f32), DC, KC)
    gk1 = np.asarray(inputs["L0_attn_gk_w1"], f32)
    W["gk1_s"] = np.ascontiguousarray(gk1.reshape(DC, 128, LR).transpose(1, 0, 2).reshape(128, DC * LR))
    W["wo_s"] = lay_lhsT(np.asarray(inputs["L0_attn_wo"], f32), VC, DC)
    W["fgu_s"] = lay_gu(np.asarray(inputs["L0_attn_w_gate"], f32),
                        np.asarray(inputs["L0_attn_w_up"], f32), DC, FC)
    W["fd_s"] = lay_lhsT(np.asarray(inputs["L0_attn_w_down"], f32), FC, DC)
    W["qgu_s"] = lay_gu(np.asarray(inputs["L0_ffn_w_gate"]),
                        np.asarray(inputs["L0_ffn_w_up"]), DC, QC)
    W["qd_s"] = lay_lhsT(np.asarray(inputs["L0_ffn_w_down"]), QC, DC)
    W["gk2_s"] = np.ascontiguousarray(np.asarray(inputs["L0_attn_gk_w2"], f32))
    W["nw_s"] = lay_vec(inputs["L0_attn_norm1_w"])
    W["fw_s"] = lay_vec(inputs["L0_attn_ffn_norm_w"])
    W["gb_s"] = lay_vec(inputs["L0_attn_gk_b"])
    W["sg_s"] = lay_vec(inputs["L0_ffn_s_gate"])
    W["su_s"] = lay_vec(inputs["L0_ffn_s_up"])
    W["sd_s"] = lay_vec(inputs["L0_ffn_s_down"])
    gnw = np.asarray(inputs["L0_attn_gnorm_w"], f32)
    W["gnw_s"] = np.ascontiguousarray(gnw.reshape(2, 128).T)
    cst = np.zeros((128, 192), dtype=ml_dtypes.bfloat16)
    cst[:, 0:128] = np.eye(128, dtype=ml_dtypes.bfloat16)
    um = np.triu(np.ones((64, 64), np.float32)).astype(ml_dtypes.bfloat16)
    cst[0:64, 128:192] = um
    cst[64:128, 128:192] = um
    W["cst_s"] = cst
    return W


def kernel(**inputs):
    x = np.asarray(inputs["L0_attn_x"], np.float32)
    B = x.shape[0]
    W = _prep_weights(inputs)
    nc = build_nc(debug=False)
    in_maps = []
    for b in range(B):
        m = {"x0": np.ascontiguousarray(x[b].T)}
        m.update(W)
        in_maps.append(m)
    res = run_bass_kernel_spmd(nc, in_maps, core_ids=list(range(B)))
    out = np.stack([np.ascontiguousarray(res.results[b]["y"].T) for b in range(B)], axis=0)
    return out
